# version 09
# baseline (speedup 1.0000x reference)
import contextlib
import numpy as np
import ml_dtypes
import concourse.bass as bass
import concourse.mybir as mybir
from concourse.bass_utils import run_bass_kernel_spmd

F32 = mybir.dt.float32
BF16 = mybir.dt.bfloat16
AF = mybir.ActivationFunctionType
ALU = mybir.AluOpType

N = 4096
E = 2048
H = 16
FF = 5632
NC_ = 8
NTI = 32
KE = 16
NF = FF // 128
LN_EPS = 1e-5
RMS_EPS = 1e-6


SIM_SAFE = [False]
A1_CAP = [256]
NO_POOL = [False]


def rsqrt_act(scalar, out, in_, eps):
    if SIM_SAFE[0]:
        scalar.activation(out, in_, AF.Ln, bias=eps)
        return scalar.activation(out, out, AF.Exp, scale=-0.5)
    return scalar.activation(out, in_, AF.Abs_reciprocal_sqrt, bias=eps)


def preload_rsqrt(scalar):
    if SIM_SAFE[0]:
        return preload_set_with(scalar, [AF.Ln, AF.Exp])
    return scalar.preload_act_table(AF.Abs_reciprocal_sqrt)


def preload_set_with(scalar, funcs):
    from concourse.hw_specs import get_activation_tables
    for idx, fs in enumerate(get_activation_tables(scalar.bass.m.arch).values()):
        if all(f in fs for f in funcs):
            return scalar.add_instruction(
                mybir.InstLoadActFuncSet(
                    name=scalar.bass.get_next_instruction_name(),
                    act_func_set_id=idx, ins=[], outs=[]))
    raise ValueError("no table set with %s" % funcs)


class Sem:
    def __init__(self, es, nc, name):
        self.h = es.enter_context(nc.semaphore(name))
        self.n = 0

    def inc(self, instr, k=1):
        instr.then_inc(self.h, k)
        self.n += k


def _chunk(c):
    return slice(c * 512, (c + 1) * 512)


def build_program(debug=False, upto='D3'):
    nc = bass.Bass("TRN2", target_bir_lowering=False)

    x_in = nc.dram_tensor("x", [N, E], F32, kind="ExternalInput").ap()
    w_qkv = nc.dram_tensor("w_qkv", [E, 3 * E], F32, kind="ExternalInput").ap()
    w_out = nc.dram_tensor("w_out", [E, E], F32, kind="ExternalInput").ap()
    wv_in = nc.dram_tensor("wv", [E, E], F32, kind="ExternalInput").ap()
    wo_in = nc.dram_tensor("wo", [E, E], F32, kind="ExternalInput").ap()
    wg_in = nc.dram_tensor("w_gate", [E, FF], F32, kind="ExternalInput").ap()
    wu_in = nc.dram_tensor("w_up", [E, FF], F32, kind="ExternalInput").ap()
    wd_in = nc.dram_tensor("w_down", [FF, E], F32, kind="ExternalInput").ap()
    gam_in = nc.dram_tensor("gam", [128, 1], F32, kind="ExternalInput").ap()
    bet_in = nc.dram_tensor("bet", [128, 1], F32, kind="ExternalInput").ap()
    w1_in = nc.dram_tensor("w1t", [128, KE], F32, kind="ExternalInput").ap()
    w2_in = nc.dram_tensor("w2t", [128, KE], F32, kind="ExternalInput").ap()
    bv_in = nc.dram_tensor("bvt", [128, KE], F32, kind="ExternalInput").ap()
    wr_in = nc.dram_tensor("wr_rep", [E, 128], F32, kind="ExternalInput").ap()
    br_in = nc.dram_tensor("br_t", [128, 1], F32, kind="ExternalInput").ap()
    c2_in = nc.dram_tensor("c2", [128, N], F32, kind="ExternalInput").ap()
    s2_in = nc.dram_tensor("s2", [128, N], F32, kind="ExternalInput").ap()
    msk_in = nc.dram_tensor("msk", [128, 4 * 512], BF16, kind="ExternalInput").ap()
    ones_in = nc.dram_tensor("ones128", [128, 128], BF16, kind="ExternalInput").ap()
    eps_in = nc.dram_tensor("epsv", [128, 2], F32, kind="ExternalInput").ap()
    id_in = nc.dram_tensor("ident", [128, 128], BF16, kind="ExternalInput").ap()

    out_t = nc.dram_tensor("out", [N, E], F32, kind="ExternalOutput").ap()

    dk = "ExternalOutput" if debug else "Internal"
    xbf = nc.dram_tensor("xbf", [N, E], BF16, kind="Internal").ap()
    xlo = nc.dram_tensor("xlo", [N, E], BF16, kind="Internal").ap()
    qT = nc.dram_tensor("qT", [H, 128, N], BF16, kind=dk).ap()
    kTs = nc.dram_tensor("kTs", [H, 128, N], BF16, kind=dk).ap()
    vsc = nc.dram_tensor("vsc", [NTI, 128, E], BF16, kind=dk).ap()
    aT = nc.dram_tensor("aT", [H, 128, N], BF16, kind=dk).ap()
    x1sc = nc.dram_tensor("x1sc", [KE, 128, N], BF16, kind=dk).ap()
    msc = nc.dram_tensor("msc", [NC_, 128, 512], BF16, kind=dk).ap()
    vfsc = nc.dram_tensor("vfsc", [KE, 128, N], BF16, kind="Internal").ap()
    hsc = nc.dram_tensor("hsc", [KE, 128, N], BF16, kind=dk).ap()
    h2sc = nc.dram_tensor("h2sc", [KE, 128, N], BF16, kind=dk).ap()
    wvq_bf = nc.dram_tensor("wvq_bf", [E, E], BF16, kind="Internal").ap()
    wout_bf = nc.dram_tensor("wout_bf", [E, E], BF16, kind="Internal").ap()
    wvf_bf = nc.dram_tensor("wvf_bf", [E, E], BF16, kind="Internal").ap()
    wo_bf = nc.dram_tensor("wo_bf", [E, E], BF16, kind="Internal").ap()
    gate_bf = nc.dram_tensor("gate_bf", [NF, E, 128], BF16, kind="Internal").ap()
    up_bf = nc.dram_tensor("up_bf", [NF, E, 128], BF16, kind="Internal").ap()
    down_bf = nc.dram_tensor("down_bf", [KE, FF, 128], BF16, kind="Internal").ap()

    top = contextlib.ExitStack()
    with top:
        ones128 = top.enter_context(nc.sbuf_tensor("ones_sb", [128, 128], BF16))
        ident = top.enter_context(nc.sbuf_tensor("id_sb", [128, 128], BF16))
        gam = top.enter_context(nc.sbuf_tensor("gam_sb", [128, 1], F32))
        bet = top.enter_context(nc.sbuf_tensor("bet_sb", [128, 1], F32))
        w1sb = top.enter_context(nc.sbuf_tensor("w1_sb", [128, KE], F32))
        w2sb = top.enter_context(nc.sbuf_tensor("w2_sb", [128, KE], F32))
        bvsb = top.enter_context(nc.sbuf_tensor("bv_sb", [128, KE], F32))
        brsb = top.enter_context(nc.sbuf_tensor("br_sb", [128, 1], F32))
        epsb = top.enter_context(nc.sbuf_tensor("eps_sb", [128, 2], F32))
        barsb = top.enter_context(nc.sbuf_tensor("bar_sb", [128, 16], F32))
        nc.const_aps.register(F32, LN_EPS, epsb[:, 0:1])
        nc.const_aps.register(F32, RMS_EPS, epsb[:, 1:2])

        def S(name):
            return Sem(top, nc, name)

        s_cst = S("cst"); s_xt = S("xt")
        s_wq = [S("wq0"), S("wq1")]
        s_wcast = S("wcast"); s_ph = S("ph")
        s_a1mm = S("a1mm"); s_a1pf = S("a1pf"); s_a1r = S("a1r")
        s_a1st = S("a1st"); s_a1sf = S("a1sf"); s_a1var = S("a1var")
        s_a1rb = S("a1rb"); s_a1rl = S("a1rl"); s_a1qo = S("a1qo"); s_a1sw = S("a1sw")
        s_xbf = S("xbf")
        s_a1sp = [S("a1sp0"), S("a1sp1")]
        s_wv = S("wv"); s_a2mm = S("a2mm"); s_a2ev = S("a2ev")
        s_a2sp = [S("a2sp0"), S("a2sp1")]
        s_bld = [S("bld0"), S("bld1")]
        s_bmm1 = S("bmm1"); s_bsq1 = S("bsq1")
        s_bsq2 = S("bsq2"); s_bmm2 = S("bmm2"); s_bep = S("bep")
        s_bsp = [S("bsp0"), S("bsp1")]
        s_xlld = [S("xlld0"), S("xlld1")]
        s_xl = S("xl")
        s_xlsp = [S("xlsp0"), S("xlsp1")]
        s_cw = S("cw")
        s_cld = [S("cld0"), S("cld1")]
        s_cx0 = S("cx0")
        s_cmm = S("cmm"); s_cev = S("cev")
        s_csq = S("csq"); s_cst2 = S("cst2"); s_csf = S("csf"); s_cvar = S("cvar")
        s_crb = S("crb"); s_cx1 = S("cx1"); s_cx1sp = S("cx1sp"); s_crt = S("crt")
        s_cms = S("cms")
        s_cmsp = [S("cmsp0"), S("cmsp1")]
        s_d1w = S("d1w")
        s_d1ld = [S("d1ld0"), S("d1ld1")]
        s_d1xq = S("d1xq"); s_d1ss = S("d1ss")
        s_d1rb = S("d1rb"); s_d1h1 = S("d1h1"); s_d1mm = S("d1mm"); s_d1ev = S("d1ev")
        s_d1sp = [S("d1sp0"), S("d1sp1")]
        s_d2w = S("d2w")
        s_d2ld = [S("d2ld0"), S("d2ld1")]
        s_d2mm = S("d2mm"); s_d2h = S("d2h")
        s_d2sq = S("d2sq"); s_d2st = S("d2st"); s_d2rb = S("d2rb")
        s_d2h2 = S("d2h2")
        s_d2sp = [S("d2sp0"), S("d2sp1")]
        s_d2hsp = S("d2hsp")
        s_d3wf = S("d3wf"); s_d3wd = S("d3wd")
        s_d3h2ld = [S("d3h2ld0"), S("d3h2ld1")]
        s_d3xld = S("d3xld")
        s_d3mm = S("d3mm"); s_d3sg = S("d3sg"); s_d3gf = S("d3gf"); s_d3dn = S("d3dn")
        s_d3z = S("d3z"); s_d3zq = S("d3zq"); s_d3st = S("d3st"); s_d3sf = S("d3sf")
        s_d3var = S("d3var"); s_d3rb = S("d3rb"); s_d3x2 = S("d3x2"); s_d3tr = S("d3tr")
        s_d3oev = S("d3oev")
        s_out = [S("outd0"), S("outd1")]

        PHC = [0]
        CSTB = [0]
        _dma_sems_all = [s_cst, s_xt, s_wq[0], s_wq[1], s_wcast,
                         s_a1sp[0], s_a1sp[1], s_wv, s_a2sp[0], s_a2sp[1],
                         s_bld[0], s_bld[1], s_bsp[0], s_bsp[1],
                         s_xlld[0], s_xlld[1], s_xlsp[0], s_xlsp[1],
                         s_cw, s_cld[0], s_cld[1], s_cx0, s_cx1sp, s_cmsp[0], s_cmsp[1],
                         s_d1w, s_d1ld[0], s_d1ld[1], s_d1sp[0], s_d1sp[1],
                         s_d2w, s_d2ld[0], s_d2ld[1], s_d2sp[0], s_d2sp[1], s_d2hsp,
                         s_d3wf, s_d3wd, s_d3h2ld[0], s_d3h2ld[1], s_d3xld,
                         s_out[0], s_out[1]]

        def barrier_block(sync_waits=(), scalar_waits=()):
            tgt = PHC[0] + 5
            with nc.Block() as blk:
                @blk.sync
                def _(e):
                    for (sem, v) in sync_waits:
                        e.wait_ge(sem.h, v)
                    e.nop().then_inc(s_ph.h, 1); s_ph.n += 1
                    e.wait_ge(s_ph.h, tgt)

                @blk.gpsimd
                def _(e):
                    e.nop().then_inc(s_ph.h, 1); s_ph.n += 1
                    e.wait_ge(s_ph.h, tgt)

                @blk.tensor
                def _(e):
                    e.nop().then_inc(s_ph.h, 1); s_ph.n += 1
                    e.wait_ge(s_ph.h, tgt)

                @blk.vector
                def _(e):
                    e.wait_ge(s_cst.h, 144)
                    ii = e.tensor_copy(barsb[:, 0:1], epsb[:, 0:1])
                    ii.then_inc(s_ph.h, 1); s_ph.n += 1
                    e.wait_ge(s_ph.h, tgt)

                @blk.scalar
                def _(e):
                    for (sem, v) in scalar_waits:
                        e.wait_ge(sem.h, v)
                    e.wait_ge(s_cst.h, 144)
                    ii = e.copy(barsb[:, 1:2], epsb[:, 0:1])
                    ii.then_inc(s_ph.h, 1); s_ph.n += 1
                    e.wait_ge(s_ph.h, tgt)
            PHC[0] = s_ph.n
            assert PHC[0] == tgt

        def drain_block():
            with nc.Block() as blk:
                @blk.sync
                def _(sync):
                    for sem in _dma_sems_all:
                        if sem.n > 0:
                            sync.wait_ge(sem.h, sem.n)

        with nc.Block() as blk:
            @blk.sync
            def _(sync):
                for dst, src in [(ones128, ones_in), (ident, id_in),
                                 (gam, gam_in), (bet, bet_in), (w1sb, w1_in),
                                 (w2sb, w2_in), (bvsb, bv_in), (brsb, br_in),
                                 (epsb, eps_in)]:
                    sync.dma_start(dst[:], src).then_inc(s_cst.h, 16)
                    s_cst.n += 16

            @blk.gpsimd
            def _(gpsimd):
                for c in range(NC_):
                    gpsimd.dma_start(xbf[_chunk(c), :], x_in[_chunk(c), :]
                                     ).then_inc(s_xbf.h, 16)
                    s_xbf.n += 16



        CST0 = s_cst.n

        with contextlib.ExitStack() as pa:
            xT = pa.enter_context(nc.sbuf_tensor("xT", [128, KE * N], BF16))

            def xTs(k, c=None):
                if c is None:
                    return xT[:, k * N:(k + 1) * N]
                return xT[:, k * N + c * 512: k * N + (c + 1) * 512]

            with contextlib.ExitStack() as pa1:
                c2sb = pa1.enter_context(nc.sbuf_tensor("c2_sb", [128, N], F32))
                s2sb = pa1.enter_context(nc.sbuf_tensor("s2_sb", [128, N], F32))
                wq = pa1.enter_context(nc.sbuf_tensor("wq_sb", [128, 2 * KE * 128], BF16))
                swp = pa1.enter_context(nc.sbuf_tensor("swp", [128, 2 * 512], F32))
                tc = pa1.enter_context(nc.sbuf_tensor("tc", [128, 512], F32))
                tsb = pa1.enter_context(nc.sbuf_tensor("tsb", [128, 512], F32))
                rbuf = pa1.enter_context(nc.sbuf_tensor("rbuf", [128, 3 * 512], F32))
                rbf = pa1.enter_context(nc.sbuf_tensor("rbf", [128, 2 * 512], BF16))
                rsqb = pa1.enter_context(nc.sbuf_tensor("rsqb", [128, 2 * 512], BF16))
                mbb = pa1.enter_context(nc.sbuf_tensor("mbb", [128, 2 * 512], BF16))
                uqb = pa1.enter_context(nc.sbuf_tensor("uqb", [128, 2 * 512], F32))
                varb = pa1.enter_context(nc.sbuf_tensor("varb", [128, 512], F32))
                rbb = pa1.enter_context(nc.sbuf_tensor("rbb", [128, 2 * 512], F32))
                tb2 = pa1.enter_context(nc.sbuf_tensor("tb2", [128, 512], BF16))
                tb2f = pa1.enter_context(nc.sbuf_tensor("tb2f", [128, 512], F32))
                rlb = pa1.enter_context(nc.sbuf_tensor("rlb", [128, 2 * 512], BF16))
                qob = pa1.enter_context(nc.sbuf_tensor("qob", [128, 2 * 512], BF16))
                p_p = [pa1.enter_context(nc.psum_tensor("pp%d" % i, [128, 512], F32))
                       for i in range(2)]
                p_sa = [pa1.enter_context(nc.psum_tensor("psa%d" % i, [128, 512], F32))
                        for i in range(2)]
                p_sb = [pa1.enter_context(nc.psum_tensor("psb%d" % i, [128, 512], F32))
                        for i in range(2)]

                def b2(buf, u):
                    return buf[:, (u % 2) * 512:(u % 2 + 1) * 512]

                def b3(buf, u):
                    return buf[:, (u % 3) * 512:(u % 3 + 1) * 512]

                with nc.Block() as blk:
                    @blk.sync
                    def _(sync):
                        sync.wait_ge(s_cst.h, CST0)
                        sync.dma_start(c2sb[:], c2_in).then_inc(s_cst.h, 16)
                        sync.dma_start(s2sb[:], s2_in).then_inc(s_cst.h, 16)
                        s_cst.n += 32
                        sync.wait_ge(s_xbf.h, 128)
                        for c in range(NC_):
                            for k in range(KE):
                                sync.dma_start(
                                    xTs(k, c),
                                    xbf[_chunk(c), k * 128:(k + 1) * 128],
                                    transpose=True).then_inc(s_xt.h, 16)
                                s_xt.n += 16
                        for u in range(256):
                            s, c = divmod(u, NC_)
                            t = qT if s < H else kTs
                            sync.wait_ge(s_a1qo.h, u + 1)
                            sync.dma_start(t[s % H][:, _chunk(c)], b2(qob, u)
                                           ).then_inc(s_a1sp[u % 2].h, 16)
                            s_a1sp[u % 2].n += 16

                    @blk.gpsimd
                    def _(gpsimd):
                        casts = []
                        for tgt, srcap in [(wvq_bf, w_qkv[:, 2 * E:3 * E]),
                                           (wout_bf, w_out),
                                           (wvf_bf, wv_in),
                                           (wo_bf, wo_in)]:
                            for i8 in range(8):
                                rs8 = slice(i8 * 256, (i8 + 1) * 256)
                                casts.append((tgt[rs8, :], srcap[rs8, :]))
                        for s in range(2 * H):
                            if s >= 2:
                                gpsimd.wait_ge(s_a1mm.h, 8 * (s - 1))
                            col0 = s * 128 if s < H else E + (s - H) * 128
                            src = w_qkv[:, col0:col0 + 128].rearrange(
                                "(k p) c -> p k c", p=128)
                            dst = wq[:, (s % 2) * KE * 128:(s % 2 + 1) * KE * 128
                                     ].rearrange("p (k c) -> p k c", k=KE)
                            gpsimd.dma_start(dst, src).then_inc(s_wq[s % 2].h, 16)
                            s_wq[s % 2].n += 16
                            gpsimd.dma_start(casts[s][0], casts[s][1]
                                             ).then_inc(s_a1sw.h, 16)
                            s_a1sw.n += 16
                        for f in range(NF):
                            gpsimd.dma_start(gate_bf[f], wg_in[:, f * 128:(f + 1) * 128]
                                             ).then_inc(s_wcast.h, 16)
                            gpsimd.dma_start(up_bf[f], wu_in[:, f * 128:(f + 1) * 128]
                                             ).then_inc(s_wcast.h, 16)
                            s_wcast.n += 32
                        for m in range(KE):
                            gpsimd.dma_start(down_bf[m], wd_in[:, m * 128:(m + 1) * 128]
                                             ).then_inc(s_wcast.h, 16)
                            s_wcast.n += 16

                    @blk.tensor
                    def _(tensor):
                        tensor.wait_ge(s_cst.h, CST0)
                        tensor.wait_ge(s_xt.h, 16 * KE * NC_)

                        def stats(v):
                            tensor.wait_ge(s_a1r.h, v + 1)
                            if v >= 2:
                                tensor.wait_ge(s_a1sf.h, v - 1)
                            tensor.matmul(p_sa[v % 2][:], ones128[:], b2(rbf, v),
                                          start=True, stop=True)
                            ii = tensor.matmul(p_sb[v % 2][:], ones128[:], b2(rsqb, v),
                                               start=True, stop=True)
                            s_a1st.inc(ii)

                        for s in range(2 * H):
                            for c in range(NC_):
                                u = s * NC_ + c
                                tensor.wait_ge(s_wq[s % 2].h, 16 * (s // 2 + 1))
                                if u >= 2:
                                    tensor.wait_ge(s_a1pf.h, u - 1)
                                wqs = wq[:, (s % 2) * KE * 128:(s % 2 + 1) * KE * 128]
                                ii = None
                                for k in range(KE):
                                    ii = tensor.matmul(
                                        p_p[u % 2][:],
                                        wqs[:, k * 128:(k + 1) * 128],
                                        xTs(k, c),
                                        start=(k == 0), stop=(k == KE - 1))
                                s_a1mm.inc(ii)
                                if u >= 1:
                                    stats(u - 1)
                        stats(255)

                    @blk.vector
                    def _(vector):
                        vector.wait_ge(s_cst.h, CST0 + 32)
                        for u in range(258):
                            if u < 256:
                                s, c = divmod(u, NC_)
                                cc = _chunk(c)
                                vector.wait_ge(s_a1mm.h, u + 1)
                                if u >= 2:
                                    vector.wait_ge(s_a1st.h, u - 1)
                                sw = slice((u % 2) * 512, (u % 2 + 1) * 512)
                                vector.tensor_copy(swp[0:64, sw],
                                                   p_p[u % 2][64:128, :])
                                vector.tensor_copy(swp[64:128, sw],
                                                   p_p[u % 2][0:64, :])
                                ii = vector.tensor_mul(tc[:], p_p[u % 2][:],
                                                       c2sb[:, cc])
                                s_a1pf.inc(ii)
                                vector.tensor_mul(tsb[:], swp[:, sw], s2sb[:, cc])
                                vector.tensor_add(b3(rbuf, u), tc[:], tsb[:])
                                vector.tensor_copy(b2(rbf, u), b3(rbuf, u))
                                ii = vector.tensor_mul(b2(rsqb, u), b2(rbf, u),
                                                       b2(rbf, u))
                                s_a1r.inc(ii)
                            if 0 <= u - 1 < 256:
                                v = u - 1
                                vector.wait_ge(s_a1sf.h, v + 1)
                                if v >= 1:
                                    vector.wait_ge(s_a1rb.h, v)
                                vector.tensor_mul(tb2[:], b2(mbb, v), b2(mbb, v))
                                ii = vector.tensor_sub(varb[:], b2(uqb, v), tb2[:])
                                s_a1var.inc(ii)
                            if 0 <= u - 2 < 256:
                                w = u - 2
                                vector.wait_ge(s_a1rb.h, w + 1)
                                if w >= 2:
                                    vector.wait_ge(s_a1qo.h, w - 1)
                                vector.tensor_sub(tb2f[:], b3(rbuf, w), b2(mbb, w))
                                ii = vector.tensor_mul(b2(rlb, w), tb2f[:],
                                                       b2(rbb, w))
                                s_a1rl.inc(ii)

                    @blk.scalar
                    def _(scalar):
                        scalar.wait_ge(s_cst.h, CST0)
                        preload_rsqrt(scalar)
                        for u in range(258):
                            if 0 <= u - 1 < 256:
                                v = u - 1
                                scalar.wait_ge(s_a1st.h, v + 1)
                                if v >= 2:
                                    scalar.wait_ge(s_a1rl.h, v - 1)
                                    scalar.wait_ge(s_a1var.h, v - 1)
                                scalar.activation(b2(mbb, v), p_sa[v % 2][:],
                                                  AF.Identity, scale=1.0 / 128)
                                ii = scalar.activation(b2(uqb, v), p_sb[v % 2][:],
                                                       AF.Identity, scale=1.0 / 128)
                                s_a1sf.inc(ii)
                                scalar.wait_ge(s_a1var.h, v + 1)
                                ii = rsqrt_act(scalar, b2(rbb, v), varb[:], LN_EPS)
                                s_a1rb.inc(ii)
                            if 0 <= u - 2 < 256:
                                w = u - 2
                                scalar.wait_ge(s_a1rl.h, w + 1)
                                if w >= 2:
                                    scalar.wait_ge(s_a1sp[w % 2].h, 16 * (w // 2))
                                ii = scalar.activation(b2(qob, w), b2(rlb, w),
                                                       AF.Identity,
                                                       scale=gam[:, 0:1],
                                                       bias=bet[:, 0:1])
                                s_a1qo.inc(ii)

                barrier_block(sync_waits=[(s_a1sp[0], s_a1sp[0].n), (s_a1sp[1], s_a1sp[1].n)])
                if upto == 'A1':
                    drain_block()
                    return nc

            with contextlib.ExitStack() as pa2:
                wvc = pa2.enter_context(nc.sbuf_tensor("wvc_sb", [128, KE * E], BF16))
                vst = pa2.enter_context(nc.sbuf_tensor("vst", [128, 2 * 512], BF16))
                p_v = [pa2.enter_context(nc.psum_tensor("pv%d" % i, [128, 512], F32))
                       for i in range(4)]

                with nc.Block() as blk:
                    @blk.gpsimd
                    def _(gpsimd):
                        gpsimd.wait_ge(s_ph.h, PHC[0])
                        gpsimd.wait_ge(s_a1sw.h, s_a1sw.n)
                        for k in range(KE):
                            gpsimd.dma_start(
                                wvc[:, k * E:(k + 1) * E],
                                wvq_bf[k * 128:(k + 1) * 128, :]
                            ).then_inc(s_wv.h, 16)
                            s_wv.n += 16

                    @blk.scalar
                    def _(scalar):
                        scalar.wait_ge(s_ph.h, PHC[0])
                        for i in range(NTI):
                            for vc in range(4):
                                u = 4 * i + vc
                                scalar.wait_ge(s_a2mm.h, u + 1)
                                if u >= 2:
                                    scalar.wait_ge(s_a2sp[u % 2].h, 16 * (u // 2))
                                ii = scalar.activation(
                                    vst[:, (u % 2) * 512:(u % 2 + 1) * 512],
                                    p_v[vc][:], AF.Identity)
                                s_a2ev.inc(ii)

                    @blk.tensor
                    def _(tensor):
                        tensor.wait_ge(s_ph.h, PHC[0])
                        tensor.wait_ge(s_wv.h, 256)
                        for i in range(NTI):
                            if i >= 1:
                                tensor.wait_ge(s_a2ev.h, 4 * i)
                            ii = None
                            for k in range(KE):
                                for vc in range(4):
                                    ii = tensor.matmul(
                                        p_v[vc][:],
                                        xT[:, k * N + i * 128: k * N + (i + 1) * 128],
                                        wvc[:, k * E + vc * 512: k * E + (vc + 1) * 512],
                                        start=(k == 0), stop=(k == KE - 1))
                                    if k == KE - 1:
                                        s_a2mm.inc(ii)

                    @blk.sync
                    def _(sync):
                        sync.wait_ge(s_ph.h, PHC[0])
                        for i in range(NTI):
                            for vc in range(4):
                                u = 4 * i + vc
                                sync.wait_ge(s_a2ev.h, u + 1)
                                sync.dma_start(
                                    vsc[i][:, vc * 512:(vc + 1) * 512],
                                    vst[:, (u % 2) * 512:(u % 2 + 1) * 512]
                                ).then_inc(s_a2sp[u % 2].h, 16)
                                s_a2sp[u % 2].n += 16

                barrier_block(sync_waits=[(s_a2sp[0], s_a2sp[0].n),
                                          (s_a2sp[1], s_a2sp[1].n)])
                if upto == 'A2':
                    drain_block()
                    return nc

        with contextlib.ExitStack() as pb:
            masks = pb.enter_context(nc.sbuf_tensor("msk_sb", [128, 4 * 512], BF16))
            qsb = pb.enter_context(nc.sbuf_tensor("qsb", [128, 2 * N], BF16))
            ksb = pb.enter_context(nc.sbuf_tensor("ksb", [128, 2 * N], BF16))
            vsb = pb.enter_context(nc.sbuf_tensor("vsb", [128, 2 * N], BF16))
            a2b = pb.enter_context(nc.sbuf_tensor("a2b", [128, 4 * 512], BF16))
            a4b = pb.enter_context(nc.sbuf_tensor("a4b", [128, 4 * 512], BF16))
            rbB = pb.enter_context(nc.sbuf_tensor("rbB", [128, 2 * 512], F32))
            astg = pb.enter_context(nc.sbuf_tensor("astg", [128, 2 * 512], BF16))
            xstg = pb.enter_context(nc.sbuf_tensor("xstg", [128, 2 * 512], F32))
            xbstg = pb.enter_context(nc.sbuf_tensor("xbstg", [128, 2 * 512], BF16))
            xlostg = pb.enter_context(nc.sbuf_tensor("xlostg", [128, 2 * 512], BF16))
            p_s = [pb.enter_context(nc.psum_tensor("ps%d" % i, [128, 512], F32))
                   for i in range(4)]
            p_o = [pb.enter_context(nc.psum_tensor("po%d" % i, [128, 512], F32))
                   for i in range(2)]
            p_d = [pb.enter_context(nc.psum_tensor("pd%d" % i, [128, 512], F32))
                   for i in range(2)]
            LAG = 3

            def xlt(t):
                r, cb = divmod(t, 4)
                return (slice(r * 128, (r + 1) * 128), slice(cb * 512, (cb + 1) * 512))

            units = []
            U = 0
            for h in range(H):
                for c in range(NC_):
                    C = h * NC_ + c
                    for j in range(4 * c + 4):
                        units.append((h, c, j, C, U, j - 4 * c))
                        U += 1
            NU = len(units)
            chunk_last = {}
            for (h, c, j, C, u, p) in units:
                chunk_last[C] = u

            def cs_after(u):
                return (u + 1) - ((u + 1) // 6)

            bep_evac = {}
            bep_astg = {}
            _bep = 0
            for idx, pu in enumerate(units):
                if pu[4] % 6 == 5:
                    _bep += 1
                    bep_evac[pu[4]] = _bep
                if idx >= LAG:
                    pu2 = units[idx - LAG]
                    if pu2[4] == chunk_last[pu2[3]]:
                        _bep += 1
                        bep_astg[pu2[3]] = _bep
            for pu2 in units[-LAG:]:
                if pu2[4] == chunk_last[pu2[3]]:
                    _bep += 1
                    bep_astg[pu2[3]] = _bep

            with nc.Block() as blk:
                @blk.gpsimd
                def _(gpsimd):
                    gpsimd.wait_ge(s_ph.h, PHC[0])

                    def xloads(t):
                        rs, cs2 = xlt(t)
                        o = (t % 2) * 512
                        gpsimd.dma_start(xstg[:, o:o + 512], x_in[rs, cs2]
                                         ).then_inc(s_xlld[t % 2].h, 16)
                        gpsimd.dma_start(xbstg[:, o:o + 512], xbf[rs, cs2]
                                         ).then_inc(s_xlld[t % 2].h, 16)
                        s_xlld[t % 2].n += 32

                    xloads(0)
                    xloads(1)
                    for t in range(128):
                        rs, cs2 = xlt(t)
                        o = (t % 2) * 512
                        gpsimd.wait_ge(s_xlld[t % 2].h, 32 * (t // 2 + 1))
                        if t >= 2:
                            gpsimd.wait_ge(s_xlsp[t % 2].h, 16 * (t // 2))
                        ii = gpsimd.tensor_sub(xlostg[:, o:o + 512],
                                               xstg[:, o:o + 512],
                                               xbstg[:, o:o + 512])
                        s_xl.inc(ii)
                        gpsimd.wait_ge(s_xl.h, s_xl.n)
                        gpsimd.dma_start(xlo[rs, cs2], xlostg[:, o:o + 512]
                                         ).then_inc(s_xlsp[t % 2].h, 16)
                        s_xlsp[t % 2].n += 16
                        if t + 2 < 128:
                            xloads(t + 2)

                @blk.sync
                def _(sync):
                    sync.wait_ge(s_ph.h, PHC[0])
                    sync.dma_start(masks[:], msk_in).then_inc(s_cst.h, 16)
                    s_cst.n += 16
                    CSTB[0] = s_cst.n

                    def loads(h):
                        if h >= 2:
                            sync.wait_ge(s_bsp[0].h, 64 * (h - 1))
                            sync.wait_ge(s_bsp[1].h, 64 * (h - 1))
                        o = (h % 2) * N
                        sync.dma_start(qsb[:, o:o + N], qT[h]).then_inc(s_bld[h % 2].h, 16)
                        s_bld[h % 2].n += 16
                        sync.dma_start(ksb[:, o:o + N], kTs[h]).then_inc(s_bld[h % 2].h, 16)
                        s_bld[h % 2].n += 16
                        src = vsc[:, :, h * 128:(h + 1) * 128].rearrange("j p d -> p j d")
                        dst = vsb[:, o:o + N].rearrange("p (j d) -> p j d", j=NTI)
                        sync.dma_start(dst, src).then_inc(s_bld[h % 2].h, 16)
                        s_bld[h % 2].n += 16

                    loads(0)
                    loads(1)
                    for h in range(H):
                        for c in range(NC_):
                            C = h * NC_ + c
                            sync.wait_ge(s_bep.h, bep_astg[C])
                            sync.dma_start(aT[h][:, _chunk(c)],
                                           astg[:, (C % 2) * 512:(C % 2 + 1) * 512]
                                           ).then_inc(s_bsp[C % 2].h, 16)
                            s_bsp[C % 2].n += 16
                        if h + 2 < H:
                            loads(h + 2)

                @blk.tensor
                def _(tensor):
                    tensor.wait_ge(s_ph.h, PHC[0])

                    def mm2den(pu):
                        (h, c, j, C, u, p) = pu
                        o = (h % 2) * N
                        if j == 0 and C >= 2:
                            tensor.wait_ge(s_bep.h, bep_astg[C - 2])
                        tensor.wait_ge(s_bsq2.h, u + 1)
                        tensor.matmul(p_o[C % 2][:],
                                      vsb[:, o + j * 128: o + (j + 1) * 128],
                                      a4b[:, (u % 4) * 512:(u % 4 + 1) * 512],
                                      start=(j == 0), stop=(j == 4 * c + 3))
                        ii = tensor.matmul(p_d[C % 2][:], ones128[:],
                                           a4b[:, (u % 4) * 512:(u % 4 + 1) * 512],
                                           start=(j == 0), stop=(j == 4 * c + 3))
                        s_bmm2.inc(ii)

                    for idx, pu in enumerate(units):
                        (h, c, j, C, u, p) = pu
                        o = (h % 2) * N
                        if j == 0:
                            tensor.wait_ge(s_bld[h % 2].h, 48 * (h // 2 + 1))
                        if u >= 4:
                            w = u - 4
                            if w % 6 == 5:
                                tensor.wait_ge(s_bep.h, bep_evac[w])
                            else:
                                tensor.wait_ge(s_bsq1.h, cs_after(w))
                        ii = tensor.matmul(p_s[u % 4][:],
                                           ksb[:, o + j * 128: o + (j + 1) * 128],
                                           qsb[:, o + c * 512: o + (c + 1) * 512],
                                           start=True, stop=True)
                        s_bmm1.inc(ii)
                        if idx >= LAG:
                            mm2den(units[idx - LAG])
                    for pu in units[-LAG:]:
                        mm2den(pu)

                @blk.scalar
                def _(scalar):
                    scalar.wait_ge(s_ph.h, PHC[0])
                    scalar.preload_act_table(AF.Square)
                    for pu in units:
                        (h, c, j, C, u, p) = pu
                        if u % 6 == 5:
                            continue
                        scalar.wait_ge(s_bmm1.h, u + 1)
                        if u >= 4:
                            scalar.wait_ge(s_bsq2.h, u - 3)
                        ii = scalar.activation(a2b[:, (u % 4) * 512:(u % 4 + 1) * 512],
                                               p_s[u % 4][:], AF.Square)
                        s_bsq1.inc(ii)

                @blk.vector
                def _(vector):
                    vector.wait_ge(s_ph.h, PHC[0])
                    vector.wait_ge(s_cst.h, CSTB[0])

                    def denom(pu):
                        (h, c, j, C, u, p) = pu
                        if u != chunk_last[C]:
                            return
                        vector.wait_ge(s_bmm2.h, u + 1)
                        if C >= 2:
                            vector.wait_ge(s_bsp[C % 2].h, 16 * (C // 2))
                        cs = slice((C % 2) * 512, (C % 2 + 1) * 512)
                        vector.reciprocal(rbB[:, cs], p_d[C % 2][:])
                        ii = vector.tensor_mul(astg[:, cs], p_o[C % 2][:],
                                               rbB[:, cs])
                        s_bep.inc(ii)

                    for idx, pu in enumerate(units):
                        (h, c, j, C, u, p) = pu
                        sl = slice((u % 4) * 512, (u % 4 + 1) * 512)
                        if u % 6 == 5:
                            vector.wait_ge(s_bmm1.h, u + 1)
                            vector.tensor_copy(a2b[:, sl], p_s[u % 4][:])
                            ii = vector.tensor_mul(a2b[:, sl], a2b[:, sl],
                                                   a2b[:, sl])
                            s_bep.inc(ii)
                        else:
                            vector.wait_ge(s_bsq1.h, cs_after(u))
                        if u >= 4:
                            vector.wait_ge(s_bmm2.h, u - 3)
                        ii = vector.tensor_mul(a4b[:, sl], a2b[:, sl], a2b[:, sl])
                        if p >= 0:
                            ii = vector.tensor_mul(a4b[:, sl], a4b[:, sl],
                                                   masks[:, p * 512:(p + 1) * 512])
                        s_bsq2.inc(ii)
                        if idx >= LAG:
                            denom(units[idx - LAG])
                    for pu in units[-LAG:]:
                        denom(pu)

            barrier_block(sync_waits=[(s_bsp[0], s_bsp[0].n), (s_bsp[1], s_bsp[1].n),
                                      (s_xlsp[0], s_xlsp[0].n),
                                      (s_xlsp[1], s_xlsp[1].n)])
            if upto == 'B':
                drain_block()
                return nc

        with contextlib.ExitStack() as pc:
            wout = pc.enter_context(nc.sbuf_tensor("wout_sb", [128, KE * E], BF16))
            wrsb = pc.enter_context(nc.sbuf_tensor("wr_sb", [128, KE * 128], F32))
            asb = pc.enter_context(nc.sbuf_tensor("asb", [128, KE * 512], BF16))
            x0sb = pc.enter_context(nc.sbuf_tensor("x0sb", [128, KE * 512], BF16))
            x0lo = pc.enter_context(nc.sbuf_tensor("x0lo", [128, KE * 512], BF16))
            ysb = pc.enter_context(nc.sbuf_tensor("ysb", [128, KE * 512], F32))
            ybf = pc.enter_context(nc.sbuf_tensor("ybf", [128, 2 * 512], BF16))
            x1f = pc.enter_context(nc.sbuf_tensor("x1f", [128, KE * 512], F32))
            sqst = pc.enter_context(nc.sbuf_tensor("sqst", [128, 2 * 512], BF16))
            mbc = pc.enter_context(nc.sbuf_tensor("mbc", [128, 2 * 512], BF16))
            uqc = pc.enter_context(nc.sbuf_tensor("uqc", [128, 2 * 512], F32))
            varc = pc.enter_context(nc.sbuf_tensor("varc", [128, 2 * 512], F32))
            rbc = pc.enter_context(nc.sbuf_tensor("rbc", [128, 2 * 512], F32))
            tbc = pc.enter_context(nc.sbuf_tensor("tbc", [128, 512], F32))
            msbst = pc.enter_context(nc.sbuf_tensor("msbst", [128, 2 * 512], BF16))
            p_w = [pc.enter_context(nc.psum_tensor("pw%d" % i, [128, 512], F32))
                   for i in range(2)]
            p_s1 = pc.enter_context(nc.psum_tensor("ps1c", [128, 512], F32))
            p_s2 = pc.enter_context(nc.psum_tensor("ps2c", [128, 512], F32))
            p_r = pc.enter_context(nc.psum_tensor("prc", [128, 512], F32))

            def cbuf(buf, c, m):
                return buf[:, m * 512:(m + 1) * 512]

            with nc.Block() as blk:
                @blk.gpsimd
                def _(gpsimd):
                    gpsimd.wait_ge(s_ph.h, PHC[0])
                    for k in range(KE):
                        gpsimd.dma_start(wout[:, k * E:(k + 1) * E],
                                         wout_bf[k * 128:(k + 1) * 128, :]
                                         ).then_inc(s_cw.h, 16)
                        s_cw.n += 16
                        gpsimd.dma_start(wrsb[:, k * 128:(k + 1) * 128],
                                         wr_in[k * 128:(k + 1) * 128, :]
                                         ).then_inc(s_cw.h, 16)
                        s_cw.n += 16
                    for c in range(NC_):
                        for m in range(KE):
                            cm = c * KE + m
                            gpsimd.wait_ge(s_cx1.h, cm + 1)
                            gpsimd.dma_start(x1sc[m][:, _chunk(c)],
                                             x1f[:, m * 512:(m + 1) * 512]
                                             ).then_inc(s_cx1sp.h, 16)
                            s_cx1sp.n += 16

                @blk.sync
                def _(sync):
                    sync.wait_ge(s_ph.h, PHC[0])

                    def loads(c):
                        if c >= 1:
                            sync.wait_ge(s_cmm.h, 16 * c)
                        for m in range(KE):
                            sync.dma_start(cbuf(asb, c, m), aT[m][:, _chunk(c)]
                                           ).then_inc(s_cld[0].h, 16)
                            s_cld[0].n += 16

                    def x0loads(c):
                        if c >= 1:
                            sync.wait_ge(s_cev.h, 16 * c)
                        for m in range(KE):
                            sync.dma_start(
                                x0sb[:, m * 512:(m + 1) * 512],
                                xbf[_chunk(c), m * 128:(m + 1) * 128],
                                transpose=True).then_inc(s_cx0.h, 16)
                            sync.dma_start(
                                x0lo[:, m * 512:(m + 1) * 512],
                                xlo[_chunk(c), m * 128:(m + 1) * 128],
                                transpose=True).then_inc(s_cx0.h, 16)
                            s_cx0.n += 32

                    loads(0)
                    x0loads(0)
                    for c in range(NC_):
                        if c + 1 < NC_:
                            loads(c + 1)
                            x0loads(c + 1)
                        sync.wait_ge(s_cms.h, c + 1)
                        sync.dma_start(msc[c], msbst[:, (c % 2) * 512:(c % 2 + 1) * 512]
                                       ).then_inc(s_cmsp[c % 2].h, 16)
                        s_cmsp[c % 2].n += 16

                @blk.tensor
                def _(tensor):
                    tensor.wait_ge(s_ph.h, PHC[0])
                    tensor.wait_ge(s_cw.h, 512)

                    def stats(c, m):
                        cm = c * KE + m
                        tensor.wait_ge(s_cev.h, cm + 1)
                        tensor.wait_ge(s_csq.h, cm + 1)
                        if m == 0:
                            tensor.wait_ge(s_csf.h, c)
                        tensor.matmul(p_s1[:], ones128[:],
                                      ybf[:, (cm % 2) * 512:(cm % 2 + 1) * 512],
                                      start=(m == 0), stop=(m == KE - 1))
                        ii = tensor.matmul(p_s2[:], ones128[:],
                                           sqst[:, (cm % 2) * 512:(cm % 2 + 1) * 512],
                                           start=(m == 0), stop=(m == KE - 1))
                        s_cst2.inc(ii)

                    for c in range(NC_):
                        tensor.wait_ge(s_cld[0].h, 256 * (c + 1))
                        for m in range(KE):
                            cm = c * KE + m
                            if cm >= 2:
                                tensor.wait_ge(s_cev.h, cm - 1)
                            ii = None
                            for k in range(KE):
                                ii = tensor.matmul(
                                    p_w[cm % 2][:],
                                    wout[:, k * E + m * 128: k * E + (m + 1) * 128],
                                    cbuf(asb, c, k),
                                    start=(k == 0), stop=(k == KE - 1))
                            s_cmm.inc(ii)
                            if m >= 1:
                                stats(c, m - 1)
                        stats(c, KE - 1)
                        tensor.wait_ge(s_cx1.h, KE * (c + 1))
                        if c >= 1:
                            tensor.wait_ge(s_cms.h, c)
                        ii = None
                        for k in range(KE):
                            ii = tensor.matmul(p_r[:],
                                               wrsb[:, k * 128:(k + 1) * 128],
                                               x1f[:, k * 512:(k + 1) * 512],
                                               start=(k == 0), stop=(k == KE - 1))
                        s_crt.inc(ii)

                @blk.vector
                def _(vector):
                    vector.wait_ge(s_ph.h, PHC[0])
                    for c in range(NC_):
                        vector.wait_ge(s_cx0.h, 512 * (c + 1))
                        for m in range(KE):
                            cm = c * KE + m
                            vector.wait_ge(s_cmm.h, cm + 1)
                            if cm >= 2:
                                vector.wait_ge(s_csq.h, cm - 1)
                                vector.wait_ge(s_cst2.h, cm - 1)
                            vector.tensor_add(ysb[:, m * 512:(m + 1) * 512],
                                              p_w[cm % 2][:],
                                              x0sb[:, m * 512:(m + 1) * 512])
                            vector.tensor_add(ysb[:, m * 512:(m + 1) * 512],
                                              ysb[:, m * 512:(m + 1) * 512],
                                              x0lo[:, m * 512:(m + 1) * 512])
                            ii = vector.tensor_copy(ybf[:, (cm % 2) * 512:(cm % 2 + 1) * 512],
                                                    ysb[:, m * 512:(m + 1) * 512])
                            s_cev.inc(ii)
                        vector.wait_ge(s_csf.h, c + 1)
                        if c >= 2:
                            vector.wait_ge(s_crb.h, c - 1)
                        cs = slice((c % 2) * 512, (c % 2 + 1) * 512)
                        vector.tensor_mul(tbc[:], mbc[:, cs], mbc[:, cs])
                        ii = vector.tensor_sub(varc[:, cs], uqc[:, cs], tbc[:])
                        s_cvar.inc(ii)
                        vector.wait_ge(s_crb.h, c + 1)
                        if c >= 1:
                            vector.wait_ge(s_crt.h, c)
                            vector.wait_ge(s_cx1sp.h, 256 * c)
                        for m in range(KE):
                            cm = c * KE + m
                            vector.tensor_sub(tbc[:], ysb[:, m * 512:(m + 1) * 512],
                                              mbc[:, cs])
                            ii = vector.tensor_mul(x1f[:, m * 512:(m + 1) * 512],
                                                   tbc[:], rbc[:, cs])
                            s_cx1.inc(ii)
                        vector.wait_ge(s_crt.h, c + 1)
                        if c >= 2:
                            vector.wait_ge(s_cmsp[c % 2].h, 16 * (c // 2))
                        ii = vector.tensor_scalar(msbst[:, cs], p_r[:],
                                                  brsb[:, 0:1], 0.0,
                                                  ALU.add, ALU.is_gt)
                        s_cms.inc(ii)

                @blk.scalar
                def _(scalar):
                    scalar.wait_ge(s_ph.h, PHC[0])
                    preload_rsqrt(scalar)
                    for c in range(NC_):
                        cs = slice((c % 2) * 512, (c % 2 + 1) * 512)
                        for m in range(KE):
                            cm = c * KE + m
                            scalar.wait_ge(s_cev.h, cm + 1)
                            if cm >= 2:
                                scalar.wait_ge(s_cst2.h, cm - 1)
                            ii = scalar.activation(
                                sqst[:, (cm % 2) * 512:(cm % 2 + 1) * 512],
                                ybf[:, (cm % 2) * 512:(cm % 2 + 1) * 512], AF.Square)
                            s_csq.inc(ii)
                        scalar.wait_ge(s_cst2.h, 16 * (c + 1))
                        if c >= 2:
                            scalar.wait_ge(s_cvar.h, c - 1)
                            scalar.wait_ge(s_cx1.h, 16 * (c - 1))
                        scalar.activation(mbc[:, cs], p_s1[:], AF.Identity,
                                          scale=1.0 / E)
                        ii = scalar.activation(uqc[:, cs], p_s2[:], AF.Identity,
                                               scale=1.0 / E)
                        s_csf.inc(ii)
                        scalar.wait_ge(s_cvar.h, c + 1)
                        ii = rsqrt_act(scalar, rbc[:, cs], varc[:, cs], LN_EPS)
                        s_crb.inc(ii)

            barrier_block(sync_waits=[(s_cx1sp, s_cx1sp.n), (s_cmsp[0], s_cmsp[0].n), (s_cmsp[1], s_cmsp[1].n)])
            if upto == 'C':
                drain_block()
                return nc

        with contextlib.ExitStack() as pd1:
            wvsb = pd1.enter_context(nc.sbuf_tensor("wv_sb", [128, KE * E], BF16))
            x1d = pd1.enter_context(nc.sbuf_tensor("x1d", [128, 2 * KE * 512], BF16))
            xqb = pd1.enter_context(nc.sbuf_tensor("xqb", [128, 2 * 512], BF16))
            h1sb = pd1.enter_context(nc.sbuf_tensor("h1sb", [128, 2 * KE * 512], BF16))
            ust = pd1.enter_context(nc.sbuf_tensor("ustd1", [128, 512], F32))
            rb1 = pd1.enter_context(nc.sbuf_tensor("rb1", [128, 2 * 512], BF16))
            vstd = pd1.enter_context(nc.sbuf_tensor("vstd", [128, 2 * 512], BF16))
            td1 = pd1.enter_context(nc.sbuf_tensor("td1", [128, 512], BF16))
            p_ss = pd1.enter_context(nc.psum_tensor("pssd1", [128, 512], F32))
            p_v1 = [pd1.enter_context(nc.psum_tensor("pv1%d" % i, [128, 512], F32))
                    for i in range(2)]

            def dbuf(buf, c, m):
                o = ((c % 2) * KE + m) * 512
                return buf[:, o:o + 512]

            with nc.Block() as blk:
                @blk.gpsimd
                def _(gpsimd):
                    gpsimd.wait_ge(s_ph.h, PHC[0])
                    for k in range(KE):
                        gpsimd.dma_start(wvsb[:, k * E:(k + 1) * E],
                                         wvf_bf[k * 128:(k + 1) * 128, :]
                                         ).then_inc(s_d1w.h, 16)
                        s_d1w.n += 16

                @blk.sync
                def _(sync):
                    sync.wait_ge(s_ph.h, PHC[0])
                    for c in range(NC_):
                        if c >= 2:
                            sync.wait_ge(s_d1h1.h, 16 * (c - 1))
                        for m in range(KE):
                            sync.dma_start(dbuf(x1d, c, m), x1sc[m][:, _chunk(c)]
                                           ).then_inc(s_d1ld[c % 2].h, 16)
                            s_d1ld[c % 2].n += 16

                @blk.vector
                def _(vector):
                    vector.wait_ge(s_ph.h, PHC[0])
                    for c in range(NC_):
                        vector.wait_ge(s_d1ld[c % 2].h, 256 * (c // 2 + 1))
                        for k in range(KE):
                            ck = c * KE + k
                            if ck >= 2:
                                vector.wait_ge(s_d1ss.h, ck - 1)
                            ii = vector.tensor_mul(
                                xqb[:, (ck % 2) * 512:(ck % 2 + 1) * 512],
                                dbuf(x1d, c, k), dbuf(x1d, c, k))
                            s_d1xq.inc(ii)
                        vector.wait_ge(s_d1rb.h, c + 1)
                        if c >= 2:
                            vector.wait_ge(s_d1mm.h, 16 * (c - 1))
                        for k in range(KE):
                            vector.tensor_mul(td1[:], dbuf(x1d, c, k),
                                              rb1[:, (c % 2) * 512:(c % 2 + 1) * 512])
                            ii = vector.tensor_scalar(dbuf(h1sb, c, k), td1[:],
                                                      w1sb[:, k:k + 1], None, ALU.mult)
                            s_d1h1.inc(ii)

                @blk.tensor
                def _(tensor):
                    tensor.wait_ge(s_ph.h, PHC[0])
                    tensor.wait_ge(s_d1w.h, 256)
                    for c in range(NC_):
                        for k in range(KE):
                            ck = c * KE + k
                            tensor.wait_ge(s_d1xq.h, ck + 1)
                            if c >= 1 and k == 0:
                                tensor.wait_ge(s_d1rb.h, c)
                            ii = tensor.matmul(
                                p_ss[:], ones128[:],
                                xqb[:, (ck % 2) * 512:(ck % 2 + 1) * 512],
                                start=(k == 0), stop=(k == KE - 1))
                            s_d1ss.inc(ii)
                        tensor.wait_ge(s_d1h1.h, 16 * (c + 1))
                        for m in range(KE):
                            cm = c * KE + m
                            if cm >= 2:
                                tensor.wait_ge(s_d1ev.h, cm - 1)
                            ii = None
                            for k in range(KE):
                                ii = tensor.matmul(
                                    p_v1[cm % 2][:],
                                    wvsb[:, k * E + m * 128: k * E + (m + 1) * 128],
                                    dbuf(h1sb, c, k),
                                    start=(k == 0), stop=(k == KE - 1))
                            s_d1mm.inc(ii)

                @blk.scalar
                def _(scalar):
                    scalar.wait_ge(s_ph.h, PHC[0])
                    preload_rsqrt(scalar)
                    for c in range(NC_):
                        scalar.wait_ge(s_d1ss.h, 16 * (c + 1))
                        scalar.activation(ust[:], p_ss[:], AF.Identity, scale=1.0 / E)
                        ii = rsqrt_act(scalar,
                                       rb1[:, (c % 2) * 512:(c % 2 + 1) * 512],
                                       ust[:], RMS_EPS)
                        s_d1rb.inc(ii)
                        for m in range(KE):
                            cm = c * KE + m
                            scalar.wait_ge(s_d1mm.h, cm + 1)
                            if cm >= 2:
                                scalar.wait_ge(s_d1sp[cm % 2].h, 16 * (cm // 2))
                            ii = scalar.activation(
                                vstd[:, (cm % 2) * 512:(cm % 2 + 1) * 512],
                                p_v1[cm % 2][:], AF.Identity, bias=bvsb[:, m:m + 1])
                            s_d1ev.inc(ii)
                            scalar.dma_start(vfsc[m][:, _chunk(c)],
                                             vstd[:, (cm % 2) * 512:(cm % 2 + 1) * 512]
                                             ).then_inc(s_d1sp[cm % 2].h, 16)
                            s_d1sp[cm % 2].n += 16

            barrier_block(scalar_waits=[(s_d1sp[0], s_d1sp[0].n), (s_d1sp[1], s_d1sp[1].n)])
            if upto == 'D1':
                drain_block()
                return nc

        with contextlib.ExitStack() as pd2:
            wosb = pd2.enter_context(nc.sbuf_tensor("wo_sb", [128, KE * E], BF16))
            x1d2 = pd2.enter_context(nc.sbuf_tensor("x1d2", [128, 2 * KE * 512], BF16))
            vd = pd2.enter_context(nc.sbuf_tensor("vd", [128, 2 * KE * 512], BF16))
            hsb = pd2.enter_context(nc.sbuf_tensor("hsb2", [128, KE * 512], BF16))
            hsqst = pd2.enter_context(nc.sbuf_tensor("hsqst", [128, 2 * 512], BF16))
            ust2 = pd2.enter_context(nc.sbuf_tensor("ustd2", [128, 512], F32))
            rb2 = pd2.enter_context(nc.sbuf_tensor("rb2", [128, 2 * 512], BF16))
            h2st = pd2.enter_context(nc.sbuf_tensor("h2st", [128, 2 * 512], BF16))
            td2 = pd2.enter_context(nc.sbuf_tensor("td2", [128, 512], BF16))
            p_w2 = [pd2.enter_context(nc.psum_tensor("pw2%d" % i, [128, 512], F32))
                    for i in range(2)]
            p_ss2 = pd2.enter_context(nc.psum_tensor("pss2", [128, 512], F32))

            def dbuf2(buf, c, m):
                o = ((c % 2) * KE + m) * 512
                return buf[:, o:o + 512]

            with nc.Block() as blk:
                @blk.gpsimd
                def _(gpsimd):
                    gpsimd.wait_ge(s_ph.h, PHC[0])
                    for k in range(KE):
                        gpsimd.dma_start(wosb[:, k * E:(k + 1) * E],
                                         wo_bf[k * 128:(k + 1) * 128, :]
                                         ).then_inc(s_d2w.h, 16)
                        s_d2w.n += 16

                @blk.sync
                def _(sync):
                    sync.wait_ge(s_ph.h, PHC[0])

                    def loads(c):
                        if c >= 2:
                            sync.wait_ge(s_d2mm.h, 16 * (c - 1))
                            sync.wait_ge(s_d2h.h, 16 * (c - 1))
                        for m in range(KE):
                            sync.dma_start(dbuf2(x1d2, c, m), x1sc[m][:, _chunk(c)]
                                           ).then_inc(s_d2ld[c % 2].h, 16)
                            s_d2ld[c % 2].n += 16
                            sync.dma_start(dbuf2(vd, c, m), vfsc[m][:, _chunk(c)]
                                           ).then_inc(s_d2ld[c % 2].h, 16)
                            s_d2ld[c % 2].n += 16

                    loads(0)
                    loads(1)
                    for c in range(NC_):
                        for m in range(KE):
                            cm = c * KE + m
                            sync.wait_ge(s_d2h.h, cm + 1)
                            sync.dma_start(hsc[m][:, _chunk(c)],
                                           hsb[:, m * 512:(m + 1) * 512]
                                           ).then_inc(s_d2hsp.h, 16)
                            s_d2hsp.n += 16
                        if c + 2 < NC_:
                            loads(c + 2)

                @blk.tensor
                def _(tensor):
                    tensor.wait_ge(s_ph.h, PHC[0])
                    tensor.wait_ge(s_d2w.h, 256)

                    def stats(c, m):
                        cm = c * KE + m
                        tensor.wait_ge(s_d2sq.h, cm + 1)
                        if m == 0:
                            tensor.wait_ge(s_d2rb.h, c)
                        ii = tensor.matmul(p_ss2[:], ones128[:],
                                           hsqst[:, (cm % 2) * 512:(cm % 2 + 1) * 512],
                                           start=(m == 0), stop=(m == KE - 1))
                        s_d2st.inc(ii)

                    for c in range(NC_):
                        tensor.wait_ge(s_d2ld[c % 2].h, 512 * (c // 2 + 1))
                        for m in range(KE):
                            cm = c * KE + m
                            if cm >= 2:
                                tensor.wait_ge(s_d2h.h, cm - 1)
                            ii = None
                            for k in range(KE):
                                ii = tensor.matmul(
                                    p_w2[cm % 2][:],
                                    wosb[:, k * E + m * 128: k * E + (m + 1) * 128],
                                    dbuf2(vd, c, k),
                                    start=(k == 0), stop=(k == KE - 1))
                            s_d2mm.inc(ii)
                            if m >= 1:
                                stats(c, m - 1)
                        stats(c, KE - 1)

                @blk.vector
                def _(vector):
                    vector.wait_ge(s_ph.h, PHC[0])
                    for c in range(NC_):
                        for m in range(KE):
                            cm = c * KE + m
                            vector.wait_ge(s_d2mm.h, cm + 1)
                            if m == 0:
                                vector.wait_ge(s_d2sq.h, 16 * c)
                                vector.wait_ge(s_d2hsp.h, 256 * c)
                            ii = vector.tensor_add(hsb[:, m * 512:(m + 1) * 512],
                                                   p_w2[cm % 2][:], dbuf2(x1d2, c, m))
                            s_d2h.inc(ii)
                        vector.wait_ge(s_d2rb.h, c + 1)
                        for m in range(KE):
                            cm = c * KE + m
                            if cm >= 2:
                                vector.wait_ge(s_d2sp[cm % 2].h, 16 * (cm // 2))
                            vector.tensor_mul(td2[:], hsb[:, m * 512:(m + 1) * 512],
                                              rb2[:, (c % 2) * 512:(c % 2 + 1) * 512])
                            ii = vector.tensor_scalar(
                                h2st[:, (cm % 2) * 512:(cm % 2 + 1) * 512],
                                td2[:], w2sb[:, m:m + 1], None, ALU.mult)
                            s_d2h2.inc(ii)

                @blk.scalar
                def _(scalar):
                    scalar.wait_ge(s_ph.h, PHC[0])
                    preload_rsqrt(scalar)
                    for c in range(NC_):
                        for m in range(KE):
                            cm = c * KE + m
                            scalar.wait_ge(s_d2h.h, cm + 1)
                            if cm >= 2:
                                scalar.wait_ge(s_d2st.h, cm - 1)
                            ii = scalar.activation(
                                hsqst[:, (cm % 2) * 512:(cm % 2 + 1) * 512],
                                hsb[:, m * 512:(m + 1) * 512], AF.Square)
                            s_d2sq.inc(ii)
                        scalar.wait_ge(s_d2st.h, 16 * (c + 1))
                        if c >= 2:
                            scalar.wait_ge(s_d2h2.h, 16 * (c - 1))
                        scalar.activation(ust2[:], p_ss2[:], AF.Identity, scale=1.0 / E)
                        ii = rsqrt_act(scalar,
                                       rb2[:, (c % 2) * 512:(c % 2 + 1) * 512],
                                       ust2[:], RMS_EPS)
                        s_d2rb.inc(ii)
                        for m in range(KE):
                            cm = c * KE + m
                            scalar.wait_ge(s_d2h2.h, cm + 1)
                            scalar.dma_start(h2sc[m][:, _chunk(c)],
                                             h2st[:, (cm % 2) * 512:(cm % 2 + 1) * 512]
                                             ).then_inc(s_d2sp[cm % 2].h, 16)
                            s_d2sp[cm % 2].n += 16

            barrier_block(sync_waits=[(s_d2hsp, s_d2hsp.n)],
                          scalar_waits=[(s_d2sp[0], s_d2sp[0].n),
                                        (s_d2sp[1], s_d2sp[1].n)])
            if upto == 'D2':
                drain_block()
                return nc

        with contextlib.ExitStack() as pd3:
            x1d3 = pd3.enter_context(nc.sbuf_tensor("x1d3", [128, KE * 512], BF16))
            hd3 = pd3.enter_context(nc.sbuf_tensor("hd3", [128, KE * 512], BF16))
            h2d = pd3.enter_context(nc.sbuf_tensor("h2d", [128, 2 * KE * 512], BF16))
            msbuf = pd3.enter_context(nc.sbuf_tensor("msbuf", [128, 512], BF16))
            wgf = pd3.enter_context(nc.sbuf_tensor("wgf", [128, 2 * KE * 128], BF16))
            wuf = pd3.enter_context(nc.sbuf_tensor("wuf", [128, 2 * KE * 128], BF16))
            wdm = pd3.enter_context(nc.sbuf_tensor("wdm", [128, 2 * NF * 128], BF16))
            gbuf = pd3.enter_context(nc.sbuf_tensor("gbuf", [128, NF * 512], BF16))
            sgst = pd3.enter_context(nc.sbuf_tensor("sgst", [128, 2 * 512], BF16))
            s1st = pd3.enter_context(nc.sbuf_tensor("s1st", [128, 512], BF16))
            s2st = pd3.enter_context(nc.sbuf_tensor("s2st", [128, 512], F32))
            s3st = pd3.enter_context(nc.sbuf_tensor("s3st", [128, 512], BF16))
            zsb = pd3.enter_context(nc.sbuf_tensor("zsb", [128, KE * 512], BF16))
            zqst = pd3.enter_context(nc.sbuf_tensor("zqst", [128, 2 * 512], BF16))
            mbz = pd3.enter_context(nc.sbuf_tensor("mbz", [128, 512], BF16))
            uqz = pd3.enter_context(nc.sbuf_tensor("uqz", [128, 512], BF16))
            varz = pd3.enter_context(nc.sbuf_tensor("varz", [128, 512], BF16))
            rbz = pd3.enter_context(nc.sbuf_tensor("rbz", [128, 512], BF16))
            tz = pd3.enter_context(nc.sbuf_tensor("tz", [128, 512], BF16))
            x2st = pd3.enter_context(nc.sbuf_tensor("x2st", [128, 2 * 512], BF16))
            ost = pd3.enter_context(nc.sbuf_tensor("ost", [128, 2 * 512], F32))
            p_g = pd3.enter_context(nc.psum_tensor("pg", [128, 512], F32))
            p_u = pd3.enter_context(nc.psum_tensor("pu", [128, 512], F32))
            p_mlp = [pd3.enter_context(nc.psum_tensor("pmlp%d" % i, [128, 512], F32))
                     for i in range(2)]
            p_z = pd3.enter_context(nc.psum_tensor("pz", [128, 512], F32))
            p_z2 = pd3.enter_context(nc.psum_tensor("pz2", [128, 512], F32))
            p_tr = [pd3.enter_context(nc.psum_tensor("ptr%d" % i, [128, 512], BF16))
                    for i in range(2)]

            with nc.Block() as blk:
                @blk.sync
                def _(sync):
                    sync.wait_ge(s_ph.h, PHC[0])
                    sync.wait_ge(s_wcast.h, s_wcast.n)

                    def wloads(c):
                        if c >= 2:
                            sync.wait_ge(s_d3mm.h, NF * (c - 1))
                        for m in range(KE):
                            sync.dma_start(
                                h2d[:, ((c % 2) * KE + m) * 512:((c % 2) * KE + m + 1) * 512],
                                h2sc[m][:, _chunk(c)]).then_inc(s_d3h2ld[c % 2].h, 16)
                            s_d3h2ld[c % 2].n += 16

                    def xloads(c):
                        if c >= 1:
                            sync.wait_ge(s_d3z.h, KE * c)
                        for m in range(KE):
                            sync.dma_start(x1d3[:, m * 512:(m + 1) * 512],
                                           x1sc[m][:, _chunk(c)]).then_inc(s_d3xld.h, 16)
                            s_d3xld.n += 16
                            sync.dma_start(hd3[:, m * 512:(m + 1) * 512],
                                           hsc[m][:, _chunk(c)]).then_inc(s_d3xld.h, 16)
                            s_d3xld.n += 16
                        sync.dma_start(msbuf[:], msc[c]).then_inc(s_d3xld.h, 16)
                        s_d3xld.n += 16

                    wloads(0)
                    xloads(0)
                    wloads(1)
                    for c in range(NC_):
                        for f in range(NF):
                            F_ = c * NF + f
                            if F_ >= 2:
                                sync.wait_ge(s_d3mm.h, F_ - 1)
                            o = (F_ % 2) * KE * 128
                            sync.dma_start(
                                wgf[:, o:o + KE * 128].rearrange("p (k c) -> p k c", k=KE),
                                gate_bf[f].rearrange("(k p) c -> p k c", p=128)
                            ).then_inc(s_d3wf.h, 16)
                            sync.dma_start(
                                wuf[:, o:o + KE * 128].rearrange("p (k c) -> p k c", k=KE),
                                up_bf[f].rearrange("(k p) c -> p k c", p=128)
                            ).then_inc(s_d3wf.h, 16)
                            s_d3wf.n += 32
                        for m in range(KE):
                            cm = c * KE + m
                            if cm >= 2:
                                sync.wait_ge(s_d3dn.h, cm - 1)
                            o = (cm % 2) * NF * 128
                            sync.dma_start(
                                wdm[:, o:o + NF * 128].rearrange("p (f c) -> p f c", f=NF),
                                down_bf[m].rearrange("(f p) c -> p f c", p=128)
                            ).then_inc(s_d3wd.h, 16)
                            s_d3wd.n += 16
                        if c + 1 < NC_:
                            xloads(c + 1)
                        if c + 2 < NC_:
                            wloads(c + 2)

                @blk.tensor
                def _(tensor):
                    tensor.wait_ge(s_ph.h, PHC[0])
                    for c in range(NC_):
                        tensor.wait_ge(s_d3h2ld[c % 2].h, 256 * (c // 2 + 1))
                        for f in range(NF):
                            F_ = c * NF + f
                            tensor.wait_ge(s_d3wf.h, 32 * (F_ + 1))
                            if F_ >= 1:
                                tensor.wait_ge(s_d3sg.h, F_)
                                tensor.wait_ge(s_d3gf.h, F_)
                            for k in range(KE):
                                tensor.matmul(
                                    p_g[:],
                                    wgf[:, (F_ % 2) * KE * 128 + k * 128:
                                        (F_ % 2) * KE * 128 + (k + 1) * 128],
                                    h2d[:, ((c % 2) * KE + k) * 512:
                                        ((c % 2) * KE + k + 1) * 512],
                                    start=(k == 0), stop=(k == KE - 1))
                            ii = None
                            for k in range(KE):
                                ii = tensor.matmul(
                                    p_u[:],
                                    wuf[:, (F_ % 2) * KE * 128 + k * 128:
                                        (F_ % 2) * KE * 128 + (k + 1) * 128],
                                    h2d[:, ((c % 2) * KE + k) * 512:
                                        ((c % 2) * KE + k + 1) * 512],
                                    start=(k == 0), stop=(k == KE - 1))
                            s_d3mm.inc(ii)

                        def stats(c, m):
                            cm = c * KE + m
                            tensor.wait_ge(s_d3z.h, cm + 1)
                            tensor.wait_ge(s_d3zq.h, cm + 1)
                            if m == 0:
                                tensor.wait_ge(s_d3sf.h, c)
                            tensor.matmul(p_z[:], ones128[:],
                                          zsb[:, m * 512:(m + 1) * 512],
                                          start=(m == 0), stop=(m == KE - 1))
                            ii = tensor.matmul(p_z2[:], ones128[:],
                                               zqst[:, (cm % 2) * 512:(cm % 2 + 1) * 512],
                                               start=(m == 0), stop=(m == KE - 1))
                            s_d3st.inc(ii)

                        tensor.wait_ge(s_d3gf.h, NF * (c + 1))
                        for m in range(KE):
                            cm = c * KE + m
                            tensor.wait_ge(s_d3wd.h, 16 * (cm + 1))
                            if cm >= 2:
                                tensor.wait_ge(s_d3z.h, cm - 1)
                            ii = None
                            for f in range(NF):
                                ii = tensor.matmul(
                                    p_mlp[cm % 2][:],
                                    wdm[:, (cm % 2) * NF * 128 + f * 128:
                                        (cm % 2) * NF * 128 + (f + 1) * 128],
                                    gbuf[:, f * 512:(f + 1) * 512],
                                    start=(f == 0), stop=(f == NF - 1))
                            s_d3dn.inc(ii)
                            if m >= 1:
                                stats(c, m - 1)
                        stats(c, KE - 1)
                        for m in range(KE):
                            cm = c * KE + m
                            tensor.wait_ge(s_d3x2.h, cm + 1)
                            if cm >= 2:
                                tensor.wait_ge(s_d3oev.h, cm - 1)
                            ii = None
                            for t in range(4):
                                ii = tensor.transpose(
                                    p_tr[cm % 2][:, t * 128:(t + 1) * 128],
                                    x2st[:, (cm % 2) * 512 + t * 128:
                                         (cm % 2) * 512 + (t + 1) * 128],
                                    ident[:])
                            s_d3tr.inc(ii)

                @blk.scalar
                def _(scalar):
                    scalar.wait_ge(s_ph.h, PHC[0])
                    for c in range(NC_):
                        scalar.preload_act_table(AF.Silu)
                        for f in range(NF):
                            F_ = c * NF + f
                            scalar.wait_ge(s_d3mm.h, F_ + 1)
                            if F_ >= 2:
                                scalar.wait_ge(s_d3gf.h, F_ - 1)
                            ii = scalar.activation(
                                sgst[:, (F_ % 2) * 512:(F_ % 2 + 1) * 512],
                                p_g[:], AF.Silu)
                            s_d3sg.inc(ii)
                        for m in range(KE):
                            cm = c * KE + m
                            scalar.wait_ge(s_d3z.h, cm + 1)
                            if cm >= 2:
                                scalar.wait_ge(s_d3st.h, cm - 1)
                            ii = scalar.activation(
                                zqst[:, (cm % 2) * 512:(cm % 2 + 1) * 512],
                                zsb[:, m * 512:(m + 1) * 512], AF.Square)
                            s_d3zq.inc(ii)
                        scalar.wait_ge(s_d3st.h, 16 * (c + 1))
                        scalar.activation(mbz[:], p_z[:], AF.Identity, scale=1.0 / E)
                        ii = scalar.activation(uqz[:], p_z2[:], AF.Identity,
                                               scale=1.0 / E)
                        s_d3sf.inc(ii)
                        preload_rsqrt(scalar)
                        scalar.wait_ge(s_d3var.h, c + 1)
                        ii = rsqrt_act(scalar, rbz[:], varz[:], LN_EPS)
                        s_d3rb.inc(ii)
                        for m in range(KE):
                            cm = c * KE + m
                            scalar.wait_ge(s_d3tr.h, cm + 1)
                            if cm >= 2:
                                scalar.wait_ge(s_out[cm % 2].h, 16 * (cm // 2))
                            ii = scalar.activation(
                                ost[:, (cm % 2) * 512:(cm % 2 + 1) * 512],
                                p_tr[cm % 2][:], AF.Identity)
                            s_d3oev.inc(ii)
                            dst = out_t[_chunk(c), m * 128:(m + 1) * 128].rearrange(
                                "(t p) e -> p t e", p=128)
                            src = ost[:, (cm % 2) * 512:(cm % 2 + 1) * 512].rearrange(
                                "p (t e) -> p t e", t=4)
                            scalar.dma_start(dst, src).then_inc(s_out[cm % 2].h, 16)
                            s_out[cm % 2].n += 16

                @blk.vector
                def _(vector):
                    vector.wait_ge(s_ph.h, PHC[0])
                    for c in range(NC_):
                        for f in range(NF):
                            F_ = c * NF + f
                            vector.wait_ge(s_d3sg.h, F_ + 1)
                            if f == 0 and c >= 1:
                                vector.wait_ge(s_d3dn.h, KE * c)
                            ii = vector.tensor_mul(gbuf[:, f * 512:(f + 1) * 512],
                                                   sgst[:, (F_ % 2) * 512:(F_ % 2 + 1) * 512],
                                                   p_u[:])
                            s_d3gf.inc(ii)
                        vector.wait_ge(s_d3xld.h, (2 * KE + 1) * 16 * (c + 1))
                        for m in range(KE):
                            cm = c * KE + m
                            vector.wait_ge(s_d3dn.h, cm + 1)
                            vector.tensor_sub(s1st[:], hd3[:, m * 512:(m + 1) * 512],
                                              x1d3[:, m * 512:(m + 1) * 512])
                            vector.tensor_add(s2st[:], p_mlp[cm % 2][:], s1st[:])
                            vector.tensor_mul(s3st[:], msbuf[:], s2st[:])
                            ii = vector.scalar_tensor_tensor(
                                zsb[:, m * 512:(m + 1) * 512],
                                x1d3[:, m * 512:(m + 1) * 512], 2.0, s3st[:],
                                ALU.mult, ALU.add)
                            s_d3z.inc(ii)
                        vector.wait_ge(s_d3sf.h, c + 1)
                        vector.tensor_mul(tz[:], mbz[:], mbz[:])
                        ii = vector.tensor_sub(varz[:], uqz[:], tz[:])
                        s_d3var.inc(ii)
                        vector.wait_ge(s_d3rb.h, c + 1)
                        for m in range(KE):
                            cm = c * KE + m
                            if cm >= 2:
                                vector.wait_ge(s_d3tr.h, cm - 1)
                            vector.tensor_sub(tz[:], zsb[:, m * 512:(m + 1) * 512],
                                              mbz[:])
                            ii = vector.tensor_mul(
                                x2st[:, (cm % 2) * 512:(cm % 2 + 1) * 512],
                                tz[:], rbz[:])
                            s_d3x2.inc(ii)

            drain_block()

    return nc


def _host_consts():
    pos = np.arange(N, dtype=np.float64)
    inv = 1.0 / (10000.0 ** (np.arange(0, 128, 2, dtype=np.float64) / 128.0))
    ang = pos[:, None] * inv[None, :]
    c = np.cos(ang).astype(np.float32)
    s = np.sin(ang).astype(np.float32)
    c2 = np.ascontiguousarray(np.concatenate([c, c], axis=1).T)
    s2 = np.ascontiguousarray(np.concatenate([-s, s], axis=1).T)
    kk = np.arange(128)[:, None]
    qq = np.arange(512)[None, :]
    msk = np.concatenate([(qq >= kk + 128 * p) for p in range(4)],
                         axis=1).astype(ml_dtypes.bfloat16)
    ones = np.ones((128, 128), dtype=ml_dtypes.bfloat16)
    ident = np.eye(128, dtype=ml_dtypes.bfloat16)
    return c2, s2, np.ascontiguousarray(msk), ones, ident


def kernel(L0_attn_x, L0_attn_w_qkv, L0_attn_w_out, L0_attn_ln_scale, L0_attn_ln_bias,
           L0_ffn_w_router, L0_ffn_b_router, L0_ffn_norm1_w, L0_ffn_wv, L0_ffn_bv,
           L0_ffn_wo, L0_ffn_norm2_w, L0_ffn_w_gate, L0_ffn_w_up, L0_ffn_w_down):
    c2, s2, msk, ones, ident = _host_consts()
    A = np.ascontiguousarray
    shared = {
        "w_qkv": A(L0_attn_w_qkv, dtype=np.float32),
        "w_out": A(L0_attn_w_out, dtype=np.float32),
        "wv": A(L0_ffn_wv, dtype=np.float32),
        "wo": A(L0_ffn_wo, dtype=np.float32),
        "w_gate": A(L0_ffn_w_gate, dtype=np.float32),
        "w_up": A(L0_ffn_w_up, dtype=np.float32),
        "w_down": A(L0_ffn_w_down, dtype=np.float32),
        "gam": A(np.asarray(L0_attn_ln_scale, dtype=np.float32).reshape(128, 1)),
        "bet": A(np.asarray(L0_attn_ln_bias, dtype=np.float32).reshape(128, 1)),
        "w1t": A(np.asarray(L0_ffn_norm1_w, dtype=np.float32).reshape(KE, 128).T),
        "w2t": A(np.asarray(L0_ffn_norm2_w, dtype=np.float32).reshape(KE, 128).T),
        "bvt": A(np.asarray(L0_ffn_bv, dtype=np.float32).reshape(KE, 128).T),
        "wr_rep": A(np.broadcast_to(
            np.asarray(L0_ffn_w_router, dtype=np.float32)[:, None], (E, 128))),
        "br_t": A(np.broadcast_to(
            np.asarray(L0_ffn_b_router, dtype=np.float32), (128, 1))),
        "c2": c2, "s2": s2, "msk": msk, "ones128": ones, "ident": ident,
        "epsv": A(np.broadcast_to(np.array([LN_EPS, RMS_EPS], dtype=np.float32),
                                  (128, 2))),
    }
    xs = np.asarray(L0_attn_x, dtype=np.float32)
    nc = build_program(debug=False)
    in_maps = []
    for i in range(8):
        m = dict(shared)
        m["x"] = A(xs[i])
        in_maps.append(m)
    res = run_bass_kernel_spmd(nc, in_maps, core_ids=list(range(8)))
    return np.stack([r["out"] for r in res.results], axis=0)


# version 10
# speedup vs baseline: 1.0798x; 1.0798x over previous
import contextlib
import numpy as np
import ml_dtypes
import concourse.bass as bass
import concourse.mybir as mybir
from concourse.bass_utils import run_bass_kernel_spmd

F32 = mybir.dt.float32
BF16 = mybir.dt.bfloat16
AF = mybir.ActivationFunctionType
ALU = mybir.AluOpType

N = 4096
E = 2048
H = 16
FF = 5632
NC_ = 8
NTI = 32
KE = 16
NF = FF // 128
LN_EPS = 1e-5
RMS_EPS = 1e-6


SIM_SAFE = [False]
A1_CAP = [256]
NO_POOL = [False]


def rsqrt_act(scalar, out, in_, eps):
    if SIM_SAFE[0]:
        scalar.activation(out, in_, AF.Ln, bias=eps)
        return scalar.activation(out, out, AF.Exp, scale=-0.5)
    return scalar.activation(out, in_, AF.Abs_reciprocal_sqrt, bias=eps)


def preload_rsqrt(scalar):
    if SIM_SAFE[0]:
        return preload_set_with(scalar, [AF.Ln, AF.Exp])
    return scalar.preload_act_table(AF.Abs_reciprocal_sqrt)


def preload_set_with(scalar, funcs):
    from concourse.hw_specs import get_activation_tables
    for idx, fs in enumerate(get_activation_tables(scalar.bass.m.arch).values()):
        if all(f in fs for f in funcs):
            return scalar.add_instruction(
                mybir.InstLoadActFuncSet(
                    name=scalar.bass.get_next_instruction_name(),
                    act_func_set_id=idx, ins=[], outs=[]))
    raise ValueError("no table set with %s" % funcs)


class Sem:
    def __init__(self, es, nc, name):
        self.h = es.enter_context(nc.semaphore(name))
        self.n = 0

    def inc(self, instr, k=1):
        instr.then_inc(self.h, k)
        self.n += k


def _chunk(c):
    return slice(c * 512, (c + 1) * 512)


def build_program(debug=False, upto='D3'):
    nc = bass.Bass("TRN2", target_bir_lowering=False)

    x_in = nc.dram_tensor("x", [N, E], F32, kind="ExternalInput").ap()
    w_qkv = nc.dram_tensor("w_qkv", [E, 3 * E], F32, kind="ExternalInput").ap()
    w_out = nc.dram_tensor("w_out", [E, E], F32, kind="ExternalInput").ap()
    wv_in = nc.dram_tensor("wv", [E, E], F32, kind="ExternalInput").ap()
    wo_in = nc.dram_tensor("wo", [E, E], F32, kind="ExternalInput").ap()
    wg_in = nc.dram_tensor("w_gate", [E, FF], F32, kind="ExternalInput").ap()
    wu_in = nc.dram_tensor("w_up", [E, FF], F32, kind="ExternalInput").ap()
    wd_in = nc.dram_tensor("w_down", [FF, E], F32, kind="ExternalInput").ap()
    gam_in = nc.dram_tensor("gam", [128, 1], F32, kind="ExternalInput").ap()
    bet_in = nc.dram_tensor("bet", [128, 1], F32, kind="ExternalInput").ap()
    w1_in = nc.dram_tensor("w1t", [128, KE], F32, kind="ExternalInput").ap()
    w2_in = nc.dram_tensor("w2t", [128, KE], F32, kind="ExternalInput").ap()
    bv_in = nc.dram_tensor("bvt", [128, KE], F32, kind="ExternalInput").ap()
    wr_in = nc.dram_tensor("wr_rep", [E, 128], F32, kind="ExternalInput").ap()
    br_in = nc.dram_tensor("br_t", [128, 1], F32, kind="ExternalInput").ap()
    c2_in = nc.dram_tensor("c2", [128, N], F32, kind="ExternalInput").ap()
    s2_in = nc.dram_tensor("s2", [128, N], F32, kind="ExternalInput").ap()
    msk_in = nc.dram_tensor("msk", [128, 4 * 512], BF16, kind="ExternalInput").ap()
    ones_in = nc.dram_tensor("ones128", [128, 128], BF16, kind="ExternalInput").ap()
    eps_in = nc.dram_tensor("epsv", [128, 2], F32, kind="ExternalInput").ap()
    id_in = nc.dram_tensor("ident", [128, 128], BF16, kind="ExternalInput").ap()

    out_t = nc.dram_tensor("out", [N, E], F32, kind="ExternalOutput").ap()

    dk = "ExternalOutput" if debug else "Internal"
    xbf = nc.dram_tensor("xbf", [N, E], BF16, kind="Internal").ap()
    xlo = nc.dram_tensor("xlo", [N, E], BF16, kind="Internal").ap()
    qT = nc.dram_tensor("qT", [H, 128, N], BF16, kind=dk).ap()
    kTs = nc.dram_tensor("kTs", [H, 128, N], BF16, kind=dk).ap()
    vsc = nc.dram_tensor("vsc", [NTI, 128, E], BF16, kind=dk).ap()
    aT = nc.dram_tensor("aT", [H, 128, N], BF16, kind=dk).ap()
    x1sc = nc.dram_tensor("x1sc", [KE, 128, N], BF16, kind=dk).ap()
    msc = nc.dram_tensor("msc", [NC_, 128, 512], BF16, kind=dk).ap()
    vfsc = nc.dram_tensor("vfsc", [KE, 128, N], BF16, kind="Internal").ap()
    hsc = nc.dram_tensor("hsc", [KE, 128, N], BF16, kind=dk).ap()
    h2sc = nc.dram_tensor("h2sc", [KE, 128, N], BF16, kind=dk).ap()
    wvq_bf = nc.dram_tensor("wvq_bf", [E, E], BF16, kind="Internal").ap()
    wout_bf = nc.dram_tensor("wout_bf", [E, E], BF16, kind="Internal").ap()
    wvf_bf = nc.dram_tensor("wvf_bf", [E, E], BF16, kind="Internal").ap()
    wo_bf = nc.dram_tensor("wo_bf", [E, E], BF16, kind="Internal").ap()
    gate_bf = nc.dram_tensor("gate_bf", [NF, E, 128], BF16, kind="Internal").ap()
    up_bf = nc.dram_tensor("up_bf", [NF, E, 128], BF16, kind="Internal").ap()
    down_bf = nc.dram_tensor("down_bf", [KE, FF, 128], BF16, kind="Internal").ap()

    top = contextlib.ExitStack()
    with top:
        ones128 = top.enter_context(nc.sbuf_tensor("ones_sb", [128, 128], BF16))
        ident = top.enter_context(nc.sbuf_tensor("id_sb", [128, 128], BF16))
        gam = top.enter_context(nc.sbuf_tensor("gam_sb", [128, 1], F32))
        bet = top.enter_context(nc.sbuf_tensor("bet_sb", [128, 1], F32))
        w1sb = top.enter_context(nc.sbuf_tensor("w1_sb", [128, KE], F32))
        w2sb = top.enter_context(nc.sbuf_tensor("w2_sb", [128, KE], F32))
        bvsb = top.enter_context(nc.sbuf_tensor("bv_sb", [128, KE], F32))
        brsb = top.enter_context(nc.sbuf_tensor("br_sb", [128, 1], F32))
        epsb = top.enter_context(nc.sbuf_tensor("eps_sb", [128, 2], F32))
        barsb = top.enter_context(nc.sbuf_tensor("bar_sb", [128, 16], F32))
        nc.const_aps.register(F32, LN_EPS, epsb[:, 0:1])
        nc.const_aps.register(F32, RMS_EPS, epsb[:, 1:2])

        def S(name):
            return Sem(top, nc, name)

        s_cst = S("cst"); s_xt = S("xt")
        s_wq = [S("wq0"), S("wq1")]
        s_wcast = S("wcast"); s_ph = S("ph")
        s_a1mm = S("a1mm"); s_a1pf = S("a1pf"); s_a1r = S("a1r")
        s_a1st = S("a1st"); s_a1sf = S("a1sf"); s_a1var = S("a1var")
        s_a1rb = S("a1rb"); s_a1rl = S("a1rl"); s_a1qo = S("a1qo"); s_a1sw = S("a1sw")
        s_xbf = S("xbf")
        s_a1sp = [S("a1sp0"), S("a1sp1")]
        s_wv = S("wv"); s_a2mm = S("a2mm"); s_a2ev = S("a2ev")
        s_a2sp = [S("a2sp0"), S("a2sp1")]
        s_bld = [S("bld0"), S("bld1")]
        s_bmm1 = S("bmm1"); s_bsq1 = S("bsq1")
        s_bsq2 = S("bsq2"); s_bmm2 = S("bmm2"); s_bep = S("bep")
        s_bsp = [S("bsp0"), S("bsp1")]
        s_xlld = [S("xlld0"), S("xlld1")]
        s_xl = S("xl")
        s_xlsp = [S("xlsp0"), S("xlsp1")]
        s_cw = S("cw")
        s_cld = [S("cld0"), S("cld1")]
        s_cx0 = S("cx0")
        s_cmm = S("cmm"); s_cev = S("cev")
        s_csq = S("csq"); s_cst2 = S("cst2"); s_csf = S("csf"); s_cvar = S("cvar")
        s_crb = S("crb"); s_cx1 = S("cx1"); s_cx1sp = S("cx1sp"); s_crt = S("crt")
        s_cms = S("cms")
        s_cmsp = [S("cmsp0"), S("cmsp1")]
        s_d1w = S("d1w")
        s_d1ld = [S("d1ld0"), S("d1ld1")]
        s_d1xq = S("d1xq"); s_d1ss = S("d1ss")
        s_d1rb = S("d1rb"); s_d1h1 = S("d1h1"); s_d1mm = S("d1mm"); s_d1ev = S("d1ev")
        s_d1sp = [S("d1sp0"), S("d1sp1")]
        s_d2w = S("d2w")
        s_d2ld = [S("d2ld0"), S("d2ld1")]
        s_d2mm = S("d2mm"); s_d2h = S("d2h")
        s_d2sq = S("d2sq"); s_d2st = S("d2st"); s_d2rb = S("d2rb")
        s_d2h2 = S("d2h2")
        s_d2sp = [S("d2sp0"), S("d2sp1")]
        s_d2hsp = S("d2hsp")
        s_d3wf = S("d3wf"); s_d3wd = S("d3wd")
        s_d3h2ld = [S("d3h2ld0"), S("d3h2ld1")]
        s_d3xld = S("d3xld")
        s_d3mm = S("d3mm"); s_d3sg = S("d3sg"); s_d3gf = S("d3gf"); s_d3dn = S("d3dn")
        s_d3z = S("d3z"); s_d3zq = S("d3zq"); s_d3st = S("d3st"); s_d3sf = S("d3sf")
        s_d3var = S("d3var"); s_d3rb = S("d3rb"); s_d3x2 = S("d3x2"); s_d3tr = S("d3tr")
        s_d3oev = S("d3oev")
        s_out = [S("outd0"), S("outd1")]

        PHC = [0]
        CSTB = [0]
        _dma_sems_all = [s_cst, s_xt, s_wq[0], s_wq[1], s_wcast,
                         s_a1sp[0], s_a1sp[1], s_wv, s_a2sp[0], s_a2sp[1],
                         s_bld[0], s_bld[1], s_bsp[0], s_bsp[1],
                         s_xlld[0], s_xlld[1], s_xlsp[0], s_xlsp[1],
                         s_cw, s_cld[0], s_cld[1], s_cx0, s_cx1sp, s_cmsp[0], s_cmsp[1],
                         s_d1w, s_d1ld[0], s_d1ld[1], s_d1sp[0], s_d1sp[1],
                         s_d2w, s_d2ld[0], s_d2ld[1], s_d2sp[0], s_d2sp[1], s_d2hsp,
                         s_d3wf, s_d3wd, s_d3h2ld[0], s_d3h2ld[1], s_d3xld,
                         s_out[0], s_out[1]]

        def barrier_block(sync_waits=(), scalar_waits=()):
            tgt = PHC[0] + 5
            with nc.Block() as blk:
                @blk.sync
                def _(e):
                    for (sem, v) in sync_waits:
                        e.wait_ge(sem.h, v)
                    e.nop().then_inc(s_ph.h, 1); s_ph.n += 1
                    e.wait_ge(s_ph.h, tgt)

                @blk.gpsimd
                def _(e):
                    e.nop().then_inc(s_ph.h, 1); s_ph.n += 1
                    e.wait_ge(s_ph.h, tgt)

                @blk.tensor
                def _(e):
                    e.nop().then_inc(s_ph.h, 1); s_ph.n += 1
                    e.wait_ge(s_ph.h, tgt)

                @blk.vector
                def _(e):
                    e.wait_ge(s_cst.h, 144)
                    ii = e.tensor_copy(barsb[:, 0:1], epsb[:, 0:1])
                    ii.then_inc(s_ph.h, 1); s_ph.n += 1
                    e.wait_ge(s_ph.h, tgt)

                @blk.scalar
                def _(e):
                    for (sem, v) in scalar_waits:
                        e.wait_ge(sem.h, v)
                    e.wait_ge(s_cst.h, 144)
                    ii = e.copy(barsb[:, 1:2], epsb[:, 0:1])
                    ii.then_inc(s_ph.h, 1); s_ph.n += 1
                    e.wait_ge(s_ph.h, tgt)
            PHC[0] = s_ph.n
            assert PHC[0] == tgt

        def drain_block():
            with nc.Block() as blk:
                @blk.sync
                def _(sync):
                    for sem in _dma_sems_all:
                        if sem.n > 0:
                            sync.wait_ge(sem.h, sem.n)

        with nc.Block() as blk:
            @blk.sync
            def _(sync):
                for dst, src in [(ones128, ones_in), (ident, id_in),
                                 (gam, gam_in), (bet, bet_in), (w1sb, w1_in),
                                 (w2sb, w2_in), (bvsb, bv_in), (brsb, br_in),
                                 (epsb, eps_in)]:
                    sync.dma_start(dst[:], src).then_inc(s_cst.h, 16)
                    s_cst.n += 16

            @blk.gpsimd
            def _(gpsimd):
                for c in range(NC_):
                    gpsimd.dma_start(xbf[_chunk(c), :], x_in[_chunk(c), :]
                                     ).then_inc(s_xbf.h, 16)
                    s_xbf.n += 16



        CST0 = s_cst.n

        with contextlib.ExitStack() as pa:
            xT = pa.enter_context(nc.sbuf_tensor("xT", [128, KE * N], BF16))

            def xTs(k, c=None):
                if c is None:
                    return xT[:, k * N:(k + 1) * N]
                return xT[:, k * N + c * 512: k * N + (c + 1) * 512]

            with contextlib.ExitStack() as pa1:
                c2sb = pa1.enter_context(nc.sbuf_tensor("c2_sb", [128, N], F32))
                s2sb = pa1.enter_context(nc.sbuf_tensor("s2_sb", [128, N], F32))
                wq = pa1.enter_context(nc.sbuf_tensor("wq_sb", [128, 2 * KE * 128], BF16))
                swp = pa1.enter_context(nc.sbuf_tensor("swp", [128, 2 * 512], BF16))
                tc = pa1.enter_context(nc.sbuf_tensor("tc", [128, 512], F32))
                tsb = pa1.enter_context(nc.sbuf_tensor("tsb", [128, 512], F32))
                rbuf = pa1.enter_context(nc.sbuf_tensor("rbuf", [128, 3 * 512], F32))
                rbf = pa1.enter_context(nc.sbuf_tensor("rbf", [128, 2 * 512], BF16))
                rsqb = pa1.enter_context(nc.sbuf_tensor("rsqb", [128, 2 * 512], BF16))
                mbb = pa1.enter_context(nc.sbuf_tensor("mbb", [128, 2 * 512], BF16))
                uqb = pa1.enter_context(nc.sbuf_tensor("uqb", [128, 2 * 512], F32))
                varb = pa1.enter_context(nc.sbuf_tensor("varb", [128, 512], F32))
                rbb = pa1.enter_context(nc.sbuf_tensor("rbb", [128, 2 * 512], F32))
                tb2 = pa1.enter_context(nc.sbuf_tensor("tb2", [128, 512], BF16))
                tb2f = pa1.enter_context(nc.sbuf_tensor("tb2f", [128, 512], F32))
                rlb = pa1.enter_context(nc.sbuf_tensor("rlb", [128, 2 * 512], BF16))
                qob = pa1.enter_context(nc.sbuf_tensor("qob", [128, 2 * 512], BF16))
                p_p = [pa1.enter_context(nc.psum_tensor("pp%d" % i, [128, 512], F32))
                       for i in range(2)]
                p_sa = [pa1.enter_context(nc.psum_tensor("psa%d" % i, [128, 512], F32))
                        for i in range(2)]
                p_sb = [pa1.enter_context(nc.psum_tensor("psb%d" % i, [128, 512], F32))
                        for i in range(2)]

                def b2(buf, u):
                    return buf[:, (u % 2) * 512:(u % 2 + 1) * 512]

                def b3(buf, u):
                    return buf[:, (u % 3) * 512:(u % 3 + 1) * 512]

                with nc.Block() as blk:
                    @blk.sync
                    def _(sync):
                        sync.wait_ge(s_cst.h, CST0)
                        sync.dma_start(c2sb[:], c2_in).then_inc(s_cst.h, 16)
                        sync.dma_start(s2sb[:], s2_in).then_inc(s_cst.h, 16)
                        s_cst.n += 32
                        sync.wait_ge(s_xbf.h, 128)
                        for c in range(NC_):
                            for k in range(KE):
                                sync.dma_start(
                                    xTs(k, c),
                                    xbf[_chunk(c), k * 128:(k + 1) * 128],
                                    transpose=True).then_inc(s_xt.h, 16)
                                s_xt.n += 16
                        for u in range(256):
                            s, c = divmod(u, NC_)
                            t = qT if s < H else kTs
                            sync.wait_ge(s_a1qo.h, u + 1)
                            sync.dma_start(t[s % H][:, _chunk(c)], b2(qob, u)
                                           ).then_inc(s_a1sp[u % 2].h, 16)
                            s_a1sp[u % 2].n += 16

                    @blk.gpsimd
                    def _(gpsimd):
                        casts = []
                        for tgt, srcap in [(wvq_bf, w_qkv[:, 2 * E:3 * E]),
                                           (wout_bf, w_out),
                                           (wvf_bf, wv_in),
                                           (wo_bf, wo_in)]:
                            for i8 in range(8):
                                rs8 = slice(i8 * 256, (i8 + 1) * 256)
                                casts.append((tgt[rs8, :], srcap[rs8, :]))
                        for s in range(2 * H):
                            if s >= 2:
                                gpsimd.wait_ge(s_a1mm.h, 8 * (s - 1))
                            col0 = s * 128 if s < H else E + (s - H) * 128
                            src = w_qkv[:, col0:col0 + 128].rearrange(
                                "(k p) c -> p k c", p=128)
                            dst = wq[:, (s % 2) * KE * 128:(s % 2 + 1) * KE * 128
                                     ].rearrange("p (k c) -> p k c", k=KE)
                            gpsimd.dma_start(dst, src).then_inc(s_wq[s % 2].h, 16)
                            s_wq[s % 2].n += 16
                            gpsimd.dma_start(casts[s][0], casts[s][1]
                                             ).then_inc(s_a1sw.h, 16)
                            s_a1sw.n += 16
                        for f in range(NF):
                            gpsimd.dma_start(gate_bf[f], wg_in[:, f * 128:(f + 1) * 128]
                                             ).then_inc(s_wcast.h, 16)
                            gpsimd.dma_start(up_bf[f], wu_in[:, f * 128:(f + 1) * 128]
                                             ).then_inc(s_wcast.h, 16)
                            s_wcast.n += 32
                        for m in range(KE):
                            gpsimd.dma_start(down_bf[m], wd_in[:, m * 128:(m + 1) * 128]
                                             ).then_inc(s_wcast.h, 16)
                            s_wcast.n += 16

                    @blk.tensor
                    def _(tensor):
                        tensor.wait_ge(s_cst.h, CST0)
                        tensor.wait_ge(s_xt.h, 16 * KE * NC_)

                        def stats(v):
                            tensor.wait_ge(s_a1r.h, v + 1)
                            if v >= 2:
                                tensor.wait_ge(s_a1sf.h, v - 1)
                            tensor.matmul(p_sa[v % 2][:], ones128[:], b2(rbf, v),
                                          start=True, stop=True)
                            ii = tensor.matmul(p_sb[v % 2][:], ones128[:], b2(rsqb, v),
                                               start=True, stop=True)
                            s_a1st.inc(ii)

                        for s in range(2 * H):
                            for c in range(NC_):
                                u = s * NC_ + c
                                tensor.wait_ge(s_wq[s % 2].h, 16 * (s // 2 + 1))
                                if u >= 2:
                                    tensor.wait_ge(s_a1pf.h, u - 1)
                                wqs = wq[:, (s % 2) * KE * 128:(s % 2 + 1) * KE * 128]
                                ii = None
                                for k in range(KE):
                                    ii = tensor.matmul(
                                        p_p[u % 2][:],
                                        wqs[:, k * 128:(k + 1) * 128],
                                        xTs(k, c),
                                        start=(k == 0), stop=(k == KE - 1))
                                s_a1mm.inc(ii)
                                if u >= 1:
                                    stats(u - 1)
                        stats(255)

                    @blk.vector
                    def _(vector):
                        vector.wait_ge(s_cst.h, CST0 + 32)
                        for u in range(258):
                            if u < 256:
                                s, c = divmod(u, NC_)
                                cc = _chunk(c)
                                vector.wait_ge(s_a1mm.h, u + 1)
                                if u >= 2:
                                    vector.wait_ge(s_a1st.h, u - 1)
                                sw = slice((u % 2) * 512, (u % 2 + 1) * 512)
                                vector.tensor_copy(swp[0:64, sw],
                                                   p_p[u % 2][64:128, :])
                                vector.tensor_copy(swp[64:128, sw],
                                                   p_p[u % 2][0:64, :])
                                ii = vector.tensor_mul(tc[:], p_p[u % 2][:],
                                                       c2sb[:, cc])
                                s_a1pf.inc(ii)
                                vector.tensor_mul(tsb[:], swp[:, sw], s2sb[:, cc])
                                vector.tensor_add(b3(rbuf, u), tc[:], tsb[:])
                                vector.tensor_copy(b2(rbf, u), b3(rbuf, u))
                                ii = vector.tensor_mul(b2(rsqb, u), b2(rbf, u),
                                                       b2(rbf, u))
                                s_a1r.inc(ii)
                            if 0 <= u - 1 < 256:
                                v = u - 1
                                vector.wait_ge(s_a1sf.h, v + 1)
                                if v >= 1:
                                    vector.wait_ge(s_a1rb.h, v)
                                vector.tensor_mul(tb2[:], b2(mbb, v), b2(mbb, v))
                                ii = vector.tensor_sub(varb[:], b2(uqb, v), tb2[:])
                                s_a1var.inc(ii)
                            if 0 <= u - 2 < 256:
                                w = u - 2
                                vector.wait_ge(s_a1rb.h, w + 1)
                                if w >= 2:
                                    vector.wait_ge(s_a1qo.h, w - 1)
                                vector.tensor_sub(tb2f[:], b3(rbuf, w), b2(mbb, w))
                                ii = vector.tensor_mul(b2(rlb, w), tb2f[:],
                                                       b2(rbb, w))
                                s_a1rl.inc(ii)

                    @blk.scalar
                    def _(scalar):
                        scalar.wait_ge(s_cst.h, CST0)
                        preload_rsqrt(scalar)
                        for u in range(258):
                            if 0 <= u - 1 < 256:
                                v = u - 1
                                scalar.wait_ge(s_a1st.h, v + 1)
                                if v >= 2:
                                    scalar.wait_ge(s_a1rl.h, v - 1)
                                    scalar.wait_ge(s_a1var.h, v - 1)
                                scalar.activation(b2(mbb, v), p_sa[v % 2][:],
                                                  AF.Identity, scale=1.0 / 128)
                                ii = scalar.activation(b2(uqb, v), p_sb[v % 2][:],
                                                       AF.Identity, scale=1.0 / 128)
                                s_a1sf.inc(ii)
                                scalar.wait_ge(s_a1var.h, v + 1)
                                ii = rsqrt_act(scalar, b2(rbb, v), varb[:], LN_EPS)
                                s_a1rb.inc(ii)
                            if 0 <= u - 2 < 256:
                                w = u - 2
                                scalar.wait_ge(s_a1rl.h, w + 1)
                                if w >= 2:
                                    scalar.wait_ge(s_a1sp[w % 2].h, 16 * (w // 2))
                                ii = scalar.activation(b2(qob, w), b2(rlb, w),
                                                       AF.Identity,
                                                       scale=gam[:, 0:1],
                                                       bias=bet[:, 0:1])
                                s_a1qo.inc(ii)

                barrier_block(sync_waits=[(s_a1sp[0], s_a1sp[0].n), (s_a1sp[1], s_a1sp[1].n)])
                if upto == 'A1':
                    drain_block()
                    return nc

            with contextlib.ExitStack() as pa2:
                wvc = pa2.enter_context(nc.sbuf_tensor("wvc_sb", [128, KE * E], BF16))
                vst = pa2.enter_context(nc.sbuf_tensor("vst", [128, 2 * 512], BF16))
                p_v = [pa2.enter_context(nc.psum_tensor("pv%d" % i, [128, 512], F32))
                       for i in range(4)]

                with nc.Block() as blk:
                    @blk.gpsimd
                    def _(gpsimd):
                        gpsimd.wait_ge(s_ph.h, PHC[0])
                        gpsimd.wait_ge(s_a1sw.h, s_a1sw.n)
                        for k in range(KE):
                            gpsimd.dma_start(
                                wvc[:, k * E:(k + 1) * E],
                                wvq_bf[k * 128:(k + 1) * 128, :]
                            ).then_inc(s_wv.h, 16)
                            s_wv.n += 16

                    @blk.scalar
                    def _(scalar):
                        scalar.wait_ge(s_ph.h, PHC[0])
                        for i in range(NTI):
                            for vc in range(4):
                                u = 4 * i + vc
                                scalar.wait_ge(s_a2mm.h, u + 1)
                                if u >= 2:
                                    scalar.wait_ge(s_a2sp[u % 2].h, 16 * (u // 2))
                                ii = scalar.activation(
                                    vst[:, (u % 2) * 512:(u % 2 + 1) * 512],
                                    p_v[vc][:], AF.Identity)
                                s_a2ev.inc(ii)

                    @blk.tensor
                    def _(tensor):
                        tensor.wait_ge(s_ph.h, PHC[0])
                        tensor.wait_ge(s_wv.h, 256)
                        for i in range(NTI):
                            if i >= 1:
                                tensor.wait_ge(s_a2ev.h, 4 * i)
                            ii = None
                            for k in range(KE):
                                for vc in range(4):
                                    ii = tensor.matmul(
                                        p_v[vc][:],
                                        xT[:, k * N + i * 128: k * N + (i + 1) * 128],
                                        wvc[:, k * E + vc * 512: k * E + (vc + 1) * 512],
                                        start=(k == 0), stop=(k == KE - 1))
                                    if k == KE - 1:
                                        s_a2mm.inc(ii)

                    @blk.sync
                    def _(sync):
                        sync.wait_ge(s_ph.h, PHC[0])
                        for i in range(NTI):
                            for vc in range(4):
                                u = 4 * i + vc
                                sync.wait_ge(s_a2ev.h, u + 1)
                                sync.dma_start(
                                    vsc[i][:, vc * 512:(vc + 1) * 512],
                                    vst[:, (u % 2) * 512:(u % 2 + 1) * 512]
                                ).then_inc(s_a2sp[u % 2].h, 16)
                                s_a2sp[u % 2].n += 16

                barrier_block(sync_waits=[(s_a2sp[0], s_a2sp[0].n),
                                          (s_a2sp[1], s_a2sp[1].n)])
                if upto == 'A2':
                    drain_block()
                    return nc

        with contextlib.ExitStack() as pb:
            masks = pb.enter_context(nc.sbuf_tensor("msk_sb", [128, 4 * 512], BF16))
            qsb = pb.enter_context(nc.sbuf_tensor("qsb", [128, 2 * N], BF16))
            ksb = pb.enter_context(nc.sbuf_tensor("ksb", [128, 2 * N], BF16))
            vsb = pb.enter_context(nc.sbuf_tensor("vsb", [128, 2 * N], BF16))
            a2b = pb.enter_context(nc.sbuf_tensor("a2b", [128, 4 * 512], BF16))
            a4b = pb.enter_context(nc.sbuf_tensor("a4b", [128, 4 * 512], BF16))
            rbB = pb.enter_context(nc.sbuf_tensor("rbB", [128, 2 * 512], F32))
            astg = pb.enter_context(nc.sbuf_tensor("astg", [128, 2 * 512], BF16))
            xstg = pb.enter_context(nc.sbuf_tensor("xstg", [128, 2 * 512], F32))
            xbstg = pb.enter_context(nc.sbuf_tensor("xbstg", [128, 2 * 512], BF16))
            xlostg = pb.enter_context(nc.sbuf_tensor("xlostg", [128, 2 * 512], BF16))
            p_s = [pb.enter_context(nc.psum_tensor("ps%d" % i, [128, 512], F32))
                   for i in range(4)]
            p_o = [pb.enter_context(nc.psum_tensor("po%d" % i, [128, 512], F32))
                   for i in range(2)]
            p_d = [pb.enter_context(nc.psum_tensor("pd%d" % i, [128, 512], F32))
                   for i in range(2)]
            LAG = 3

            def xlt(t):
                r, cb = divmod(t, 4)
                return (slice(r * 128, (r + 1) * 128), slice(cb * 512, (cb + 1) * 512))

            units = []
            U = 0
            for h in range(H):
                for c in range(NC_):
                    C = h * NC_ + c
                    for j in range(4 * c + 4):
                        units.append((h, c, j, C, U, j - 4 * c))
                        U += 1
            NU = len(units)
            chunk_last = {}
            for (h, c, j, C, u, p) in units:
                chunk_last[C] = u

            bep_astg = {}
            _bep = 0
            for idx, pu in enumerate(units):
                if idx >= LAG:
                    pu2 = units[idx - LAG]
                    if pu2[4] == chunk_last[pu2[3]]:
                        _bep += 1
                        bep_astg[pu2[3]] = _bep
            for pu2 in units[-LAG:]:
                if pu2[4] == chunk_last[pu2[3]]:
                    _bep += 1
                    bep_astg[pu2[3]] = _bep

            with nc.Block() as blk:
                @blk.gpsimd
                def _(gpsimd):
                    gpsimd.wait_ge(s_ph.h, PHC[0])

                    def xloads(t):
                        rs, cs2 = xlt(t)
                        o = (t % 2) * 512
                        gpsimd.dma_start(xstg[:, o:o + 512], x_in[rs, cs2]
                                         ).then_inc(s_xlld[t % 2].h, 16)
                        gpsimd.dma_start(xbstg[:, o:o + 512], xbf[rs, cs2]
                                         ).then_inc(s_xlld[t % 2].h, 16)
                        s_xlld[t % 2].n += 32

                    xloads(0)
                    xloads(1)
                    for t in range(128):
                        rs, cs2 = xlt(t)
                        o = (t % 2) * 512
                        gpsimd.wait_ge(s_xlld[t % 2].h, 32 * (t // 2 + 1))
                        if t >= 2:
                            gpsimd.wait_ge(s_xlsp[t % 2].h, 16 * (t // 2))
                        ii = gpsimd.tensor_sub(xlostg[:, o:o + 512],
                                               xstg[:, o:o + 512],
                                               xbstg[:, o:o + 512])
                        s_xl.inc(ii)
                        gpsimd.wait_ge(s_xl.h, s_xl.n)
                        gpsimd.dma_start(xlo[rs, cs2], xlostg[:, o:o + 512]
                                         ).then_inc(s_xlsp[t % 2].h, 16)
                        s_xlsp[t % 2].n += 16
                        if t + 2 < 128:
                            xloads(t + 2)

                @blk.sync
                def _(sync):
                    sync.wait_ge(s_ph.h, PHC[0])
                    sync.dma_start(masks[:], msk_in).then_inc(s_cst.h, 16)
                    s_cst.n += 16
                    CSTB[0] = s_cst.n

                    def loads(h):
                        if h >= 2:
                            sync.wait_ge(s_bsp[0].h, 64 * (h - 1))
                            sync.wait_ge(s_bsp[1].h, 64 * (h - 1))
                        o = (h % 2) * N
                        sync.dma_start(qsb[:, o:o + N], qT[h]).then_inc(s_bld[h % 2].h, 16)
                        s_bld[h % 2].n += 16
                        sync.dma_start(ksb[:, o:o + N], kTs[h]).then_inc(s_bld[h % 2].h, 16)
                        s_bld[h % 2].n += 16
                        src = vsc[:, :, h * 128:(h + 1) * 128].rearrange("j p d -> p j d")
                        dst = vsb[:, o:o + N].rearrange("p (j d) -> p j d", j=NTI)
                        sync.dma_start(dst, src).then_inc(s_bld[h % 2].h, 16)
                        s_bld[h % 2].n += 16

                    loads(0)
                    loads(1)
                    for h in range(H):
                        for c in range(NC_):
                            C = h * NC_ + c
                            sync.wait_ge(s_bep.h, bep_astg[C])
                            sync.dma_start(aT[h][:, _chunk(c)],
                                           astg[:, (C % 2) * 512:(C % 2 + 1) * 512]
                                           ).then_inc(s_bsp[C % 2].h, 16)
                            s_bsp[C % 2].n += 16
                        if h + 2 < H:
                            loads(h + 2)

                @blk.tensor
                def _(tensor):
                    tensor.wait_ge(s_ph.h, PHC[0])

                    def mm2den(pu):
                        (h, c, j, C, u, p) = pu
                        o = (h % 2) * N
                        if j == 0 and C >= 2:
                            tensor.wait_ge(s_bep.h, bep_astg[C - 2])
                        tensor.wait_ge(s_bsq2.h, u + 1)
                        tensor.matmul(p_o[C % 2][:],
                                      vsb[:, o + j * 128: o + (j + 1) * 128],
                                      a4b[:, (u % 4) * 512:(u % 4 + 1) * 512],
                                      start=(j == 0), stop=(j == 4 * c + 3))
                        ii = tensor.matmul(p_d[C % 2][:], ones128[:],
                                           a4b[:, (u % 4) * 512:(u % 4 + 1) * 512],
                                           start=(j == 0), stop=(j == 4 * c + 3))
                        s_bmm2.inc(ii)

                    for idx, pu in enumerate(units):
                        (h, c, j, C, u, p) = pu
                        o = (h % 2) * N
                        if j == 0:
                            tensor.wait_ge(s_bld[h % 2].h, 48 * (h // 2 + 1))
                        if u >= 4:
                            tensor.wait_ge(s_bsq1.h, u - 3)
                        ii = tensor.matmul(p_s[u % 4][:],
                                           ksb[:, o + j * 128: o + (j + 1) * 128],
                                           qsb[:, o + c * 512: o + (c + 1) * 512],
                                           start=True, stop=True)
                        s_bmm1.inc(ii)
                        if idx >= LAG:
                            mm2den(units[idx - LAG])
                    for pu in units[-LAG:]:
                        mm2den(pu)

                @blk.scalar
                def _(scalar):
                    scalar.wait_ge(s_ph.h, PHC[0])
                    scalar.preload_act_table(AF.Square)
                    for pu in units:
                        (h, c, j, C, u, p) = pu
                        scalar.wait_ge(s_bmm1.h, u + 1)
                        if u >= 4:
                            scalar.wait_ge(s_bsq2.h, u - 3)
                        ii = scalar.activation(a2b[:, (u % 4) * 512:(u % 4 + 1) * 512],
                                               p_s[u % 4][:], AF.Square)
                        s_bsq1.inc(ii)

                @blk.vector
                def _(vector):
                    vector.wait_ge(s_ph.h, PHC[0])
                    vector.wait_ge(s_cst.h, CSTB[0])

                    def denom(pu):
                        (h, c, j, C, u, p) = pu
                        if u != chunk_last[C]:
                            return
                        vector.wait_ge(s_bmm2.h, u + 1)
                        if C >= 2:
                            vector.wait_ge(s_bsp[C % 2].h, 16 * (C // 2))
                        cs = slice((C % 2) * 512, (C % 2 + 1) * 512)
                        vector.reciprocal(rbB[:, cs], p_d[C % 2][:])
                        ii = vector.tensor_mul(astg[:, cs], p_o[C % 2][:],
                                               rbB[:, cs])
                        s_bep.inc(ii)

                    for idx, pu in enumerate(units):
                        (h, c, j, C, u, p) = pu
                        sl = slice((u % 4) * 512, (u % 4 + 1) * 512)
                        vector.wait_ge(s_bsq1.h, u + 1)
                        if u >= 4:
                            vector.wait_ge(s_bmm2.h, u - 3)
                        ii = vector.tensor_mul(a4b[:, sl], a2b[:, sl], a2b[:, sl])
                        if p >= 0:
                            ii = vector.tensor_mul(a4b[:, sl], a4b[:, sl],
                                                   masks[:, p * 512:(p + 1) * 512])
                        s_bsq2.inc(ii)
                        if idx >= LAG:
                            denom(units[idx - LAG])
                    for pu in units[-LAG:]:
                        denom(pu)

            barrier_block(sync_waits=[(s_bsp[0], s_bsp[0].n), (s_bsp[1], s_bsp[1].n),
                                      (s_xlsp[0], s_xlsp[0].n),
                                      (s_xlsp[1], s_xlsp[1].n)])
            if upto == 'B':
                drain_block()
                return nc

        with contextlib.ExitStack() as pc:
            wout = pc.enter_context(nc.sbuf_tensor("wout_sb", [128, KE * E], BF16))
            wrsb = pc.enter_context(nc.sbuf_tensor("wr_sb", [128, KE * 128], F32))
            asb = pc.enter_context(nc.sbuf_tensor("asb", [128, KE * 512], BF16))
            x0sb = pc.enter_context(nc.sbuf_tensor("x0sb", [128, KE * 512], BF16))
            x0lo = pc.enter_context(nc.sbuf_tensor("x0lo", [128, KE * 512], BF16))
            ysb = pc.enter_context(nc.sbuf_tensor("ysb", [128, KE * 512], F32))
            ybf = pc.enter_context(nc.sbuf_tensor("ybf", [128, 2 * 512], BF16))
            x1f = pc.enter_context(nc.sbuf_tensor("x1f", [128, KE * 512], F32))
            sqst = pc.enter_context(nc.sbuf_tensor("sqst", [128, 2 * 512], BF16))
            mbc = pc.enter_context(nc.sbuf_tensor("mbc", [128, 2 * 512], BF16))
            uqc = pc.enter_context(nc.sbuf_tensor("uqc", [128, 2 * 512], F32))
            varc = pc.enter_context(nc.sbuf_tensor("varc", [128, 2 * 512], F32))
            rbc = pc.enter_context(nc.sbuf_tensor("rbc", [128, 2 * 512], F32))
            tbc = pc.enter_context(nc.sbuf_tensor("tbc", [128, 512], F32))
            msbst = pc.enter_context(nc.sbuf_tensor("msbst", [128, 2 * 512], BF16))
            p_w = [pc.enter_context(nc.psum_tensor("pw%d" % i, [128, 512], F32))
                   for i in range(2)]
            p_s1 = pc.enter_context(nc.psum_tensor("ps1c", [128, 512], F32))
            p_s2 = pc.enter_context(nc.psum_tensor("ps2c", [128, 512], F32))
            p_r = pc.enter_context(nc.psum_tensor("prc", [128, 512], F32))

            def cbuf(buf, c, m):
                return buf[:, m * 512:(m + 1) * 512]

            with nc.Block() as blk:
                @blk.gpsimd
                def _(gpsimd):
                    gpsimd.wait_ge(s_ph.h, PHC[0])
                    for k in range(KE):
                        gpsimd.dma_start(wout[:, k * E:(k + 1) * E],
                                         wout_bf[k * 128:(k + 1) * 128, :]
                                         ).then_inc(s_cw.h, 16)
                        s_cw.n += 16
                        gpsimd.dma_start(wrsb[:, k * 128:(k + 1) * 128],
                                         wr_in[k * 128:(k + 1) * 128, :]
                                         ).then_inc(s_cw.h, 16)
                        s_cw.n += 16
                    for c in range(NC_):
                        for m in range(KE):
                            cm = c * KE + m
                            gpsimd.wait_ge(s_cx1.h, cm + 1)
                            gpsimd.dma_start(x1sc[m][:, _chunk(c)],
                                             x1f[:, m * 512:(m + 1) * 512]
                                             ).then_inc(s_cx1sp.h, 16)
                            s_cx1sp.n += 16

                @blk.sync
                def _(sync):
                    sync.wait_ge(s_ph.h, PHC[0])

                    def loads(c):
                        if c >= 1:
                            sync.wait_ge(s_cmm.h, 16 * c)
                        for m in range(KE):
                            sync.dma_start(cbuf(asb, c, m), aT[m][:, _chunk(c)]
                                           ).then_inc(s_cld[0].h, 16)
                            s_cld[0].n += 16

                    def x0loads(c):
                        if c >= 1:
                            sync.wait_ge(s_cev.h, 16 * c)
                        for m in range(KE):
                            sync.dma_start(
                                x0sb[:, m * 512:(m + 1) * 512],
                                xbf[_chunk(c), m * 128:(m + 1) * 128],
                                transpose=True).then_inc(s_cx0.h, 16)
                            sync.dma_start(
                                x0lo[:, m * 512:(m + 1) * 512],
                                xlo[_chunk(c), m * 128:(m + 1) * 128],
                                transpose=True).then_inc(s_cx0.h, 16)
                            s_cx0.n += 32

                    loads(0)
                    x0loads(0)
                    for c in range(NC_):
                        if c + 1 < NC_:
                            loads(c + 1)
                            x0loads(c + 1)
                        sync.wait_ge(s_cms.h, c + 1)
                        sync.dma_start(msc[c], msbst[:, (c % 2) * 512:(c % 2 + 1) * 512]
                                       ).then_inc(s_cmsp[c % 2].h, 16)
                        s_cmsp[c % 2].n += 16

                @blk.tensor
                def _(tensor):
                    tensor.wait_ge(s_ph.h, PHC[0])
                    tensor.wait_ge(s_cw.h, 512)

                    def stats(c, m):
                        cm = c * KE + m
                        tensor.wait_ge(s_cev.h, cm + 1)
                        tensor.wait_ge(s_csq.h, cm + 1)
                        if m == 0:
                            tensor.wait_ge(s_csf.h, c)
                        tensor.matmul(p_s1[:], ones128[:],
                                      ybf[:, (cm % 2) * 512:(cm % 2 + 1) * 512],
                                      start=(m == 0), stop=(m == KE - 1))
                        ii = tensor.matmul(p_s2[:], ones128[:],
                                           sqst[:, (cm % 2) * 512:(cm % 2 + 1) * 512],
                                           start=(m == 0), stop=(m == KE - 1))
                        s_cst2.inc(ii)

                    for c in range(NC_):
                        tensor.wait_ge(s_cld[0].h, 256 * (c + 1))
                        for m in range(KE):
                            cm = c * KE + m
                            if cm >= 2:
                                tensor.wait_ge(s_cev.h, cm - 1)
                            ii = None
                            for k in range(KE):
                                ii = tensor.matmul(
                                    p_w[cm % 2][:],
                                    wout[:, k * E + m * 128: k * E + (m + 1) * 128],
                                    cbuf(asb, c, k),
                                    start=(k == 0), stop=(k == KE - 1))
                            s_cmm.inc(ii)
                            if m >= 1:
                                stats(c, m - 1)
                        stats(c, KE - 1)
                        tensor.wait_ge(s_cx1.h, KE * (c + 1))
                        if c >= 1:
                            tensor.wait_ge(s_cms.h, c)
                        ii = None
                        for k in range(KE):
                            ii = tensor.matmul(p_r[:],
                                               wrsb[:, k * 128:(k + 1) * 128],
                                               x1f[:, k * 512:(k + 1) * 512],
                                               start=(k == 0), stop=(k == KE - 1))
                        s_crt.inc(ii)

                @blk.vector
                def _(vector):
                    vector.wait_ge(s_ph.h, PHC[0])
                    for c in range(NC_):
                        vector.wait_ge(s_cx0.h, 512 * (c + 1))
                        for m in range(KE):
                            cm = c * KE + m
                            vector.wait_ge(s_cmm.h, cm + 1)
                            if cm >= 2:
                                vector.wait_ge(s_csq.h, cm - 1)
                                vector.wait_ge(s_cst2.h, cm - 1)
                            vector.tensor_add(ysb[:, m * 512:(m + 1) * 512],
                                              p_w[cm % 2][:],
                                              x0sb[:, m * 512:(m + 1) * 512])
                            vector.tensor_add(ysb[:, m * 512:(m + 1) * 512],
                                              ysb[:, m * 512:(m + 1) * 512],
                                              x0lo[:, m * 512:(m + 1) * 512])
                            ii = vector.tensor_copy(ybf[:, (cm % 2) * 512:(cm % 2 + 1) * 512],
                                                    ysb[:, m * 512:(m + 1) * 512])
                            s_cev.inc(ii)
                        vector.wait_ge(s_csf.h, c + 1)
                        if c >= 2:
                            vector.wait_ge(s_crb.h, c - 1)
                        cs = slice((c % 2) * 512, (c % 2 + 1) * 512)
                        vector.tensor_mul(tbc[:], mbc[:, cs], mbc[:, cs])
                        ii = vector.tensor_sub(varc[:, cs], uqc[:, cs], tbc[:])
                        s_cvar.inc(ii)
                        vector.wait_ge(s_crb.h, c + 1)
                        if c >= 1:
                            vector.wait_ge(s_crt.h, c)
                            vector.wait_ge(s_cx1sp.h, 256 * c)
                        for m in range(KE):
                            cm = c * KE + m
                            vector.tensor_sub(tbc[:], ysb[:, m * 512:(m + 1) * 512],
                                              mbc[:, cs])
                            ii = vector.tensor_mul(x1f[:, m * 512:(m + 1) * 512],
                                                   tbc[:], rbc[:, cs])
                            s_cx1.inc(ii)
                        vector.wait_ge(s_crt.h, c + 1)
                        if c >= 2:
                            vector.wait_ge(s_cmsp[c % 2].h, 16 * (c // 2))
                        ii = vector.tensor_scalar(msbst[:, cs], p_r[:],
                                                  brsb[:, 0:1], 0.0,
                                                  ALU.add, ALU.is_gt)
                        s_cms.inc(ii)

                @blk.scalar
                def _(scalar):
                    scalar.wait_ge(s_ph.h, PHC[0])
                    preload_rsqrt(scalar)
                    for c in range(NC_):
                        cs = slice((c % 2) * 512, (c % 2 + 1) * 512)
                        for m in range(KE):
                            cm = c * KE + m
                            scalar.wait_ge(s_cev.h, cm + 1)
                            if cm >= 2:
                                scalar.wait_ge(s_cst2.h, cm - 1)
                            ii = scalar.activation(
                                sqst[:, (cm % 2) * 512:(cm % 2 + 1) * 512],
                                ybf[:, (cm % 2) * 512:(cm % 2 + 1) * 512], AF.Square)
                            s_csq.inc(ii)
                        scalar.wait_ge(s_cst2.h, 16 * (c + 1))
                        if c >= 2:
                            scalar.wait_ge(s_cvar.h, c - 1)
                            scalar.wait_ge(s_cx1.h, 16 * (c - 1))
                        scalar.activation(mbc[:, cs], p_s1[:], AF.Identity,
                                          scale=1.0 / E)
                        ii = scalar.activation(uqc[:, cs], p_s2[:], AF.Identity,
                                               scale=1.0 / E)
                        s_csf.inc(ii)
                        scalar.wait_ge(s_cvar.h, c + 1)
                        ii = rsqrt_act(scalar, rbc[:, cs], varc[:, cs], LN_EPS)
                        s_crb.inc(ii)

            barrier_block(sync_waits=[(s_cx1sp, s_cx1sp.n), (s_cmsp[0], s_cmsp[0].n), (s_cmsp[1], s_cmsp[1].n)])
            if upto == 'C':
                drain_block()
                return nc

        with contextlib.ExitStack() as pd1:
            wvsb = pd1.enter_context(nc.sbuf_tensor("wv_sb", [128, KE * E], BF16))
            x1d = pd1.enter_context(nc.sbuf_tensor("x1d", [128, 2 * KE * 512], BF16))
            xqb = pd1.enter_context(nc.sbuf_tensor("xqb", [128, 2 * 512], BF16))
            h1sb = pd1.enter_context(nc.sbuf_tensor("h1sb", [128, 2 * KE * 512], BF16))
            ust = pd1.enter_context(nc.sbuf_tensor("ustd1", [128, 512], F32))
            rb1 = pd1.enter_context(nc.sbuf_tensor("rb1", [128, 2 * 512], BF16))
            vstd = pd1.enter_context(nc.sbuf_tensor("vstd", [128, 2 * 512], BF16))
            td1 = pd1.enter_context(nc.sbuf_tensor("td1", [128, 512], BF16))
            p_ss = pd1.enter_context(nc.psum_tensor("pssd1", [128, 512], F32))
            p_v1 = [pd1.enter_context(nc.psum_tensor("pv1%d" % i, [128, 512], F32))
                    for i in range(2)]

            def dbuf(buf, c, m):
                o = ((c % 2) * KE + m) * 512
                return buf[:, o:o + 512]

            with nc.Block() as blk:
                @blk.gpsimd
                def _(gpsimd):
                    gpsimd.wait_ge(s_ph.h, PHC[0])
                    for k in range(KE):
                        gpsimd.dma_start(wvsb[:, k * E:(k + 1) * E],
                                         wvf_bf[k * 128:(k + 1) * 128, :]
                                         ).then_inc(s_d1w.h, 16)
                        s_d1w.n += 16

                @blk.sync
                def _(sync):
                    sync.wait_ge(s_ph.h, PHC[0])
                    for c in range(NC_):
                        if c >= 2:
                            sync.wait_ge(s_d1h1.h, 16 * (c - 1))
                        for m in range(KE):
                            sync.dma_start(dbuf(x1d, c, m), x1sc[m][:, _chunk(c)]
                                           ).then_inc(s_d1ld[c % 2].h, 16)
                            s_d1ld[c % 2].n += 16

                @blk.vector
                def _(vector):
                    vector.wait_ge(s_ph.h, PHC[0])
                    for c in range(NC_):
                        vector.wait_ge(s_d1ld[c % 2].h, 256 * (c // 2 + 1))
                        for k in range(KE):
                            ck = c * KE + k
                            if ck >= 2:
                                vector.wait_ge(s_d1ss.h, ck - 1)
                            ii = vector.tensor_mul(
                                xqb[:, (ck % 2) * 512:(ck % 2 + 1) * 512],
                                dbuf(x1d, c, k), dbuf(x1d, c, k))
                            s_d1xq.inc(ii)
                        vector.wait_ge(s_d1rb.h, c + 1)
                        if c >= 2:
                            vector.wait_ge(s_d1mm.h, 16 * (c - 1))
                        for k in range(KE):
                            vector.tensor_mul(td1[:], dbuf(x1d, c, k),
                                              rb1[:, (c % 2) * 512:(c % 2 + 1) * 512])
                            ii = vector.tensor_scalar(dbuf(h1sb, c, k), td1[:],
                                                      w1sb[:, k:k + 1], None, ALU.mult)
                            s_d1h1.inc(ii)

                @blk.tensor
                def _(tensor):
                    tensor.wait_ge(s_ph.h, PHC[0])
                    tensor.wait_ge(s_d1w.h, 256)
                    for c in range(NC_):
                        for k in range(KE):
                            ck = c * KE + k
                            tensor.wait_ge(s_d1xq.h, ck + 1)
                            if c >= 1 and k == 0:
                                tensor.wait_ge(s_d1rb.h, c)
                            ii = tensor.matmul(
                                p_ss[:], ones128[:],
                                xqb[:, (ck % 2) * 512:(ck % 2 + 1) * 512],
                                start=(k == 0), stop=(k == KE - 1))
                            s_d1ss.inc(ii)
                        tensor.wait_ge(s_d1h1.h, 16 * (c + 1))
                        for m in range(KE):
                            cm = c * KE + m
                            if cm >= 2:
                                tensor.wait_ge(s_d1ev.h, cm - 1)
                            ii = None
                            for k in range(KE):
                                ii = tensor.matmul(
                                    p_v1[cm % 2][:],
                                    wvsb[:, k * E + m * 128: k * E + (m + 1) * 128],
                                    dbuf(h1sb, c, k),
                                    start=(k == 0), stop=(k == KE - 1))
                            s_d1mm.inc(ii)

                @blk.scalar
                def _(scalar):
                    scalar.wait_ge(s_ph.h, PHC[0])
                    preload_rsqrt(scalar)
                    for c in range(NC_):
                        scalar.wait_ge(s_d1ss.h, 16 * (c + 1))
                        scalar.activation(ust[:], p_ss[:], AF.Identity, scale=1.0 / E)
                        ii = rsqrt_act(scalar,
                                       rb1[:, (c % 2) * 512:(c % 2 + 1) * 512],
                                       ust[:], RMS_EPS)
                        s_d1rb.inc(ii)
                        for m in range(KE):
                            cm = c * KE + m
                            scalar.wait_ge(s_d1mm.h, cm + 1)
                            if cm >= 2:
                                scalar.wait_ge(s_d1sp[cm % 2].h, 16 * (cm // 2))
                            ii = scalar.activation(
                                vstd[:, (cm % 2) * 512:(cm % 2 + 1) * 512],
                                p_v1[cm % 2][:], AF.Identity, bias=bvsb[:, m:m + 1])
                            s_d1ev.inc(ii)
                            scalar.dma_start(vfsc[m][:, _chunk(c)],
                                             vstd[:, (cm % 2) * 512:(cm % 2 + 1) * 512]
                                             ).then_inc(s_d1sp[cm % 2].h, 16)
                            s_d1sp[cm % 2].n += 16

            barrier_block(scalar_waits=[(s_d1sp[0], s_d1sp[0].n), (s_d1sp[1], s_d1sp[1].n)])
            if upto == 'D1':
                drain_block()
                return nc

        with contextlib.ExitStack() as pd2:
            wosb = pd2.enter_context(nc.sbuf_tensor("wo_sb", [128, KE * E], BF16))
            x1d2 = pd2.enter_context(nc.sbuf_tensor("x1d2", [128, 2 * KE * 512], BF16))
            vd = pd2.enter_context(nc.sbuf_tensor("vd", [128, 2 * KE * 512], BF16))
            hsb = pd2.enter_context(nc.sbuf_tensor("hsb2", [128, KE * 512], BF16))
            hsqst = pd2.enter_context(nc.sbuf_tensor("hsqst", [128, 2 * 512], BF16))
            ust2 = pd2.enter_context(nc.sbuf_tensor("ustd2", [128, 512], F32))
            rb2 = pd2.enter_context(nc.sbuf_tensor("rb2", [128, 2 * 512], BF16))
            h2st = pd2.enter_context(nc.sbuf_tensor("h2st", [128, 2 * 512], BF16))
            td2 = pd2.enter_context(nc.sbuf_tensor("td2", [128, 512], BF16))
            p_w2 = [pd2.enter_context(nc.psum_tensor("pw2%d" % i, [128, 512], F32))
                    for i in range(2)]
            p_ss2 = pd2.enter_context(nc.psum_tensor("pss2", [128, 512], F32))

            def dbuf2(buf, c, m):
                o = ((c % 2) * KE + m) * 512
                return buf[:, o:o + 512]

            with nc.Block() as blk:
                @blk.gpsimd
                def _(gpsimd):
                    gpsimd.wait_ge(s_ph.h, PHC[0])
                    for k in range(KE):
                        gpsimd.dma_start(wosb[:, k * E:(k + 1) * E],
                                         wo_bf[k * 128:(k + 1) * 128, :]
                                         ).then_inc(s_d2w.h, 16)
                        s_d2w.n += 16

                @blk.sync
                def _(sync):
                    sync.wait_ge(s_ph.h, PHC[0])

                    def loads(c):
                        if c >= 2:
                            sync.wait_ge(s_d2mm.h, 16 * (c - 1))
                            sync.wait_ge(s_d2h.h, 16 * (c - 1))
                        for m in range(KE):
                            sync.dma_start(dbuf2(x1d2, c, m), x1sc[m][:, _chunk(c)]
                                           ).then_inc(s_d2ld[c % 2].h, 16)
                            s_d2ld[c % 2].n += 16
                            sync.dma_start(dbuf2(vd, c, m), vfsc[m][:, _chunk(c)]
                                           ).then_inc(s_d2ld[c % 2].h, 16)
                            s_d2ld[c % 2].n += 16

                    loads(0)
                    loads(1)
                    for c in range(NC_):
                        for m in range(KE):
                            cm = c * KE + m
                            sync.wait_ge(s_d2h.h, cm + 1)
                            sync.dma_start(hsc[m][:, _chunk(c)],
                                           hsb[:, m * 512:(m + 1) * 512]
                                           ).then_inc(s_d2hsp.h, 16)
                            s_d2hsp.n += 16
                        if c + 2 < NC_:
                            loads(c + 2)

                @blk.tensor
                def _(tensor):
                    tensor.wait_ge(s_ph.h, PHC[0])
                    tensor.wait_ge(s_d2w.h, 256)

                    def stats(c, m):
                        cm = c * KE + m
                        tensor.wait_ge(s_d2sq.h, cm + 1)
                        if m == 0:
                            tensor.wait_ge(s_d2rb.h, c)
                        ii = tensor.matmul(p_ss2[:], ones128[:],
                                           hsqst[:, (cm % 2) * 512:(cm % 2 + 1) * 512],
                                           start=(m == 0), stop=(m == KE - 1))
                        s_d2st.inc(ii)

                    for c in range(NC_):
                        tensor.wait_ge(s_d2ld[c % 2].h, 512 * (c // 2 + 1))
                        for m in range(KE):
                            cm = c * KE + m
                            if cm >= 2:
                                tensor.wait_ge(s_d2h.h, cm - 1)
                            ii = None
                            for k in range(KE):
                                ii = tensor.matmul(
                                    p_w2[cm % 2][:],
                                    wosb[:, k * E + m * 128: k * E + (m + 1) * 128],
                                    dbuf2(vd, c, k),
                                    start=(k == 0), stop=(k == KE - 1))
                            s_d2mm.inc(ii)
                            if m >= 1:
                                stats(c, m - 1)
                        stats(c, KE - 1)

                @blk.vector
                def _(vector):
                    vector.wait_ge(s_ph.h, PHC[0])
                    for c in range(NC_):
                        for m in range(KE):
                            cm = c * KE + m
                            vector.wait_ge(s_d2mm.h, cm + 1)
                            if m == 0:
                                vector.wait_ge(s_d2sq.h, 16 * c)
                                vector.wait_ge(s_d2hsp.h, 256 * c)
                            ii = vector.tensor_add(hsb[:, m * 512:(m + 1) * 512],
                                                   p_w2[cm % 2][:], dbuf2(x1d2, c, m))
                            s_d2h.inc(ii)
                        vector.wait_ge(s_d2rb.h, c + 1)
                        for m in range(KE):
                            cm = c * KE + m
                            if cm >= 2:
                                vector.wait_ge(s_d2sp[cm % 2].h, 16 * (cm // 2))
                            vector.tensor_mul(td2[:], hsb[:, m * 512:(m + 1) * 512],
                                              rb2[:, (c % 2) * 512:(c % 2 + 1) * 512])
                            ii = vector.tensor_scalar(
                                h2st[:, (cm % 2) * 512:(cm % 2 + 1) * 512],
                                td2[:], w2sb[:, m:m + 1], None, ALU.mult)
                            s_d2h2.inc(ii)

                @blk.scalar
                def _(scalar):
                    scalar.wait_ge(s_ph.h, PHC[0])
                    preload_rsqrt(scalar)
                    for c in range(NC_):
                        for m in range(KE):
                            cm = c * KE + m
                            scalar.wait_ge(s_d2h.h, cm + 1)
                            if cm >= 2:
                                scalar.wait_ge(s_d2st.h, cm - 1)
                            ii = scalar.activation(
                                hsqst[:, (cm % 2) * 512:(cm % 2 + 1) * 512],
                                hsb[:, m * 512:(m + 1) * 512], AF.Square)
                            s_d2sq.inc(ii)
                        scalar.wait_ge(s_d2st.h, 16 * (c + 1))
                        if c >= 2:
                            scalar.wait_ge(s_d2h2.h, 16 * (c - 1))
                        scalar.activation(ust2[:], p_ss2[:], AF.Identity, scale=1.0 / E)
                        ii = rsqrt_act(scalar,
                                       rb2[:, (c % 2) * 512:(c % 2 + 1) * 512],
                                       ust2[:], RMS_EPS)
                        s_d2rb.inc(ii)
                        for m in range(KE):
                            cm = c * KE + m
                            scalar.wait_ge(s_d2h2.h, cm + 1)
                            scalar.dma_start(h2sc[m][:, _chunk(c)],
                                             h2st[:, (cm % 2) * 512:(cm % 2 + 1) * 512]
                                             ).then_inc(s_d2sp[cm % 2].h, 16)
                            s_d2sp[cm % 2].n += 16

            barrier_block(sync_waits=[(s_d2hsp, s_d2hsp.n)],
                          scalar_waits=[(s_d2sp[0], s_d2sp[0].n),
                                        (s_d2sp[1], s_d2sp[1].n)])
            if upto == 'D2':
                drain_block()
                return nc

        with contextlib.ExitStack() as pd3:
            x1d3 = pd3.enter_context(nc.sbuf_tensor("x1d3", [128, KE * 512], BF16))
            hd3 = pd3.enter_context(nc.sbuf_tensor("hd3", [128, KE * 512], BF16))
            h2d = pd3.enter_context(nc.sbuf_tensor("h2d", [128, 2 * KE * 512], BF16))
            msbuf = pd3.enter_context(nc.sbuf_tensor("msbuf", [128, 512], BF16))
            wgf = pd3.enter_context(nc.sbuf_tensor("wgf", [128, 2 * KE * 128], BF16))
            wuf = pd3.enter_context(nc.sbuf_tensor("wuf", [128, 2 * KE * 128], BF16))
            wdm = pd3.enter_context(nc.sbuf_tensor("wdm", [128, 2 * NF * 128], BF16))
            gbuf = pd3.enter_context(nc.sbuf_tensor("gbuf", [128, NF * 512], BF16))
            sgst = pd3.enter_context(nc.sbuf_tensor("sgst", [128, 2 * 512], BF16))
            s1st = pd3.enter_context(nc.sbuf_tensor("s1st", [128, 512], BF16))
            s2st = pd3.enter_context(nc.sbuf_tensor("s2st", [128, 512], F32))
            s3st = pd3.enter_context(nc.sbuf_tensor("s3st", [128, 512], BF16))
            zsb = pd3.enter_context(nc.sbuf_tensor("zsb", [128, KE * 512], BF16))
            zqst = pd3.enter_context(nc.sbuf_tensor("zqst", [128, 2 * 512], BF16))
            mbz = pd3.enter_context(nc.sbuf_tensor("mbz", [128, 512], BF16))
            uqz = pd3.enter_context(nc.sbuf_tensor("uqz", [128, 512], BF16))
            varz = pd3.enter_context(nc.sbuf_tensor("varz", [128, 512], BF16))
            rbz = pd3.enter_context(nc.sbuf_tensor("rbz", [128, 512], BF16))
            tz = pd3.enter_context(nc.sbuf_tensor("tz", [128, 512], BF16))
            x2st = pd3.enter_context(nc.sbuf_tensor("x2st", [128, 2 * 512], BF16))
            ost = pd3.enter_context(nc.sbuf_tensor("ost", [128, 2 * 512], F32))
            p_g = pd3.enter_context(nc.psum_tensor("pg", [128, 512], F32))
            p_u = pd3.enter_context(nc.psum_tensor("pu", [128, 512], F32))
            p_mlp = [pd3.enter_context(nc.psum_tensor("pmlp%d" % i, [128, 512], F32))
                     for i in range(2)]
            p_z = pd3.enter_context(nc.psum_tensor("pz", [128, 512], F32))
            p_z2 = pd3.enter_context(nc.psum_tensor("pz2", [128, 512], F32))
            p_tr = [pd3.enter_context(nc.psum_tensor("ptr%d" % i, [128, 512], BF16))
                    for i in range(2)]

            with nc.Block() as blk:
                @blk.sync
                def _(sync):
                    sync.wait_ge(s_ph.h, PHC[0])
                    sync.wait_ge(s_wcast.h, s_wcast.n)

                    def wloads(c):
                        if c >= 2:
                            sync.wait_ge(s_d3mm.h, NF * (c - 1))
                        for m in range(KE):
                            sync.dma_start(
                                h2d[:, ((c % 2) * KE + m) * 512:((c % 2) * KE + m + 1) * 512],
                                h2sc[m][:, _chunk(c)]).then_inc(s_d3h2ld[c % 2].h, 16)
                            s_d3h2ld[c % 2].n += 16

                    def xloads(c):
                        if c >= 1:
                            sync.wait_ge(s_d3z.h, KE * c)
                        for m in range(KE):
                            sync.dma_start(x1d3[:, m * 512:(m + 1) * 512],
                                           x1sc[m][:, _chunk(c)]).then_inc(s_d3xld.h, 16)
                            s_d3xld.n += 16
                            sync.dma_start(hd3[:, m * 512:(m + 1) * 512],
                                           hsc[m][:, _chunk(c)]).then_inc(s_d3xld.h, 16)
                            s_d3xld.n += 16
                        sync.dma_start(msbuf[:], msc[c]).then_inc(s_d3xld.h, 16)
                        s_d3xld.n += 16

                    wloads(0)
                    xloads(0)
                    wloads(1)
                    for c in range(NC_):
                        for f in range(NF):
                            F_ = c * NF + f
                            if F_ >= 2:
                                sync.wait_ge(s_d3mm.h, F_ - 1)
                            o = (F_ % 2) * KE * 128
                            sync.dma_start(
                                wgf[:, o:o + KE * 128].rearrange("p (k c) -> p k c", k=KE),
                                gate_bf[f].rearrange("(k p) c -> p k c", p=128)
                            ).then_inc(s_d3wf.h, 16)
                            sync.dma_start(
                                wuf[:, o:o + KE * 128].rearrange("p (k c) -> p k c", k=KE),
                                up_bf[f].rearrange("(k p) c -> p k c", p=128)
                            ).then_inc(s_d3wf.h, 16)
                            s_d3wf.n += 32
                        for m in range(KE):
                            cm = c * KE + m
                            if cm >= 2:
                                sync.wait_ge(s_d3dn.h, cm - 1)
                            o = (cm % 2) * NF * 128
                            sync.dma_start(
                                wdm[:, o:o + NF * 128].rearrange("p (f c) -> p f c", f=NF),
                                down_bf[m].rearrange("(f p) c -> p f c", p=128)
                            ).then_inc(s_d3wd.h, 16)
                            s_d3wd.n += 16
                        if c + 1 < NC_:
                            xloads(c + 1)
                        if c + 2 < NC_:
                            wloads(c + 2)

                @blk.tensor
                def _(tensor):
                    tensor.wait_ge(s_ph.h, PHC[0])
                    for c in range(NC_):
                        tensor.wait_ge(s_d3h2ld[c % 2].h, 256 * (c // 2 + 1))
                        for f in range(NF):
                            F_ = c * NF + f
                            tensor.wait_ge(s_d3wf.h, 32 * (F_ + 1))
                            if F_ >= 1:
                                tensor.wait_ge(s_d3sg.h, F_)
                                tensor.wait_ge(s_d3gf.h, F_)
                            for k in range(KE):
                                tensor.matmul(
                                    p_g[:],
                                    wgf[:, (F_ % 2) * KE * 128 + k * 128:
                                        (F_ % 2) * KE * 128 + (k + 1) * 128],
                                    h2d[:, ((c % 2) * KE + k) * 512:
                                        ((c % 2) * KE + k + 1) * 512],
                                    start=(k == 0), stop=(k == KE - 1))
                            ii = None
                            for k in range(KE):
                                ii = tensor.matmul(
                                    p_u[:],
                                    wuf[:, (F_ % 2) * KE * 128 + k * 128:
                                        (F_ % 2) * KE * 128 + (k + 1) * 128],
                                    h2d[:, ((c % 2) * KE + k) * 512:
                                        ((c % 2) * KE + k + 1) * 512],
                                    start=(k == 0), stop=(k == KE - 1))
                            s_d3mm.inc(ii)

                        def stats(c, m):
                            cm = c * KE + m
                            tensor.wait_ge(s_d3z.h, cm + 1)
                            tensor.wait_ge(s_d3zq.h, cm + 1)
                            if m == 0:
                                tensor.wait_ge(s_d3sf.h, c)
                            tensor.matmul(p_z[:], ones128[:],
                                          zsb[:, m * 512:(m + 1) * 512],
                                          start=(m == 0), stop=(m == KE - 1))
                            ii = tensor.matmul(p_z2[:], ones128[:],
                                               zqst[:, (cm % 2) * 512:(cm % 2 + 1) * 512],
                                               start=(m == 0), stop=(m == KE - 1))
                            s_d3st.inc(ii)

                        tensor.wait_ge(s_d3gf.h, NF * (c + 1))
                        for m in range(KE):
                            cm = c * KE + m
                            tensor.wait_ge(s_d3wd.h, 16 * (cm + 1))
                            if cm >= 2:
                                tensor.wait_ge(s_d3z.h, cm - 1)
                            ii = None
                            for f in range(NF):
                                ii = tensor.matmul(
                                    p_mlp[cm % 2][:],
                                    wdm[:, (cm % 2) * NF * 128 + f * 128:
                                        (cm % 2) * NF * 128 + (f + 1) * 128],
                                    gbuf[:, f * 512:(f + 1) * 512],
                                    start=(f == 0), stop=(f == NF - 1))
                            s_d3dn.inc(ii)
                            if m >= 1:
                                stats(c, m - 1)
                        stats(c, KE - 1)
                        for m in range(KE):
                            cm = c * KE + m
                            tensor.wait_ge(s_d3x2.h, cm + 1)
                            if cm >= 2:
                                tensor.wait_ge(s_d3oev.h, cm - 1)
                            ii = None
                            for t in range(4):
                                ii = tensor.transpose(
                                    p_tr[cm % 2][:, t * 128:(t + 1) * 128],
                                    x2st[:, (cm % 2) * 512 + t * 128:
                                         (cm % 2) * 512 + (t + 1) * 128],
                                    ident[:])
                            s_d3tr.inc(ii)

                @blk.scalar
                def _(scalar):
                    scalar.wait_ge(s_ph.h, PHC[0])
                    for c in range(NC_):
                        scalar.preload_act_table(AF.Silu)
                        for f in range(NF):
                            F_ = c * NF + f
                            scalar.wait_ge(s_d3mm.h, F_ + 1)
                            if F_ >= 2:
                                scalar.wait_ge(s_d3gf.h, F_ - 1)
                            ii = scalar.activation(
                                sgst[:, (F_ % 2) * 512:(F_ % 2 + 1) * 512],
                                p_g[:], AF.Silu)
                            s_d3sg.inc(ii)
                        for m in range(KE):
                            cm = c * KE + m
                            scalar.wait_ge(s_d3z.h, cm + 1)
                            if cm >= 2:
                                scalar.wait_ge(s_d3st.h, cm - 1)
                            ii = scalar.activation(
                                zqst[:, (cm % 2) * 512:(cm % 2 + 1) * 512],
                                zsb[:, m * 512:(m + 1) * 512], AF.Square)
                            s_d3zq.inc(ii)
                        scalar.wait_ge(s_d3st.h, 16 * (c + 1))
                        scalar.activation(mbz[:], p_z[:], AF.Identity, scale=1.0 / E)
                        ii = scalar.activation(uqz[:], p_z2[:], AF.Identity,
                                               scale=1.0 / E)
                        s_d3sf.inc(ii)
                        preload_rsqrt(scalar)
                        scalar.wait_ge(s_d3var.h, c + 1)
                        ii = rsqrt_act(scalar, rbz[:], varz[:], LN_EPS)
                        s_d3rb.inc(ii)
                        for m in range(KE):
                            cm = c * KE + m
                            scalar.wait_ge(s_d3tr.h, cm + 1)
                            if cm >= 2:
                                scalar.wait_ge(s_out[cm % 2].h, 16 * (cm // 2))
                            ii = scalar.activation(
                                ost[:, (cm % 2) * 512:(cm % 2 + 1) * 512],
                                p_tr[cm % 2][:], AF.Identity)
                            s_d3oev.inc(ii)
                            dst = out_t[_chunk(c), m * 128:(m + 1) * 128].rearrange(
                                "(t p) e -> p t e", p=128)
                            src = ost[:, (cm % 2) * 512:(cm % 2 + 1) * 512].rearrange(
                                "p (t e) -> p t e", t=4)
                            scalar.dma_start(dst, src).then_inc(s_out[cm % 2].h, 16)
                            s_out[cm % 2].n += 16

                @blk.vector
                def _(vector):
                    vector.wait_ge(s_ph.h, PHC[0])
                    for c in range(NC_):
                        for f in range(NF):
                            F_ = c * NF + f
                            vector.wait_ge(s_d3sg.h, F_ + 1)
                            if f == 0 and c >= 1:
                                vector.wait_ge(s_d3dn.h, KE * c)
                            ii = vector.tensor_mul(gbuf[:, f * 512:(f + 1) * 512],
                                                   sgst[:, (F_ % 2) * 512:(F_ % 2 + 1) * 512],
                                                   p_u[:])
                            s_d3gf.inc(ii)
                        vector.wait_ge(s_d3xld.h, (2 * KE + 1) * 16 * (c + 1))
                        for m in range(KE):
                            cm = c * KE + m
                            vector.wait_ge(s_d3dn.h, cm + 1)
                            vector.tensor_sub(s1st[:], hd3[:, m * 512:(m + 1) * 512],
                                              x1d3[:, m * 512:(m + 1) * 512])
                            vector.tensor_add(s2st[:], p_mlp[cm % 2][:], s1st[:])
                            vector.tensor_mul(s3st[:], msbuf[:], s2st[:])
                            ii = vector.scalar_tensor_tensor(
                                zsb[:, m * 512:(m + 1) * 512],
                                x1d3[:, m * 512:(m + 1) * 512], 2.0, s3st[:],
                                ALU.mult, ALU.add)
                            s_d3z.inc(ii)
                        vector.wait_ge(s_d3sf.h, c + 1)
                        vector.tensor_mul(tz[:], mbz[:], mbz[:])
                        ii = vector.tensor_sub(varz[:], uqz[:], tz[:])
                        s_d3var.inc(ii)
                        vector.wait_ge(s_d3rb.h, c + 1)
                        for m in range(KE):
                            cm = c * KE + m
                            if cm >= 2:
                                vector.wait_ge(s_d3tr.h, cm - 1)
                            vector.tensor_sub(tz[:], zsb[:, m * 512:(m + 1) * 512],
                                              mbz[:])
                            ii = vector.tensor_mul(
                                x2st[:, (cm % 2) * 512:(cm % 2 + 1) * 512],
                                tz[:], rbz[:])
                            s_d3x2.inc(ii)

            drain_block()

    return nc


def _host_consts():
    pos = np.arange(N, dtype=np.float64)
    inv = 1.0 / (10000.0 ** (np.arange(0, 128, 2, dtype=np.float64) / 128.0))
    ang = pos[:, None] * inv[None, :]
    c = np.cos(ang).astype(np.float32)
    s = np.sin(ang).astype(np.float32)
    c2 = np.ascontiguousarray(np.concatenate([c, c], axis=1).T)
    s2 = np.ascontiguousarray(np.concatenate([-s, s], axis=1).T)
    kk = np.arange(128)[:, None]
    qq = np.arange(512)[None, :]
    msk = np.concatenate([(qq >= kk + 128 * p) for p in range(4)],
                         axis=1).astype(ml_dtypes.bfloat16)
    ones = np.ones((128, 128), dtype=ml_dtypes.bfloat16)
    ident = np.eye(128, dtype=ml_dtypes.bfloat16)
    return c2, s2, np.ascontiguousarray(msk), ones, ident


def kernel(L0_attn_x, L0_attn_w_qkv, L0_attn_w_out, L0_attn_ln_scale, L0_attn_ln_bias,
           L0_ffn_w_router, L0_ffn_b_router, L0_ffn_norm1_w, L0_ffn_wv, L0_ffn_bv,
           L0_ffn_wo, L0_ffn_norm2_w, L0_ffn_w_gate, L0_ffn_w_up, L0_ffn_w_down):
    c2, s2, msk, ones, ident = _host_consts()
    A = np.ascontiguousarray
    shared = {
        "w_qkv": A(L0_attn_w_qkv, dtype=np.float32),
        "w_out": A(L0_attn_w_out, dtype=np.float32),
        "wv": A(L0_ffn_wv, dtype=np.float32),
        "wo": A(L0_ffn_wo, dtype=np.float32),
        "w_gate": A(L0_ffn_w_gate, dtype=np.float32),
        "w_up": A(L0_ffn_w_up, dtype=np.float32),
        "w_down": A(L0_ffn_w_down, dtype=np.float32),
        "gam": A(np.asarray(L0_attn_ln_scale, dtype=np.float32).reshape(128, 1)),
        "bet": A(np.asarray(L0_attn_ln_bias, dtype=np.float32).reshape(128, 1)),
        "w1t": A(np.asarray(L0_ffn_norm1_w, dtype=np.float32).reshape(KE, 128).T),
        "w2t": A(np.asarray(L0_ffn_norm2_w, dtype=np.float32).reshape(KE, 128).T),
        "bvt": A(np.asarray(L0_ffn_bv, dtype=np.float32).reshape(KE, 128).T),
        "wr_rep": A(np.broadcast_to(
            np.asarray(L0_ffn_w_router, dtype=np.float32)[:, None], (E, 128))),
        "br_t": A(np.broadcast_to(
            np.asarray(L0_ffn_b_router, dtype=np.float32), (128, 1))),
        "c2": c2, "s2": s2, "msk": msk, "ones128": ones, "ident": ident,
        "epsv": A(np.broadcast_to(np.array([LN_EPS, RMS_EPS], dtype=np.float32),
                                  (128, 2))),
    }
    xs = np.asarray(L0_attn_x, dtype=np.float32)
    nc = build_program(debug=False)
    in_maps = []
    for i in range(8):
        m = dict(shared)
        m["x"] = A(xs[i])
        in_maps.append(m)
    res = run_bass_kernel_spmd(nc, in_maps, core_ids=list(range(8)))
    return np.stack([r["out"] for r in res.results], axis=0)


# version 11
# speedup vs baseline: 1.1263x; 1.0431x over previous
import contextlib
import numpy as np
import ml_dtypes
import concourse.bass as bass
import concourse.mybir as mybir
from concourse.bass_utils import run_bass_kernel_spmd

F32 = mybir.dt.float32
BF16 = mybir.dt.bfloat16
AF = mybir.ActivationFunctionType
ALU = mybir.AluOpType

N = 4096
E = 2048
H = 16
FF = 5632
NC_ = 8
NTI = 32
KE = 16
NF = FF // 128
LN_EPS = 1e-5
RMS_EPS = 1e-6


SIM_SAFE = [False]
A1_CAP = [256]
NO_POOL = [False]


def rsqrt_act(scalar, out, in_, eps):
    if SIM_SAFE[0]:
        scalar.activation(out, in_, AF.Ln, bias=eps)
        return scalar.activation(out, out, AF.Exp, scale=-0.5)
    return scalar.activation(out, in_, AF.Abs_reciprocal_sqrt, bias=eps)


def preload_rsqrt(scalar):
    if SIM_SAFE[0]:
        return preload_set_with(scalar, [AF.Ln, AF.Exp])
    return scalar.preload_act_table(AF.Abs_reciprocal_sqrt)


def preload_set_with(scalar, funcs):
    from concourse.hw_specs import get_activation_tables
    for idx, fs in enumerate(get_activation_tables(scalar.bass.m.arch).values()):
        if all(f in fs for f in funcs):
            return scalar.add_instruction(
                mybir.InstLoadActFuncSet(
                    name=scalar.bass.get_next_instruction_name(),
                    act_func_set_id=idx, ins=[], outs=[]))
    raise ValueError("no table set with %s" % funcs)


class Sem:
    def __init__(self, es, nc, name):
        self.h = es.enter_context(nc.semaphore(name))
        self.n = 0

    def inc(self, instr, k=1):
        instr.then_inc(self.h, k)
        self.n += k


def _chunk(c):
    return slice(c * 512, (c + 1) * 512)


def build_program(debug=False, upto='D3'):
    nc = bass.Bass("TRN2", target_bir_lowering=False)

    x_in = nc.dram_tensor("x", [N, E], F32, kind="ExternalInput").ap()
    w_qkv = nc.dram_tensor("w_qkv", [E, 3 * E], F32, kind="ExternalInput").ap()
    w_out = nc.dram_tensor("w_out", [E, E], F32, kind="ExternalInput").ap()
    wv_in = nc.dram_tensor("wv", [E, E], F32, kind="ExternalInput").ap()
    wo_in = nc.dram_tensor("wo", [E, E], F32, kind="ExternalInput").ap()
    wg_in = nc.dram_tensor("w_gate", [E, FF], F32, kind="ExternalInput").ap()
    wu_in = nc.dram_tensor("w_up", [E, FF], F32, kind="ExternalInput").ap()
    wd_in = nc.dram_tensor("w_down", [FF, E], F32, kind="ExternalInput").ap()
    gam_in = nc.dram_tensor("gam", [128, 1], F32, kind="ExternalInput").ap()
    bet_in = nc.dram_tensor("bet", [128, 1], F32, kind="ExternalInput").ap()
    w1_in = nc.dram_tensor("w1t", [128, KE], F32, kind="ExternalInput").ap()
    w2_in = nc.dram_tensor("w2t", [128, KE], F32, kind="ExternalInput").ap()
    bv_in = nc.dram_tensor("bvt", [128, KE], F32, kind="ExternalInput").ap()
    wr_in = nc.dram_tensor("wr_rep", [E, 128], F32, kind="ExternalInput").ap()
    br_in = nc.dram_tensor("br_t", [128, 1], F32, kind="ExternalInput").ap()
    c2_in = nc.dram_tensor("c2", [128, N], F32, kind="ExternalInput").ap()
    s2_in = nc.dram_tensor("s2", [128, N], F32, kind="ExternalInput").ap()
    msk_in = nc.dram_tensor("msk", [128, 4 * 512], BF16, kind="ExternalInput").ap()
    ones_in = nc.dram_tensor("ones128", [128, 128], BF16, kind="ExternalInput").ap()
    eps_in = nc.dram_tensor("epsv", [128, 2], F32, kind="ExternalInput").ap()
    id_in = nc.dram_tensor("ident", [128, 128], BF16, kind="ExternalInput").ap()

    out_t = nc.dram_tensor("out", [N, E], F32, kind="ExternalOutput").ap()

    dk = "ExternalOutput" if debug else "Internal"
    xbf = nc.dram_tensor("xbf", [N, E], BF16, kind="Internal").ap()
    xlo = nc.dram_tensor("xlo", [N, E], BF16, kind="Internal").ap()
    qT = nc.dram_tensor("qT", [H, 128, N], BF16, kind=dk).ap()
    kTs = nc.dram_tensor("kTs", [H, 128, N], BF16, kind=dk).ap()
    vsc = nc.dram_tensor("vsc", [NTI, 128, E], BF16, kind=dk).ap()
    aT = nc.dram_tensor("aT", [H, 128, N], BF16, kind=dk).ap()
    x1sc = nc.dram_tensor("x1sc", [KE, 128, N], BF16, kind=dk).ap()
    msc = nc.dram_tensor("msc", [NC_, 128, 512], BF16, kind=dk).ap()
    vfsc = nc.dram_tensor("vfsc", [KE, 128, N], BF16, kind="Internal").ap()
    hsc = nc.dram_tensor("hsc", [KE, 128, N], BF16, kind=dk).ap()
    h2sc = nc.dram_tensor("h2sc", [KE, 128, N], BF16, kind=dk).ap()
    wvq_bf = nc.dram_tensor("wvq_bf", [E, E], BF16, kind="Internal").ap()
    wout_bf = nc.dram_tensor("wout_bf", [E, E], BF16, kind="Internal").ap()
    wvf_bf = nc.dram_tensor("wvf_bf", [E, E], BF16, kind="Internal").ap()
    wo_bf = nc.dram_tensor("wo_bf", [E, E], BF16, kind="Internal").ap()
    gate_bf = nc.dram_tensor("gate_bf", [NF, E, 128], BF16, kind="Internal").ap()
    up_bf = nc.dram_tensor("up_bf", [NF, E, 128], BF16, kind="Internal").ap()
    down_bf = nc.dram_tensor("down_bf", [KE, FF, 128], BF16, kind="Internal").ap()

    top = contextlib.ExitStack()
    with top:
        ones128 = top.enter_context(nc.sbuf_tensor("ones_sb", [128, 128], BF16))
        ident = top.enter_context(nc.sbuf_tensor("id_sb", [128, 128], BF16))
        gam = top.enter_context(nc.sbuf_tensor("gam_sb", [128, 1], F32))
        bet = top.enter_context(nc.sbuf_tensor("bet_sb", [128, 1], F32))
        w1sb = top.enter_context(nc.sbuf_tensor("w1_sb", [128, KE], F32))
        w2sb = top.enter_context(nc.sbuf_tensor("w2_sb", [128, KE], F32))
        bvsb = top.enter_context(nc.sbuf_tensor("bv_sb", [128, KE], F32))
        brsb = top.enter_context(nc.sbuf_tensor("br_sb", [128, 1], F32))
        epsb = top.enter_context(nc.sbuf_tensor("eps_sb", [128, 2], F32))
        barsb = top.enter_context(nc.sbuf_tensor("bar_sb", [128, 16], F32))
        nc.const_aps.register(F32, LN_EPS, epsb[:, 0:1])
        nc.const_aps.register(F32, RMS_EPS, epsb[:, 1:2])

        def S(name):
            return Sem(top, nc, name)

        s_cst = S("cst"); s_xt = S("xt")
        s_wq = [S("wq0"), S("wq1")]
        s_wcast = S("wcast"); s_ph = S("ph")
        s_a1mm = S("a1mm"); s_a1pf = S("a1pf"); s_a1r = S("a1r")
        s_a1st = S("a1st"); s_a1sf = S("a1sf"); s_a1var = S("a1var")
        s_a1rb = S("a1rb"); s_a1rl = S("a1rl"); s_a1qo = S("a1qo"); s_a1sw = S("a1sw")
        s_xbf = S("xbf")
        s_a1sp = [S("a1sp0"), S("a1sp1")]
        s_wv = S("wv"); s_a2mm = S("a2mm"); s_a2ev = S("a2ev")
        s_a2sp = [S("a2sp0"), S("a2sp1")]
        s_bld = [S("bld0"), S("bld1")]
        s_bmm1 = S("bmm1"); s_bsq1 = S("bsq1")
        s_bsq2 = S("bsq2"); s_bmm2 = S("bmm2"); s_bep = S("bep")
        s_bsp = [S("bsp0"), S("bsp1")]
        s_xlld = [S("xlld0"), S("xlld1")]
        s_xl = S("xl")
        s_xlsp = [S("xlsp0"), S("xlsp1")]
        s_cw = S("cw")
        s_cld = [S("cld0"), S("cld1")]
        s_cx0 = S("cx0")
        s_cmm = S("cmm"); s_cev = S("cev")
        s_csq = S("csq"); s_cst2 = S("cst2"); s_csf = S("csf"); s_cvar = S("cvar")
        s_crb = S("crb"); s_cx1 = S("cx1"); s_cx1sp = S("cx1sp"); s_crt = S("crt")
        s_cms = S("cms")
        s_cmsp = [S("cmsp0"), S("cmsp1")]
        s_d1w = S("d1w")
        s_d1ld = [S("d1ld0"), S("d1ld1")]
        s_d1xq = S("d1xq"); s_d1ss = S("d1ss")
        s_d1rb = S("d1rb"); s_d1h1 = S("d1h1"); s_d1mm = S("d1mm"); s_d1ev = S("d1ev")
        s_d1sp = [S("d1sp0"), S("d1sp1")]
        s_d2w = S("d2w")
        s_d2ld = [S("d2ld0"), S("d2ld1")]
        s_d2mm = S("d2mm"); s_d2h = S("d2h")
        s_d2sq = S("d2sq"); s_d2st = S("d2st"); s_d2rb = S("d2rb")
        s_d2h2 = S("d2h2")
        s_d2sp = [S("d2sp0"), S("d2sp1")]
        s_d2hsp = S("d2hsp")
        s_d3wf = S("d3wf"); s_d3wd = S("d3wd")
        s_d3h2ld = [S("d3h2ld0"), S("d3h2ld1")]
        s_d3xld = S("d3xld")
        s_d3mm = S("d3mm"); s_d3sg = S("d3sg"); s_d3gf = S("d3gf"); s_d3dn = S("d3dn")
        s_d3z = S("d3z"); s_d3zq = S("d3zq"); s_d3st = S("d3st"); s_d3sf = S("d3sf")
        s_d3var = S("d3var"); s_d3rb = S("d3rb"); s_d3x2 = S("d3x2"); s_d3tr = S("d3tr")
        s_d3oev = S("d3oev")
        s_out = [S("outd0"), S("outd1")]

        PHC = [0]
        CSTB = [0]
        _dma_sems_all = [s_cst, s_xt, s_wq[0], s_wq[1], s_wcast,
                         s_a1sp[0], s_a1sp[1], s_wv, s_a2sp[0], s_a2sp[1],
                         s_bld[0], s_bld[1], s_bsp[0], s_bsp[1],
                         s_xlld[0], s_xlld[1], s_xlsp[0], s_xlsp[1],
                         s_cw, s_cld[0], s_cld[1], s_cx0, s_cx1sp, s_cmsp[0], s_cmsp[1],
                         s_d1w, s_d1ld[0], s_d1ld[1], s_d1sp[0], s_d1sp[1],
                         s_d2w, s_d2ld[0], s_d2ld[1], s_d2sp[0], s_d2sp[1], s_d2hsp,
                         s_d3wf, s_d3wd, s_d3h2ld[0], s_d3h2ld[1], s_d3xld,
                         s_out[0], s_out[1]]

        def barrier_block(sync_waits=(), scalar_waits=()):
            tgt = PHC[0] + 5
            with nc.Block() as blk:
                @blk.sync
                def _(e):
                    for (sem, v) in sync_waits:
                        e.wait_ge(sem.h, v)
                    e.nop().then_inc(s_ph.h, 1); s_ph.n += 1
                    e.wait_ge(s_ph.h, tgt)

                @blk.gpsimd
                def _(e):
                    e.nop().then_inc(s_ph.h, 1); s_ph.n += 1
                    e.wait_ge(s_ph.h, tgt)

                @blk.tensor
                def _(e):
                    e.nop().then_inc(s_ph.h, 1); s_ph.n += 1
                    e.wait_ge(s_ph.h, tgt)

                @blk.vector
                def _(e):
                    e.wait_ge(s_cst.h, 144)
                    ii = e.tensor_copy(barsb[:, 0:1], epsb[:, 0:1])
                    ii.then_inc(s_ph.h, 1); s_ph.n += 1
                    e.wait_ge(s_ph.h, tgt)

                @blk.scalar
                def _(e):
                    for (sem, v) in scalar_waits:
                        e.wait_ge(sem.h, v)
                    e.wait_ge(s_cst.h, 144)
                    ii = e.copy(barsb[:, 1:2], epsb[:, 0:1])
                    ii.then_inc(s_ph.h, 1); s_ph.n += 1
                    e.wait_ge(s_ph.h, tgt)
            PHC[0] = s_ph.n
            assert PHC[0] == tgt

        def drain_block():
            with nc.Block() as blk:
                @blk.sync
                def _(sync):
                    for sem in _dma_sems_all:
                        if sem.n > 0:
                            sync.wait_ge(sem.h, sem.n)

        with nc.Block() as blk:
            @blk.sync
            def _(sync):
                for dst, src in [(ones128, ones_in), (ident, id_in),
                                 (gam, gam_in), (bet, bet_in), (w1sb, w1_in),
                                 (w2sb, w2_in), (bvsb, bv_in), (brsb, br_in),
                                 (epsb, eps_in)]:
                    sync.dma_start(dst[:], src).then_inc(s_cst.h, 16)
                    s_cst.n += 16

            @blk.gpsimd
            def _(gpsimd):
                for c in range(NC_):
                    gpsimd.dma_start(xbf[_chunk(c), :], x_in[_chunk(c), :]
                                     ).then_inc(s_xbf.h, 16)
                    s_xbf.n += 16



        CST0 = s_cst.n

        with contextlib.ExitStack() as pa:
            xT = pa.enter_context(nc.sbuf_tensor("xT", [128, KE * N], BF16))

            def xTs(k, c=None):
                if c is None:
                    return xT[:, k * N:(k + 1) * N]
                return xT[:, k * N + c * 512: k * N + (c + 1) * 512]

            with contextlib.ExitStack() as pa1:
                c2sb = pa1.enter_context(nc.sbuf_tensor("c2_sb", [128, N], F32))
                s2sb = pa1.enter_context(nc.sbuf_tensor("s2_sb", [128, N], F32))
                wq = pa1.enter_context(nc.sbuf_tensor("wq_sb", [128, 2 * KE * 128], BF16))
                swp = pa1.enter_context(nc.sbuf_tensor("swp", [128, 2 * 512], BF16))
                tc = pa1.enter_context(nc.sbuf_tensor("tc", [128, 512], F32))
                tsb = pa1.enter_context(nc.sbuf_tensor("tsb", [128, 512], F32))
                rbuf = pa1.enter_context(nc.sbuf_tensor("rbuf", [128, 3 * 512], F32))
                rbf = pa1.enter_context(nc.sbuf_tensor("rbf", [128, 2 * 512], BF16))
                rsqb = pa1.enter_context(nc.sbuf_tensor("rsqb", [128, 2 * 512], BF16))
                mbb = pa1.enter_context(nc.sbuf_tensor("mbb", [128, 2 * 512], BF16))
                uqb = pa1.enter_context(nc.sbuf_tensor("uqb", [128, 2 * 512], F32))
                varb = pa1.enter_context(nc.sbuf_tensor("varb", [128, 512], F32))
                rbb = pa1.enter_context(nc.sbuf_tensor("rbb", [128, 2 * 512], F32))
                tb2 = pa1.enter_context(nc.sbuf_tensor("tb2", [128, 512], BF16))
                tb2f = pa1.enter_context(nc.sbuf_tensor("tb2f", [128, 512], F32))
                rlb = pa1.enter_context(nc.sbuf_tensor("rlb", [128, 2 * 512], BF16))
                qob = pa1.enter_context(nc.sbuf_tensor("qob", [128, 2 * 512], BF16))
                p_p = [pa1.enter_context(nc.psum_tensor("pp%d" % i, [128, 512], F32))
                       for i in range(2)]
                p_sa = [pa1.enter_context(nc.psum_tensor("psa%d" % i, [128, 512], F32))
                        for i in range(2)]
                p_sb = [pa1.enter_context(nc.psum_tensor("psb%d" % i, [128, 512], F32))
                        for i in range(2)]

                def b2(buf, u):
                    return buf[:, (u % 2) * 512:(u % 2 + 1) * 512]

                def b3(buf, u):
                    return buf[:, (u % 3) * 512:(u % 3 + 1) * 512]

                with nc.Block() as blk:
                    @blk.sync
                    def _(sync):
                        sync.wait_ge(s_cst.h, CST0)
                        sync.dma_start(c2sb[:], c2_in).then_inc(s_cst.h, 16)
                        sync.dma_start(s2sb[:], s2_in).then_inc(s_cst.h, 16)
                        s_cst.n += 32
                        sync.wait_ge(s_xbf.h, 128)
                        for c in range(NC_):
                            for k in range(KE):
                                sync.dma_start(
                                    xTs(k, c),
                                    xbf[_chunk(c), k * 128:(k + 1) * 128],
                                    transpose=True).then_inc(s_xt.h, 16)
                                s_xt.n += 16
                        for u in range(256):
                            s, c = divmod(u, NC_)
                            t = qT if s < H else kTs
                            sync.wait_ge(s_a1qo.h, u + 1)
                            sync.dma_start(t[s % H][:, _chunk(c)], b2(qob, u)
                                           ).then_inc(s_a1sp[u % 2].h, 16)
                            s_a1sp[u % 2].n += 16

                    @blk.gpsimd
                    def _(gpsimd):
                        casts = []
                        for tgt, srcap in [(wvq_bf, w_qkv[:, 2 * E:3 * E]),
                                           (wout_bf, w_out),
                                           (wvf_bf, wv_in),
                                           (wo_bf, wo_in)]:
                            for i8 in range(8):
                                rs8 = slice(i8 * 256, (i8 + 1) * 256)
                                casts.append((tgt[rs8, :], srcap[rs8, :]))
                        for s in range(2 * H):
                            if s >= 2:
                                gpsimd.wait_ge(s_a1mm.h, 8 * (s - 1))
                            col0 = s * 128 if s < H else E + (s - H) * 128
                            src = w_qkv[:, col0:col0 + 128].rearrange(
                                "(k p) c -> p k c", p=128)
                            dst = wq[:, (s % 2) * KE * 128:(s % 2 + 1) * KE * 128
                                     ].rearrange("p (k c) -> p k c", k=KE)
                            gpsimd.dma_start(dst, src).then_inc(s_wq[s % 2].h, 16)
                            s_wq[s % 2].n += 16
                            gpsimd.dma_start(casts[s][0], casts[s][1]
                                             ).then_inc(s_a1sw.h, 16)
                            s_a1sw.n += 16
                        for f in range(NF):
                            gpsimd.dma_start(gate_bf[f], wg_in[:, f * 128:(f + 1) * 128]
                                             ).then_inc(s_wcast.h, 16)
                            gpsimd.dma_start(up_bf[f], wu_in[:, f * 128:(f + 1) * 128]
                                             ).then_inc(s_wcast.h, 16)
                            s_wcast.n += 32
                        for m in range(KE):
                            gpsimd.dma_start(down_bf[m], wd_in[:, m * 128:(m + 1) * 128]
                                             ).then_inc(s_wcast.h, 16)
                            s_wcast.n += 16

                    @blk.tensor
                    def _(tensor):
                        tensor.wait_ge(s_cst.h, CST0)
                        tensor.wait_ge(s_xt.h, 16 * KE * NC_)

                        def stats(v):
                            tensor.wait_ge(s_a1r.h, v + 1)
                            if v >= 2:
                                tensor.wait_ge(s_a1sf.h, v - 1)
                            tensor.matmul(p_sa[v % 2][:], ones128[:], b2(rbf, v),
                                          start=True, stop=True)
                            ii = tensor.matmul(p_sb[v % 2][:], ones128[:], b2(rsqb, v),
                                               start=True, stop=True)
                            s_a1st.inc(ii)

                        for s in range(2 * H):
                            for c in range(NC_):
                                u = s * NC_ + c
                                tensor.wait_ge(s_wq[s % 2].h, 16 * (s // 2 + 1))
                                if u >= 2:
                                    tensor.wait_ge(s_a1pf.h, u - 1)
                                wqs = wq[:, (s % 2) * KE * 128:(s % 2 + 1) * KE * 128]
                                ii = None
                                for k in range(KE):
                                    ii = tensor.matmul(
                                        p_p[u % 2][:],
                                        wqs[:, k * 128:(k + 1) * 128],
                                        xTs(k, c),
                                        start=(k == 0), stop=(k == KE - 1))
                                s_a1mm.inc(ii)
                                if u >= 1:
                                    stats(u - 1)
                        stats(255)

                    @blk.vector
                    def _(vector):
                        vector.wait_ge(s_cst.h, CST0 + 32)
                        for u in range(258):
                            if u < 256:
                                s, c = divmod(u, NC_)
                                cc = _chunk(c)
                                vector.wait_ge(s_a1mm.h, u + 1)
                                if u >= 2:
                                    vector.wait_ge(s_a1st.h, u - 1)
                                sw = slice((u % 2) * 512, (u % 2 + 1) * 512)
                                vector.tensor_copy(swp[0:64, sw],
                                                   p_p[u % 2][64:128, :])
                                vector.tensor_copy(swp[64:128, sw],
                                                   p_p[u % 2][0:64, :])
                                ii = vector.tensor_mul(tc[:], p_p[u % 2][:],
                                                       c2sb[:, cc])
                                s_a1pf.inc(ii)
                                vector.tensor_mul(tsb[:], swp[:, sw], s2sb[:, cc])
                                vector.tensor_add(b3(rbuf, u), tc[:], tsb[:])
                                vector.tensor_copy(b2(rbf, u), b3(rbuf, u))
                                ii = vector.tensor_mul(b2(rsqb, u), b2(rbf, u),
                                                       b2(rbf, u))
                                s_a1r.inc(ii)
                            if 0 <= u - 1 < 256:
                                v = u - 1
                                vector.wait_ge(s_a1sf.h, v + 1)
                                if v >= 1:
                                    vector.wait_ge(s_a1rb.h, v)
                                vector.tensor_mul(tb2[:], b2(mbb, v), b2(mbb, v))
                                ii = vector.tensor_sub(varb[:], b2(uqb, v), tb2[:])
                                s_a1var.inc(ii)
                            if 0 <= u - 2 < 256:
                                w = u - 2
                                vector.wait_ge(s_a1rb.h, w + 1)
                                if w >= 2:
                                    vector.wait_ge(s_a1qo.h, w - 1)
                                vector.tensor_sub(tb2f[:], b3(rbuf, w), b2(mbb, w))
                                ii = vector.tensor_mul(b2(rlb, w), tb2f[:],
                                                       b2(rbb, w))
                                s_a1rl.inc(ii)

                    @blk.scalar
                    def _(scalar):
                        scalar.wait_ge(s_cst.h, CST0)
                        preload_rsqrt(scalar)
                        for u in range(258):
                            if 0 <= u - 1 < 256:
                                v = u - 1
                                scalar.wait_ge(s_a1st.h, v + 1)
                                if v >= 2:
                                    scalar.wait_ge(s_a1rl.h, v - 1)
                                    scalar.wait_ge(s_a1var.h, v - 1)
                                scalar.activation(b2(mbb, v), p_sa[v % 2][:],
                                                  AF.Identity, scale=1.0 / 128)
                                ii = scalar.activation(b2(uqb, v), p_sb[v % 2][:],
                                                       AF.Identity, scale=1.0 / 128)
                                s_a1sf.inc(ii)
                                scalar.wait_ge(s_a1var.h, v + 1)
                                ii = rsqrt_act(scalar, b2(rbb, v), varb[:], LN_EPS)
                                s_a1rb.inc(ii)
                            if 0 <= u - 2 < 256:
                                w = u - 2
                                scalar.wait_ge(s_a1rl.h, w + 1)
                                if w >= 2:
                                    scalar.wait_ge(s_a1sp[w % 2].h, 16 * (w // 2))
                                ii = scalar.activation(b2(qob, w), b2(rlb, w),
                                                       AF.Identity,
                                                       scale=gam[:, 0:1],
                                                       bias=bet[:, 0:1])
                                s_a1qo.inc(ii)

                barrier_block(sync_waits=[(s_a1sp[0], s_a1sp[0].n), (s_a1sp[1], s_a1sp[1].n)])
                if upto == 'A1':
                    drain_block()
                    return nc

            with contextlib.ExitStack() as pa2:
                wvc = pa2.enter_context(nc.sbuf_tensor("wvc_sb", [128, KE * E], BF16))
                vst = pa2.enter_context(nc.sbuf_tensor("vst", [128, 2 * 512], BF16))
                p_v = [pa2.enter_context(nc.psum_tensor("pv%d" % i, [128, 512], F32))
                       for i in range(4)]

                with nc.Block() as blk:
                    @blk.gpsimd
                    def _(gpsimd):
                        gpsimd.wait_ge(s_ph.h, PHC[0])
                        gpsimd.wait_ge(s_a1sw.h, s_a1sw.n)
                        for k in range(KE):
                            gpsimd.dma_start(
                                wvc[:, k * E:(k + 1) * E],
                                wvq_bf[k * 128:(k + 1) * 128, :]
                            ).then_inc(s_wv.h, 16)
                            s_wv.n += 16

                    @blk.scalar
                    def _(scalar):
                        scalar.wait_ge(s_ph.h, PHC[0])
                        for i in range(NTI):
                            for vc in range(4):
                                u = 4 * i + vc
                                scalar.wait_ge(s_a2mm.h, u + 1)
                                if u >= 2:
                                    scalar.wait_ge(s_a2sp[u % 2].h, 16 * (u // 2))
                                ii = scalar.activation(
                                    vst[:, (u % 2) * 512:(u % 2 + 1) * 512],
                                    p_v[vc][:], AF.Identity)
                                s_a2ev.inc(ii)

                    @blk.tensor
                    def _(tensor):
                        tensor.wait_ge(s_ph.h, PHC[0])
                        tensor.wait_ge(s_wv.h, 256)
                        for i in range(NTI):
                            if i >= 1:
                                tensor.wait_ge(s_a2ev.h, 4 * i)
                            ii = None
                            for k in range(KE):
                                for vc in range(4):
                                    ii = tensor.matmul(
                                        p_v[vc][:],
                                        xT[:, k * N + i * 128: k * N + (i + 1) * 128],
                                        wvc[:, k * E + vc * 512: k * E + (vc + 1) * 512],
                                        start=(k == 0), stop=(k == KE - 1))
                                    if k == KE - 1:
                                        s_a2mm.inc(ii)

                    @blk.sync
                    def _(sync):
                        sync.wait_ge(s_ph.h, PHC[0])
                        for i in range(NTI):
                            for vc in range(4):
                                u = 4 * i + vc
                                sync.wait_ge(s_a2ev.h, u + 1)
                                sync.dma_start(
                                    vsc[i][:, vc * 512:(vc + 1) * 512],
                                    vst[:, (u % 2) * 512:(u % 2 + 1) * 512]
                                ).then_inc(s_a2sp[u % 2].h, 16)
                                s_a2sp[u % 2].n += 16

                barrier_block(sync_waits=[(s_a2sp[0], s_a2sp[0].n),
                                          (s_a2sp[1], s_a2sp[1].n)])
                if upto == 'A2':
                    drain_block()
                    return nc

        with contextlib.ExitStack() as pb:
            masks = pb.enter_context(nc.sbuf_tensor("msk_sb", [128, 4 * 512], BF16))
            qsb = pb.enter_context(nc.sbuf_tensor("qsb", [128, 2 * N], BF16))
            ksb = pb.enter_context(nc.sbuf_tensor("ksb", [128, 2 * N], BF16))
            vsb = pb.enter_context(nc.sbuf_tensor("vsb", [128, 2 * N], BF16))
            a2b = pb.enter_context(nc.sbuf_tensor("a2b", [128, 4 * 512], BF16))
            a4b = pb.enter_context(nc.sbuf_tensor("a4b", [128, 4 * 512], BF16))
            rbB = pb.enter_context(nc.sbuf_tensor("rbB", [128, 2 * 512], F32))
            astg = pb.enter_context(nc.sbuf_tensor("astg", [128, 2 * 512], BF16))
            xstg = pb.enter_context(nc.sbuf_tensor("xstg", [128, 2 * 512], F32))
            xbstg = pb.enter_context(nc.sbuf_tensor("xbstg", [128, 2 * 512], BF16))
            xlostg = pb.enter_context(nc.sbuf_tensor("xlostg", [128, 2 * 512], BF16))
            p_s = [pb.enter_context(nc.psum_tensor("ps%d" % i, [128, 512], F32))
                   for i in range(4)]
            p_o = [pb.enter_context(nc.psum_tensor("po%d" % i, [128, 512], F32))
                   for i in range(2)]
            p_d = [pb.enter_context(nc.psum_tensor("pd%d" % i, [128, 512], F32))
                   for i in range(2)]
            LAG = 3

            def xlt(t):
                r, cb = divmod(t, 4)
                return (slice(r * 128, (r + 1) * 128), slice(cb * 512, (cb + 1) * 512))

            units = []
            U = 0
            for h in range(H):
                for c in range(NC_):
                    C = h * NC_ + c
                    for j in range(4 * c + 4):
                        units.append((h, c, j, C, U, j - 4 * c))
                        U += 1
            NU = len(units)
            chunk_last = {}
            for (h, c, j, C, u, p) in units:
                chunk_last[C] = u

            def cs_after(u):
                return (u + 1) - ((u + 1) // 6)

            bep_evac = {}
            bep_astg = {}
            _bep = 0
            for idx, pu in enumerate(units):
                if pu[4] % 6 == 5:
                    _bep += 1
                    bep_evac[pu[4]] = _bep
                if idx >= LAG:
                    pu2 = units[idx - LAG]
                    if pu2[4] == chunk_last[pu2[3]]:
                        _bep += 1
                        bep_astg[pu2[3]] = _bep
            for pu2 in units[-LAG:]:
                if pu2[4] == chunk_last[pu2[3]]:
                    _bep += 1
                    bep_astg[pu2[3]] = _bep

            with nc.Block() as blk:
                @blk.gpsimd
                def _(gpsimd):
                    gpsimd.wait_ge(s_ph.h, PHC[0])

                    def xloads(t):
                        rs, cs2 = xlt(t)
                        o = (t % 2) * 512
                        gpsimd.dma_start(xstg[:, o:o + 512], x_in[rs, cs2]
                                         ).then_inc(s_xlld[t % 2].h, 16)
                        gpsimd.dma_start(xbstg[:, o:o + 512], xbf[rs, cs2]
                                         ).then_inc(s_xlld[t % 2].h, 16)
                        s_xlld[t % 2].n += 32

                    xloads(0)
                    xloads(1)
                    for t in range(128):
                        rs, cs2 = xlt(t)
                        o = (t % 2) * 512
                        gpsimd.wait_ge(s_xlld[t % 2].h, 32 * (t // 2 + 1))
                        if t >= 2:
                            gpsimd.wait_ge(s_xlsp[t % 2].h, 16 * (t // 2))
                        ii = gpsimd.tensor_sub(xlostg[:, o:o + 512],
                                               xstg[:, o:o + 512],
                                               xbstg[:, o:o + 512])
                        s_xl.inc(ii)
                        gpsimd.wait_ge(s_xl.h, s_xl.n)
                        gpsimd.dma_start(xlo[rs, cs2], xlostg[:, o:o + 512]
                                         ).then_inc(s_xlsp[t % 2].h, 16)
                        s_xlsp[t % 2].n += 16
                        if t + 2 < 128:
                            xloads(t + 2)

                @blk.sync
                def _(sync):
                    sync.wait_ge(s_ph.h, PHC[0])
                    sync.dma_start(masks[:], msk_in).then_inc(s_cst.h, 16)
                    s_cst.n += 16
                    CSTB[0] = s_cst.n

                    def loads(h):
                        if h >= 2:
                            sync.wait_ge(s_bsp[0].h, 64 * (h - 1))
                            sync.wait_ge(s_bsp[1].h, 64 * (h - 1))
                        o = (h % 2) * N
                        sync.dma_start(qsb[:, o:o + N], qT[h]).then_inc(s_bld[h % 2].h, 16)
                        s_bld[h % 2].n += 16
                        sync.dma_start(ksb[:, o:o + N], kTs[h]).then_inc(s_bld[h % 2].h, 16)
                        s_bld[h % 2].n += 16
                        src = vsc[:, :, h * 128:(h + 1) * 128].rearrange("j p d -> p j d")
                        dst = vsb[:, o:o + N].rearrange("p (j d) -> p j d", j=NTI)
                        sync.dma_start(dst, src).then_inc(s_bld[h % 2].h, 16)
                        s_bld[h % 2].n += 16

                    loads(0)
                    loads(1)
                    for h in range(H):
                        for c in range(NC_):
                            C = h * NC_ + c
                            sync.wait_ge(s_bep.h, bep_astg[C])
                            sync.dma_start(aT[h][:, _chunk(c)],
                                           astg[:, (C % 2) * 512:(C % 2 + 1) * 512]
                                           ).then_inc(s_bsp[C % 2].h, 16)
                            s_bsp[C % 2].n += 16
                        if h + 2 < H:
                            loads(h + 2)

                @blk.tensor
                def _(tensor):
                    tensor.wait_ge(s_ph.h, PHC[0])

                    def mm2den(pu):
                        (h, c, j, C, u, p) = pu
                        o = (h % 2) * N
                        if j == 0 and C >= 2:
                            tensor.wait_ge(s_bep.h, bep_astg[C - 2])
                        tensor.wait_ge(s_bsq2.h, u + 1)
                        tensor.matmul(p_o[C % 2][:],
                                      vsb[:, o + j * 128: o + (j + 1) * 128],
                                      a4b[:, (u % 4) * 512:(u % 4 + 1) * 512],
                                      start=(j == 0), stop=(j == 4 * c + 3))
                        ii = tensor.matmul(p_d[C % 2][:], ones128[:],
                                           a4b[:, (u % 4) * 512:(u % 4 + 1) * 512],
                                           start=(j == 0), stop=(j == 4 * c + 3))
                        s_bmm2.inc(ii)

                    for idx, pu in enumerate(units):
                        (h, c, j, C, u, p) = pu
                        o = (h % 2) * N
                        if j == 0:
                            tensor.wait_ge(s_bld[h % 2].h, 48 * (h // 2 + 1))
                        if u >= 4:
                            w = u - 4
                            if w % 6 == 5:
                                tensor.wait_ge(s_bep.h, bep_evac[w])
                            else:
                                tensor.wait_ge(s_bsq1.h, cs_after(w))
                        ii = tensor.matmul(p_s[u % 4][:],
                                           ksb[:, o + j * 128: o + (j + 1) * 128],
                                           qsb[:, o + c * 512: o + (c + 1) * 512],
                                           start=True, stop=True)
                        s_bmm1.inc(ii)
                        if idx >= LAG:
                            mm2den(units[idx - LAG])
                    for pu in units[-LAG:]:
                        mm2den(pu)

                @blk.scalar
                def _(scalar):
                    scalar.wait_ge(s_ph.h, PHC[0])
                    scalar.preload_act_table(AF.Square)
                    for pu in units:
                        (h, c, j, C, u, p) = pu
                        if u % 6 == 5:
                            continue
                        scalar.wait_ge(s_bmm1.h, u + 1)
                        if u >= 4:
                            scalar.wait_ge(s_bsq2.h, u - 3)
                        ii = scalar.activation(a2b[:, (u % 4) * 512:(u % 4 + 1) * 512],
                                               p_s[u % 4][:], AF.Square)
                        s_bsq1.inc(ii)

                @blk.vector
                def _(vector):
                    vector.wait_ge(s_ph.h, PHC[0])
                    vector.wait_ge(s_cst.h, CSTB[0])

                    def denom(pu):
                        (h, c, j, C, u, p) = pu
                        if u != chunk_last[C]:
                            return
                        vector.wait_ge(s_bmm2.h, u + 1)
                        if C >= 2:
                            vector.wait_ge(s_bsp[C % 2].h, 16 * (C // 2))
                        cs = slice((C % 2) * 512, (C % 2 + 1) * 512)
                        vector.reciprocal(rbB[:, cs], p_d[C % 2][:])
                        ii = vector.tensor_mul(astg[:, cs], p_o[C % 2][:],
                                               rbB[:, cs])
                        s_bep.inc(ii)

                    for idx, pu in enumerate(units):
                        (h, c, j, C, u, p) = pu
                        sl = slice((u % 4) * 512, (u % 4 + 1) * 512)
                        if u % 6 == 5:
                            vector.wait_ge(s_bmm1.h, u + 1)
                            vector.tensor_copy(a2b[:, sl], p_s[u % 4][:])
                            ii = vector.tensor_mul(a2b[:, sl], a2b[:, sl],
                                                   a2b[:, sl])
                            s_bep.inc(ii)
                        else:
                            vector.wait_ge(s_bsq1.h, cs_after(u))
                        if u >= 4:
                            vector.wait_ge(s_bmm2.h, u - 3)
                        ii = vector.tensor_mul(a4b[:, sl], a2b[:, sl], a2b[:, sl])
                        if p >= 0:
                            ii = vector.tensor_mul(a4b[:, sl], a4b[:, sl],
                                                   masks[:, p * 512:(p + 1) * 512])
                        s_bsq2.inc(ii)
                        if idx >= LAG:
                            denom(units[idx - LAG])
                    for pu in units[-LAG:]:
                        denom(pu)

            barrier_block(sync_waits=[(s_bsp[0], s_bsp[0].n), (s_bsp[1], s_bsp[1].n),
                                      (s_xlsp[0], s_xlsp[0].n),
                                      (s_xlsp[1], s_xlsp[1].n)])
            if upto == 'B':
                drain_block()
                return nc

        with contextlib.ExitStack() as pc:
            wout = pc.enter_context(nc.sbuf_tensor("wout_sb", [128, KE * E], BF16))
            wrsb = pc.enter_context(nc.sbuf_tensor("wr_sb", [128, KE * 128], F32))
            asb = pc.enter_context(nc.sbuf_tensor("asb", [128, KE * 512], BF16))
            x0sb = pc.enter_context(nc.sbuf_tensor("x0sb", [128, KE * 512], BF16))
            x0lo = pc.enter_context(nc.sbuf_tensor("x0lo", [128, KE * 512], BF16))
            ysb = pc.enter_context(nc.sbuf_tensor("ysb", [128, KE * 512], F32))
            ybf = pc.enter_context(nc.sbuf_tensor("ybf", [128, 2 * 512], BF16))
            x1f = pc.enter_context(nc.sbuf_tensor("x1f", [128, KE * 512], F32))
            sqst = pc.enter_context(nc.sbuf_tensor("sqst", [128, 2 * 512], BF16))
            mbc = pc.enter_context(nc.sbuf_tensor("mbc", [128, 2 * 512], BF16))
            uqc = pc.enter_context(nc.sbuf_tensor("uqc", [128, 2 * 512], F32))
            varc = pc.enter_context(nc.sbuf_tensor("varc", [128, 2 * 512], F32))
            rbc = pc.enter_context(nc.sbuf_tensor("rbc", [128, 2 * 512], F32))
            tbc = pc.enter_context(nc.sbuf_tensor("tbc", [128, 512], F32))
            msbst = pc.enter_context(nc.sbuf_tensor("msbst", [128, 2 * 512], BF16))
            p_w = [pc.enter_context(nc.psum_tensor("pw%d" % i, [128, 512], F32))
                   for i in range(2)]
            p_s1 = pc.enter_context(nc.psum_tensor("ps1c", [128, 512], F32))
            p_s2 = pc.enter_context(nc.psum_tensor("ps2c", [128, 512], F32))
            p_r = pc.enter_context(nc.psum_tensor("prc", [128, 512], F32))

            def cbuf(buf, c, m):
                return buf[:, m * 512:(m + 1) * 512]

            with nc.Block() as blk:
                @blk.gpsimd
                def _(gpsimd):
                    gpsimd.wait_ge(s_ph.h, PHC[0])
                    for k in range(KE):
                        gpsimd.dma_start(wout[:, k * E:(k + 1) * E],
                                         wout_bf[k * 128:(k + 1) * 128, :]
                                         ).then_inc(s_cw.h, 16)
                        s_cw.n += 16
                        gpsimd.dma_start(wrsb[:, k * 128:(k + 1) * 128],
                                         wr_in[k * 128:(k + 1) * 128, :]
                                         ).then_inc(s_cw.h, 16)
                        s_cw.n += 16
                    for c in range(NC_):
                        for m in range(KE):
                            cm = c * KE + m
                            gpsimd.wait_ge(s_cx1.h, cm + 1)
                            gpsimd.dma_start(x1sc[m][:, _chunk(c)],
                                             x1f[:, m * 512:(m + 1) * 512]
                                             ).then_inc(s_cx1sp.h, 16)
                            s_cx1sp.n += 16

                @blk.sync
                def _(sync):
                    sync.wait_ge(s_ph.h, PHC[0])

                    def loads(c):
                        if c >= 1:
                            sync.wait_ge(s_cmm.h, 16 * c)
                        for m in range(KE):
                            sync.dma_start(cbuf(asb, c, m), aT[m][:, _chunk(c)]
                                           ).then_inc(s_cld[0].h, 16)
                            s_cld[0].n += 16

                    def x0loads(c):
                        if c >= 1:
                            sync.wait_ge(s_cev.h, 16 * c)
                        for m in range(KE):
                            sync.dma_start(
                                x0sb[:, m * 512:(m + 1) * 512],
                                xbf[_chunk(c), m * 128:(m + 1) * 128],
                                transpose=True).then_inc(s_cx0.h, 16)
                            sync.dma_start(
                                x0lo[:, m * 512:(m + 1) * 512],
                                xlo[_chunk(c), m * 128:(m + 1) * 128],
                                transpose=True).then_inc(s_cx0.h, 16)
                            s_cx0.n += 32

                    loads(0)
                    x0loads(0)
                    for c in range(NC_):
                        if c + 1 < NC_:
                            loads(c + 1)
                            x0loads(c + 1)
                        sync.wait_ge(s_cms.h, c + 1)
                        sync.dma_start(msc[c], msbst[:, (c % 2) * 512:(c % 2 + 1) * 512]
                                       ).then_inc(s_cmsp[c % 2].h, 16)
                        s_cmsp[c % 2].n += 16

                @blk.tensor
                def _(tensor):
                    tensor.wait_ge(s_ph.h, PHC[0])
                    tensor.wait_ge(s_cw.h, 512)

                    def stats(c, m):
                        cm = c * KE + m
                        tensor.wait_ge(s_cev.h, cm + 1)
                        tensor.wait_ge(s_csq.h, cm + 1)
                        if m == 0:
                            tensor.wait_ge(s_csf.h, c)
                        tensor.matmul(p_s1[:], ones128[:],
                                      ybf[:, (cm % 2) * 512:(cm % 2 + 1) * 512],
                                      start=(m == 0), stop=(m == KE - 1))
                        ii = tensor.matmul(p_s2[:], ones128[:],
                                           sqst[:, (cm % 2) * 512:(cm % 2 + 1) * 512],
                                           start=(m == 0), stop=(m == KE - 1))
                        s_cst2.inc(ii)

                    for c in range(NC_):
                        tensor.wait_ge(s_cld[0].h, 256 * (c + 1))
                        for m in range(KE):
                            cm = c * KE + m
                            if cm >= 2:
                                tensor.wait_ge(s_cev.h, cm - 1)
                            ii = None
                            for k in range(KE):
                                ii = tensor.matmul(
                                    p_w[cm % 2][:],
                                    wout[:, k * E + m * 128: k * E + (m + 1) * 128],
                                    cbuf(asb, c, k),
                                    start=(k == 0), stop=(k == KE - 1))
                            s_cmm.inc(ii)
                            if m >= 1:
                                stats(c, m - 1)
                        stats(c, KE - 1)
                        tensor.wait_ge(s_cx1.h, KE * (c + 1))
                        if c >= 1:
                            tensor.wait_ge(s_cms.h, c)
                        ii = None
                        for k in range(KE):
                            ii = tensor.matmul(p_r[:],
                                               wrsb[:, k * 128:(k + 1) * 128],
                                               x1f[:, k * 512:(k + 1) * 512],
                                               start=(k == 0), stop=(k == KE - 1))
                        s_crt.inc(ii)

                @blk.vector
                def _(vector):
                    vector.wait_ge(s_ph.h, PHC[0])
                    for c in range(NC_):
                        vector.wait_ge(s_cx0.h, 512 * (c + 1))
                        for m in range(KE):
                            cm = c * KE + m
                            vector.wait_ge(s_cmm.h, cm + 1)
                            if cm >= 2:
                                vector.wait_ge(s_csq.h, cm - 1)
                                vector.wait_ge(s_cst2.h, cm - 1)
                            vector.tensor_add(ysb[:, m * 512:(m + 1) * 512],
                                              p_w[cm % 2][:],
                                              x0sb[:, m * 512:(m + 1) * 512])
                            vector.tensor_add(ysb[:, m * 512:(m + 1) * 512],
                                              ysb[:, m * 512:(m + 1) * 512],
                                              x0lo[:, m * 512:(m + 1) * 512])
                            ii = vector.tensor_copy(ybf[:, (cm % 2) * 512:(cm % 2 + 1) * 512],
                                                    ysb[:, m * 512:(m + 1) * 512])
                            s_cev.inc(ii)
                        vector.wait_ge(s_csf.h, c + 1)
                        if c >= 2:
                            vector.wait_ge(s_crb.h, c - 1)
                        cs = slice((c % 2) * 512, (c % 2 + 1) * 512)
                        vector.tensor_mul(tbc[:], mbc[:, cs], mbc[:, cs])
                        ii = vector.tensor_sub(varc[:, cs], uqc[:, cs], tbc[:])
                        s_cvar.inc(ii)
                        vector.wait_ge(s_crb.h, c + 1)
                        if c >= 1:
                            vector.wait_ge(s_crt.h, c)
                            vector.wait_ge(s_cx1sp.h, 256 * c)
                        for m in range(KE):
                            cm = c * KE + m
                            vector.tensor_sub(tbc[:], ysb[:, m * 512:(m + 1) * 512],
                                              mbc[:, cs])
                            ii = vector.tensor_mul(x1f[:, m * 512:(m + 1) * 512],
                                                   tbc[:], rbc[:, cs])
                            s_cx1.inc(ii)
                        vector.wait_ge(s_crt.h, c + 1)
                        if c >= 2:
                            vector.wait_ge(s_cmsp[c % 2].h, 16 * (c // 2))
                        ii = vector.tensor_scalar(msbst[:, cs], p_r[:],
                                                  brsb[:, 0:1], 0.0,
                                                  ALU.add, ALU.is_gt)
                        s_cms.inc(ii)

                @blk.scalar
                def _(scalar):
                    scalar.wait_ge(s_ph.h, PHC[0])
                    preload_rsqrt(scalar)
                    for c in range(NC_):
                        cs = slice((c % 2) * 512, (c % 2 + 1) * 512)
                        for m in range(KE):
                            cm = c * KE + m
                            scalar.wait_ge(s_cev.h, cm + 1)
                            if cm >= 2:
                                scalar.wait_ge(s_cst2.h, cm - 1)
                            ii = scalar.activation(
                                sqst[:, (cm % 2) * 512:(cm % 2 + 1) * 512],
                                ybf[:, (cm % 2) * 512:(cm % 2 + 1) * 512], AF.Square)
                            s_csq.inc(ii)
                        scalar.wait_ge(s_cst2.h, 16 * (c + 1))
                        if c >= 2:
                            scalar.wait_ge(s_cvar.h, c - 1)
                            scalar.wait_ge(s_cx1.h, 16 * (c - 1))
                        scalar.activation(mbc[:, cs], p_s1[:], AF.Identity,
                                          scale=1.0 / E)
                        ii = scalar.activation(uqc[:, cs], p_s2[:], AF.Identity,
                                               scale=1.0 / E)
                        s_csf.inc(ii)
                        scalar.wait_ge(s_cvar.h, c + 1)
                        ii = rsqrt_act(scalar, rbc[:, cs], varc[:, cs], LN_EPS)
                        s_crb.inc(ii)

            barrier_block(sync_waits=[(s_cx1sp, s_cx1sp.n), (s_cmsp[0], s_cmsp[0].n), (s_cmsp[1], s_cmsp[1].n)])
            if upto == 'C':
                drain_block()
                return nc

        with contextlib.ExitStack() as pd1:
            wvsb = pd1.enter_context(nc.sbuf_tensor("wv_sb", [128, KE * E], BF16))
            x1d = pd1.enter_context(nc.sbuf_tensor("x1d", [128, 2 * KE * 512], BF16))
            xqb = pd1.enter_context(nc.sbuf_tensor("xqb", [128, 2 * 512], BF16))
            h1sb = pd1.enter_context(nc.sbuf_tensor("h1sb", [128, 2 * KE * 512], BF16))
            ust = pd1.enter_context(nc.sbuf_tensor("ustd1", [128, 512], F32))
            rb1 = pd1.enter_context(nc.sbuf_tensor("rb1", [128, 2 * 512], BF16))
            vstd = pd1.enter_context(nc.sbuf_tensor("vstd", [128, 2 * 512], BF16))
            td1 = pd1.enter_context(nc.sbuf_tensor("td1", [128, 512], BF16))
            p_ss = pd1.enter_context(nc.psum_tensor("pssd1", [128, 512], F32))
            p_v1 = [pd1.enter_context(nc.psum_tensor("pv1%d" % i, [128, 512], F32))
                    for i in range(2)]

            def dbuf(buf, c, m):
                o = ((c % 2) * KE + m) * 512
                return buf[:, o:o + 512]

            with nc.Block() as blk:
                @blk.gpsimd
                def _(gpsimd):
                    gpsimd.wait_ge(s_ph.h, PHC[0])
                    for k in range(KE):
                        gpsimd.dma_start(wvsb[:, k * E:(k + 1) * E],
                                         wvf_bf[k * 128:(k + 1) * 128, :]
                                         ).then_inc(s_d1w.h, 16)
                        s_d1w.n += 16

                @blk.sync
                def _(sync):
                    sync.wait_ge(s_ph.h, PHC[0])
                    for c in range(NC_):
                        if c >= 2:
                            sync.wait_ge(s_d1h1.h, 16 * (c - 1))
                        for m in range(KE):
                            sync.dma_start(dbuf(x1d, c, m), x1sc[m][:, _chunk(c)]
                                           ).then_inc(s_d1ld[c % 2].h, 16)
                            s_d1ld[c % 2].n += 16

                @blk.vector
                def _(vector):
                    vector.wait_ge(s_ph.h, PHC[0])
                    for c in range(NC_):
                        vector.wait_ge(s_d1ld[c % 2].h, 256 * (c // 2 + 1))
                        for k in range(KE):
                            ck = c * KE + k
                            if ck >= 2:
                                vector.wait_ge(s_d1ss.h, ck - 1)
                            ii = vector.tensor_mul(
                                xqb[:, (ck % 2) * 512:(ck % 2 + 1) * 512],
                                dbuf(x1d, c, k), dbuf(x1d, c, k))
                            s_d1xq.inc(ii)
                        vector.wait_ge(s_d1rb.h, c + 1)
                        if c >= 2:
                            vector.wait_ge(s_d1mm.h, 16 * (c - 1))
                        for k in range(KE):
                            vector.tensor_mul(td1[:], dbuf(x1d, c, k),
                                              rb1[:, (c % 2) * 512:(c % 2 + 1) * 512])
                            ii = vector.tensor_scalar(dbuf(h1sb, c, k), td1[:],
                                                      w1sb[:, k:k + 1], None, ALU.mult)
                            s_d1h1.inc(ii)

                @blk.tensor
                def _(tensor):
                    tensor.wait_ge(s_ph.h, PHC[0])
                    tensor.wait_ge(s_d1w.h, 256)
                    for c in range(NC_):
                        for k in range(KE):
                            ck = c * KE + k
                            tensor.wait_ge(s_d1xq.h, ck + 1)
                            if c >= 1 and k == 0:
                                tensor.wait_ge(s_d1rb.h, c)
                            ii = tensor.matmul(
                                p_ss[:], ones128[:],
                                xqb[:, (ck % 2) * 512:(ck % 2 + 1) * 512],
                                start=(k == 0), stop=(k == KE - 1))
                            s_d1ss.inc(ii)
                        tensor.wait_ge(s_d1h1.h, 16 * (c + 1))
                        for m in range(KE):
                            cm = c * KE + m
                            if cm >= 2:
                                tensor.wait_ge(s_d1ev.h, cm - 1)
                            ii = None
                            for k in range(KE):
                                ii = tensor.matmul(
                                    p_v1[cm % 2][:],
                                    wvsb[:, k * E + m * 128: k * E + (m + 1) * 128],
                                    dbuf(h1sb, c, k),
                                    start=(k == 0), stop=(k == KE - 1))
                            s_d1mm.inc(ii)

                @blk.scalar
                def _(scalar):
                    scalar.wait_ge(s_ph.h, PHC[0])
                    preload_rsqrt(scalar)
                    for c in range(NC_):
                        scalar.wait_ge(s_d1ss.h, 16 * (c + 1))
                        scalar.activation(ust[:], p_ss[:], AF.Identity, scale=1.0 / E)
                        ii = rsqrt_act(scalar,
                                       rb1[:, (c % 2) * 512:(c % 2 + 1) * 512],
                                       ust[:], RMS_EPS)
                        s_d1rb.inc(ii)
                        for m in range(KE):
                            cm = c * KE + m
                            scalar.wait_ge(s_d1mm.h, cm + 1)
                            if cm >= 2:
                                scalar.wait_ge(s_d1sp[cm % 2].h, 16 * (cm // 2))
                            ii = scalar.activation(
                                vstd[:, (cm % 2) * 512:(cm % 2 + 1) * 512],
                                p_v1[cm % 2][:], AF.Identity, bias=bvsb[:, m:m + 1])
                            s_d1ev.inc(ii)
                            scalar.dma_start(vfsc[m][:, _chunk(c)],
                                             vstd[:, (cm % 2) * 512:(cm % 2 + 1) * 512]
                                             ).then_inc(s_d1sp[cm % 2].h, 16)
                            s_d1sp[cm % 2].n += 16

            barrier_block(scalar_waits=[(s_d1sp[0], s_d1sp[0].n), (s_d1sp[1], s_d1sp[1].n)])
            if upto == 'D1':
                drain_block()
                return nc

        with contextlib.ExitStack() as pd2:
            wosb = pd2.enter_context(nc.sbuf_tensor("wo_sb", [128, KE * E], BF16))
            x1d2 = pd2.enter_context(nc.sbuf_tensor("x1d2", [128, 2 * KE * 512], BF16))
            vd = pd2.enter_context(nc.sbuf_tensor("vd", [128, 2 * KE * 512], BF16))
            hsb = pd2.enter_context(nc.sbuf_tensor("hsb2", [128, KE * 512], BF16))
            hsqst = pd2.enter_context(nc.sbuf_tensor("hsqst", [128, 2 * 512], BF16))
            ust2 = pd2.enter_context(nc.sbuf_tensor("ustd2", [128, 512], F32))
            rb2 = pd2.enter_context(nc.sbuf_tensor("rb2", [128, 2 * 512], BF16))
            h2st = pd2.enter_context(nc.sbuf_tensor("h2st", [128, 2 * 512], BF16))
            td2 = pd2.enter_context(nc.sbuf_tensor("td2", [128, 512], BF16))
            p_w2 = [pd2.enter_context(nc.psum_tensor("pw2%d" % i, [128, 512], F32))
                    for i in range(2)]
            p_ss2 = pd2.enter_context(nc.psum_tensor("pss2", [128, 512], F32))

            def dbuf2(buf, c, m):
                o = ((c % 2) * KE + m) * 512
                return buf[:, o:o + 512]

            with nc.Block() as blk:
                @blk.gpsimd
                def _(gpsimd):
                    gpsimd.wait_ge(s_ph.h, PHC[0])
                    for k in range(KE):
                        gpsimd.dma_start(wosb[:, k * E:(k + 1) * E],
                                         wo_bf[k * 128:(k + 1) * 128, :]
                                         ).then_inc(s_d2w.h, 16)
                        s_d2w.n += 16

                @blk.sync
                def _(sync):
                    sync.wait_ge(s_ph.h, PHC[0])

                    def loads(c):
                        if c >= 2:
                            sync.wait_ge(s_d2mm.h, 16 * (c - 1))
                            sync.wait_ge(s_d2h.h, 16 * (c - 1))
                        for m in range(KE):
                            sync.dma_start(dbuf2(x1d2, c, m), x1sc[m][:, _chunk(c)]
                                           ).then_inc(s_d2ld[c % 2].h, 16)
                            s_d2ld[c % 2].n += 16
                            sync.dma_start(dbuf2(vd, c, m), vfsc[m][:, _chunk(c)]
                                           ).then_inc(s_d2ld[c % 2].h, 16)
                            s_d2ld[c % 2].n += 16

                    loads(0)
                    loads(1)
                    for c in range(NC_):
                        for m in range(KE):
                            cm = c * KE + m
                            sync.wait_ge(s_d2h.h, cm + 1)
                            sync.dma_start(hsc[m][:, _chunk(c)],
                                           hsb[:, m * 512:(m + 1) * 512]
                                           ).then_inc(s_d2hsp.h, 16)
                            s_d2hsp.n += 16
                        if c + 2 < NC_:
                            loads(c + 2)

                @blk.tensor
                def _(tensor):
                    tensor.wait_ge(s_ph.h, PHC[0])
                    tensor.wait_ge(s_d2w.h, 256)

                    def stats(c, m):
                        cm = c * KE + m
                        tensor.wait_ge(s_d2sq.h, cm + 1)
                        if m == 0:
                            tensor.wait_ge(s_d2rb.h, c)
                        ii = tensor.matmul(p_ss2[:], ones128[:],
                                           hsqst[:, (cm % 2) * 512:(cm % 2 + 1) * 512],
                                           start=(m == 0), stop=(m == KE - 1))
                        s_d2st.inc(ii)

                    for c in range(NC_):
                        tensor.wait_ge(s_d2ld[c % 2].h, 512 * (c // 2 + 1))
                        for m in range(KE):
                            cm = c * KE + m
                            if cm >= 2:
                                tensor.wait_ge(s_d2h.h, cm - 1)
                            ii = None
                            for k in range(KE):
                                ii = tensor.matmul(
                                    p_w2[cm % 2][:],
                                    wosb[:, k * E + m * 128: k * E + (m + 1) * 128],
                                    dbuf2(vd, c, k),
                                    start=(k == 0), stop=(k == KE - 1))
                            s_d2mm.inc(ii)
                            if m >= 1:
                                stats(c, m - 1)
                        stats(c, KE - 1)

                @blk.vector
                def _(vector):
                    vector.wait_ge(s_ph.h, PHC[0])
                    for c in range(NC_):
                        for m in range(KE):
                            cm = c * KE + m
                            vector.wait_ge(s_d2mm.h, cm + 1)
                            if m == 0:
                                vector.wait_ge(s_d2sq.h, 16 * c)
                                vector.wait_ge(s_d2hsp.h, 256 * c)
                            ii = vector.tensor_add(hsb[:, m * 512:(m + 1) * 512],
                                                   p_w2[cm % 2][:], dbuf2(x1d2, c, m))
                            s_d2h.inc(ii)
                        vector.wait_ge(s_d2rb.h, c + 1)
                        for m in range(KE):
                            cm = c * KE + m
                            if cm >= 2:
                                vector.wait_ge(s_d2sp[cm % 2].h, 16 * (cm // 2))
                            vector.tensor_mul(td2[:], hsb[:, m * 512:(m + 1) * 512],
                                              rb2[:, (c % 2) * 512:(c % 2 + 1) * 512])
                            ii = vector.tensor_scalar(
                                h2st[:, (cm % 2) * 512:(cm % 2 + 1) * 512],
                                td2[:], w2sb[:, m:m + 1], None, ALU.mult)
                            s_d2h2.inc(ii)

                @blk.scalar
                def _(scalar):
                    scalar.wait_ge(s_ph.h, PHC[0])
                    preload_rsqrt(scalar)
                    for c in range(NC_):
                        for m in range(KE):
                            cm = c * KE + m
                            scalar.wait_ge(s_d2h.h, cm + 1)
                            if cm >= 2:
                                scalar.wait_ge(s_d2st.h, cm - 1)
                            ii = scalar.activation(
                                hsqst[:, (cm % 2) * 512:(cm % 2 + 1) * 512],
                                hsb[:, m * 512:(m + 1) * 512], AF.Square)
                            s_d2sq.inc(ii)
                        scalar.wait_ge(s_d2st.h, 16 * (c + 1))
                        if c >= 2:
                            scalar.wait_ge(s_d2h2.h, 16 * (c - 1))
                        scalar.activation(ust2[:], p_ss2[:], AF.Identity, scale=1.0 / E)
                        ii = rsqrt_act(scalar,
                                       rb2[:, (c % 2) * 512:(c % 2 + 1) * 512],
                                       ust2[:], RMS_EPS)
                        s_d2rb.inc(ii)
                        for m in range(KE):
                            cm = c * KE + m
                            scalar.wait_ge(s_d2h2.h, cm + 1)
                            scalar.dma_start(h2sc[m][:, _chunk(c)],
                                             h2st[:, (cm % 2) * 512:(cm % 2 + 1) * 512]
                                             ).then_inc(s_d2sp[cm % 2].h, 16)
                            s_d2sp[cm % 2].n += 16

            barrier_block(sync_waits=[(s_d2hsp, s_d2hsp.n)],
                          scalar_waits=[(s_d2sp[0], s_d2sp[0].n),
                                        (s_d2sp[1], s_d2sp[1].n)])
            if upto == 'D2':
                drain_block()
                return nc

        with contextlib.ExitStack() as pd3:
            x1d3 = pd3.enter_context(nc.sbuf_tensor("x1d3", [128, KE * 512], BF16))
            hd3 = pd3.enter_context(nc.sbuf_tensor("hd3", [128, KE * 512], BF16))
            h2d = pd3.enter_context(nc.sbuf_tensor("h2d", [128, 2 * KE * 512], BF16))
            msbuf = pd3.enter_context(nc.sbuf_tensor("msbuf", [128, 512], BF16))
            wgf = pd3.enter_context(nc.sbuf_tensor("wgf", [128, 2 * KE * 128], BF16))
            wuf = pd3.enter_context(nc.sbuf_tensor("wuf", [128, 2 * KE * 128], BF16))
            wdm = pd3.enter_context(nc.sbuf_tensor("wdm", [128, 2 * NF * 128], BF16))
            gbuf = pd3.enter_context(nc.sbuf_tensor("gbuf", [128, NF * 512], BF16))
            sgst = pd3.enter_context(nc.sbuf_tensor("sgst", [128, 2 * 512], BF16))
            s1st = pd3.enter_context(nc.sbuf_tensor("s1st", [128, 512], BF16))
            s2st = pd3.enter_context(nc.sbuf_tensor("s2st", [128, 512], F32))
            s3st = pd3.enter_context(nc.sbuf_tensor("s3st", [128, 512], BF16))
            zsb = pd3.enter_context(nc.sbuf_tensor("zsb", [128, KE * 512], BF16))
            zqst = pd3.enter_context(nc.sbuf_tensor("zqst", [128, 2 * 512], BF16))
            mbz = pd3.enter_context(nc.sbuf_tensor("mbz", [128, 512], BF16))
            uqz = pd3.enter_context(nc.sbuf_tensor("uqz", [128, 512], BF16))
            varz = pd3.enter_context(nc.sbuf_tensor("varz", [128, 512], BF16))
            rbz = pd3.enter_context(nc.sbuf_tensor("rbz", [128, 512], BF16))
            tz = pd3.enter_context(nc.sbuf_tensor("tz", [128, 512], BF16))
            x2st = pd3.enter_context(nc.sbuf_tensor("x2st", [128, 2 * 512], BF16))
            ost = pd3.enter_context(nc.sbuf_tensor("ost", [128, 2 * 512], F32))
            p_g = pd3.enter_context(nc.psum_tensor("pg", [128, 512], F32))
            p_u = pd3.enter_context(nc.psum_tensor("pu", [128, 512], F32))
            p_mlp = [pd3.enter_context(nc.psum_tensor("pmlp%d" % i, [128, 512], F32))
                     for i in range(2)]
            p_z = pd3.enter_context(nc.psum_tensor("pz", [128, 512], F32))
            p_z2 = pd3.enter_context(nc.psum_tensor("pz2", [128, 512], F32))
            p_tr = [pd3.enter_context(nc.psum_tensor("ptr%d" % i, [128, 512], BF16))
                    for i in range(2)]

            with nc.Block() as blk:
                @blk.sync
                def _(sync):
                    sync.wait_ge(s_ph.h, PHC[0])
                    sync.wait_ge(s_wcast.h, s_wcast.n)

                    def wloads(c):
                        if c >= 2:
                            sync.wait_ge(s_d3mm.h, NF * (c - 1))
                        for m in range(KE):
                            sync.dma_start(
                                h2d[:, ((c % 2) * KE + m) * 512:((c % 2) * KE + m + 1) * 512],
                                h2sc[m][:, _chunk(c)]).then_inc(s_d3h2ld[c % 2].h, 16)
                            s_d3h2ld[c % 2].n += 16

                    def xloads(c):
                        if c >= 1:
                            sync.wait_ge(s_d3z.h, KE * c)
                        for m in range(KE):
                            sync.dma_start(x1d3[:, m * 512:(m + 1) * 512],
                                           x1sc[m][:, _chunk(c)]).then_inc(s_d3xld.h, 16)
                            s_d3xld.n += 16
                            sync.dma_start(hd3[:, m * 512:(m + 1) * 512],
                                           hsc[m][:, _chunk(c)]).then_inc(s_d3xld.h, 16)
                            s_d3xld.n += 16
                        sync.dma_start(msbuf[:], msc[c]).then_inc(s_d3xld.h, 16)
                        s_d3xld.n += 16

                    wloads(0)
                    xloads(0)
                    wloads(1)
                    for c in range(NC_):
                        for f in range(NF):
                            F_ = c * NF + f
                            if F_ >= 2:
                                sync.wait_ge(s_d3mm.h, F_ - 1)
                            o = (F_ % 2) * KE * 128
                            sync.dma_start(
                                wgf[:, o:o + KE * 128].rearrange("p (k c) -> p k c", k=KE),
                                gate_bf[f].rearrange("(k p) c -> p k c", p=128)
                            ).then_inc(s_d3wf.h, 16)
                            sync.dma_start(
                                wuf[:, o:o + KE * 128].rearrange("p (k c) -> p k c", k=KE),
                                up_bf[f].rearrange("(k p) c -> p k c", p=128)
                            ).then_inc(s_d3wf.h, 16)
                            s_d3wf.n += 32
                        for m in range(KE):
                            cm = c * KE + m
                            if cm >= 2:
                                sync.wait_ge(s_d3dn.h, cm - 1)
                            o = (cm % 2) * NF * 128
                            sync.dma_start(
                                wdm[:, o:o + NF * 128].rearrange("p (f c) -> p f c", f=NF),
                                down_bf[m].rearrange("(f p) c -> p f c", p=128)
                            ).then_inc(s_d3wd.h, 16)
                            s_d3wd.n += 16
                        if c + 1 < NC_:
                            xloads(c + 1)
                        if c + 2 < NC_:
                            wloads(c + 2)

                @blk.tensor
                def _(tensor):
                    tensor.wait_ge(s_ph.h, PHC[0])
                    for c in range(NC_):
                        tensor.wait_ge(s_d3h2ld[c % 2].h, 256 * (c // 2 + 1))
                        for f in range(NF):
                            F_ = c * NF + f
                            tensor.wait_ge(s_d3wf.h, 32 * (F_ + 1))
                            if F_ >= 1:
                                tensor.wait_ge(s_d3sg.h, F_)
                                tensor.wait_ge(s_d3gf.h, F_)
                            for k in range(KE):
                                tensor.matmul(
                                    p_g[:],
                                    wgf[:, (F_ % 2) * KE * 128 + k * 128:
                                        (F_ % 2) * KE * 128 + (k + 1) * 128],
                                    h2d[:, ((c % 2) * KE + k) * 512:
                                        ((c % 2) * KE + k + 1) * 512],
                                    start=(k == 0), stop=(k == KE - 1))
                            ii = None
                            for k in range(KE):
                                ii = tensor.matmul(
                                    p_u[:],
                                    wuf[:, (F_ % 2) * KE * 128 + k * 128:
                                        (F_ % 2) * KE * 128 + (k + 1) * 128],
                                    h2d[:, ((c % 2) * KE + k) * 512:
                                        ((c % 2) * KE + k + 1) * 512],
                                    start=(k == 0), stop=(k == KE - 1))
                            s_d3mm.inc(ii)

                        def stats(c, m):
                            cm = c * KE + m
                            tensor.wait_ge(s_d3z.h, cm + 1)
                            tensor.wait_ge(s_d3zq.h, cm + 1)
                            if m == 0:
                                tensor.wait_ge(s_d3sf.h, c)
                            tensor.matmul(p_z[:], ones128[:],
                                          zsb[:, m * 512:(m + 1) * 512],
                                          start=(m == 0), stop=(m == KE - 1))
                            ii = tensor.matmul(p_z2[:], ones128[:],
                                               zqst[:, (cm % 2) * 512:(cm % 2 + 1) * 512],
                                               start=(m == 0), stop=(m == KE - 1))
                            s_d3st.inc(ii)

                        tensor.wait_ge(s_d3gf.h, NF * (c + 1))
                        for m in range(KE):
                            cm = c * KE + m
                            tensor.wait_ge(s_d3wd.h, 16 * (cm + 1))
                            if cm >= 2:
                                tensor.wait_ge(s_d3z.h, cm - 1)
                            ii = None
                            for f in range(NF):
                                ii = tensor.matmul(
                                    p_mlp[cm % 2][:],
                                    wdm[:, (cm % 2) * NF * 128 + f * 128:
                                        (cm % 2) * NF * 128 + (f + 1) * 128],
                                    gbuf[:, f * 512:(f + 1) * 512],
                                    start=(f == 0), stop=(f == NF - 1))
                            s_d3dn.inc(ii)
                            if m >= 1:
                                stats(c, m - 1)
                        stats(c, KE - 1)
                        for m in range(KE):
                            cm = c * KE + m
                            tensor.wait_ge(s_d3x2.h, cm + 1)
                            if cm >= 2:
                                tensor.wait_ge(s_d3oev.h, cm - 1)
                            ii = None
                            for t in range(4):
                                ii = tensor.transpose(
                                    p_tr[cm % 2][:, t * 128:(t + 1) * 128],
                                    x2st[:, (cm % 2) * 512 + t * 128:
                                         (cm % 2) * 512 + (t + 1) * 128],
                                    ident[:])
                            s_d3tr.inc(ii)

                @blk.scalar
                def _(scalar):
                    scalar.wait_ge(s_ph.h, PHC[0])
                    for c in range(NC_):
                        scalar.preload_act_table(AF.Silu)
                        for f in range(NF):
                            F_ = c * NF + f
                            scalar.wait_ge(s_d3mm.h, F_ + 1)
                            if F_ >= 2:
                                scalar.wait_ge(s_d3gf.h, F_ - 1)
                            ii = scalar.activation(
                                sgst[:, (F_ % 2) * 512:(F_ % 2 + 1) * 512],
                                p_g[:], AF.Silu)
                            s_d3sg.inc(ii)
                        for m in range(KE):
                            cm = c * KE + m
                            scalar.wait_ge(s_d3z.h, cm + 1)
                            if cm >= 2:
                                scalar.wait_ge(s_d3st.h, cm - 1)
                            ii = scalar.activation(
                                zqst[:, (cm % 2) * 512:(cm % 2 + 1) * 512],
                                zsb[:, m * 512:(m + 1) * 512], AF.Square)
                            s_d3zq.inc(ii)
                        scalar.wait_ge(s_d3st.h, 16 * (c + 1))
                        scalar.activation(mbz[:], p_z[:], AF.Identity, scale=1.0 / E)
                        ii = scalar.activation(uqz[:], p_z2[:], AF.Identity,
                                               scale=1.0 / E)
                        s_d3sf.inc(ii)
                        preload_rsqrt(scalar)
                        scalar.wait_ge(s_d3var.h, c + 1)
                        ii = rsqrt_act(scalar, rbz[:], varz[:], LN_EPS)
                        s_d3rb.inc(ii)
                        for m in range(KE):
                            cm = c * KE + m
                            scalar.wait_ge(s_d3tr.h, cm + 1)
                            if cm >= 2:
                                scalar.wait_ge(s_out[cm % 2].h, 16 * (cm // 2))
                            ii = scalar.activation(
                                ost[:, (cm % 2) * 512:(cm % 2 + 1) * 512],
                                p_tr[cm % 2][:], AF.Identity)
                            s_d3oev.inc(ii)
                            dst = out_t[_chunk(c), m * 128:(m + 1) * 128].rearrange(
                                "(t p) e -> p t e", p=128)
                            src = ost[:, (cm % 2) * 512:(cm % 2 + 1) * 512].rearrange(
                                "p (t e) -> p t e", t=4)
                            scalar.dma_start(dst, src).then_inc(s_out[cm % 2].h, 16)
                            s_out[cm % 2].n += 16

                @blk.vector
                def _(vector):
                    vector.wait_ge(s_ph.h, PHC[0])
                    for c in range(NC_):
                        for f in range(NF):
                            F_ = c * NF + f
                            vector.wait_ge(s_d3sg.h, F_ + 1)
                            if f == 0 and c >= 1:
                                vector.wait_ge(s_d3dn.h, KE * c)
                            ii = vector.tensor_mul(gbuf[:, f * 512:(f + 1) * 512],
                                                   sgst[:, (F_ % 2) * 512:(F_ % 2 + 1) * 512],
                                                   p_u[:])
                            s_d3gf.inc(ii)
                        vector.wait_ge(s_d3xld.h, (2 * KE + 1) * 16 * (c + 1))
                        for m in range(KE):
                            cm = c * KE + m
                            vector.wait_ge(s_d3dn.h, cm + 1)
                            vector.tensor_sub(s1st[:], hd3[:, m * 512:(m + 1) * 512],
                                              x1d3[:, m * 512:(m + 1) * 512])
                            vector.tensor_add(s2st[:], p_mlp[cm % 2][:], s1st[:])
                            vector.tensor_mul(s3st[:], msbuf[:], s2st[:])
                            ii = vector.scalar_tensor_tensor(
                                zsb[:, m * 512:(m + 1) * 512],
                                x1d3[:, m * 512:(m + 1) * 512], 2.0, s3st[:],
                                ALU.mult, ALU.add)
                            s_d3z.inc(ii)
                        vector.wait_ge(s_d3sf.h, c + 1)
                        vector.tensor_mul(tz[:], mbz[:], mbz[:])
                        ii = vector.tensor_sub(varz[:], uqz[:], tz[:])
                        s_d3var.inc(ii)
                        vector.wait_ge(s_d3rb.h, c + 1)
                        for m in range(KE):
                            cm = c * KE + m
                            if cm >= 2:
                                vector.wait_ge(s_d3tr.h, cm - 1)
                            vector.tensor_sub(tz[:], zsb[:, m * 512:(m + 1) * 512],
                                              mbz[:])
                            ii = vector.tensor_mul(
                                x2st[:, (cm % 2) * 512:(cm % 2 + 1) * 512],
                                tz[:], rbz[:])
                            s_d3x2.inc(ii)

            drain_block()

    return nc


def _host_consts():
    pos = np.arange(N, dtype=np.float64)
    inv = 1.0 / (10000.0 ** (np.arange(0, 128, 2, dtype=np.float64) / 128.0))
    ang = pos[:, None] * inv[None, :]
    c = np.cos(ang).astype(np.float32)
    s = np.sin(ang).astype(np.float32)
    c2 = np.ascontiguousarray(np.concatenate([c, c], axis=1).T)
    s2 = np.ascontiguousarray(np.concatenate([-s, s], axis=1).T)
    kk = np.arange(128)[:, None]
    qq = np.arange(512)[None, :]
    msk = np.concatenate([(qq >= kk + 128 * p) for p in range(4)],
                         axis=1).astype(ml_dtypes.bfloat16)
    ones = np.ones((128, 128), dtype=ml_dtypes.bfloat16)
    ident = np.eye(128, dtype=ml_dtypes.bfloat16)
    return c2, s2, np.ascontiguousarray(msk), ones, ident


def kernel(L0_attn_x, L0_attn_w_qkv, L0_attn_w_out, L0_attn_ln_scale, L0_attn_ln_bias,
           L0_ffn_w_router, L0_ffn_b_router, L0_ffn_norm1_w, L0_ffn_wv, L0_ffn_bv,
           L0_ffn_wo, L0_ffn_norm2_w, L0_ffn_w_gate, L0_ffn_w_up, L0_ffn_w_down):
    c2, s2, msk, ones, ident = _host_consts()
    A = np.ascontiguousarray
    shared = {
        "w_qkv": A(L0_attn_w_qkv, dtype=np.float32),
        "w_out": A(L0_attn_w_out, dtype=np.float32),
        "wv": A(L0_ffn_wv, dtype=np.float32),
        "wo": A(L0_ffn_wo, dtype=np.float32),
        "w_gate": A(L0_ffn_w_gate, dtype=np.float32),
        "w_up": A(L0_ffn_w_up, dtype=np.float32),
        "w_down": A(L0_ffn_w_down, dtype=np.float32),
        "gam": A(np.asarray(L0_attn_ln_scale, dtype=np.float32).reshape(128, 1)),
        "bet": A(np.asarray(L0_attn_ln_bias, dtype=np.float32).reshape(128, 1)),
        "w1t": A(np.asarray(L0_ffn_norm1_w, dtype=np.float32).reshape(KE, 128).T),
        "w2t": A(np.asarray(L0_ffn_norm2_w, dtype=np.float32).reshape(KE, 128).T),
        "bvt": A(np.asarray(L0_ffn_bv, dtype=np.float32).reshape(KE, 128).T),
        "wr_rep": A(np.broadcast_to(
            np.asarray(L0_ffn_w_router, dtype=np.float32)[:, None], (E, 128))),
        "br_t": A(np.broadcast_to(
            np.asarray(L0_ffn_b_router, dtype=np.float32), (128, 1))),
        "c2": c2, "s2": s2, "msk": msk, "ones128": ones, "ident": ident,
        "epsv": A(np.broadcast_to(np.array([LN_EPS, RMS_EPS], dtype=np.float32),
                                  (128, 2))),
    }
    xs = np.asarray(L0_attn_x, dtype=np.float32)
    nc = build_program(debug=False)
    in_maps = []
    for i in range(8):
        m = dict(shared)
        m["x"] = A(xs[i])
        in_maps.append(m)
    res = run_bass_kernel_spmd(nc, in_maps, core_ids=list(range(8)))
    return np.stack([r["out"] for r in res.results], axis=0)


# version 12 (final)
# speedup vs baseline: 1.1677x; 1.0368x over previous
import contextlib
import numpy as np
import ml_dtypes
import concourse.bass as bass
import concourse.mybir as mybir
from concourse.bass_utils import run_bass_kernel_spmd

F32 = mybir.dt.float32
BF16 = mybir.dt.bfloat16
AF = mybir.ActivationFunctionType
ALU = mybir.AluOpType

N = 4096
E = 2048
H = 16
FF = 5632
NC_ = 8
NTI = 32
KE = 16
NF = FF // 128
LN_EPS = 1e-5
RMS_EPS = 1e-6


SIM_SAFE = [False]
A1_CAP = [256]
NO_POOL = [False]


def rsqrt_act(scalar, out, in_, eps):
    if SIM_SAFE[0]:
        scalar.activation(out, in_, AF.Ln, bias=eps)
        return scalar.activation(out, out, AF.Exp, scale=-0.5)
    return scalar.activation(out, in_, AF.Abs_reciprocal_sqrt, bias=eps)


def preload_rsqrt(scalar):
    if SIM_SAFE[0]:
        return preload_set_with(scalar, [AF.Ln, AF.Exp])
    return scalar.preload_act_table(AF.Abs_reciprocal_sqrt)


def preload_set_with(scalar, funcs):
    from concourse.hw_specs import get_activation_tables
    for idx, fs in enumerate(get_activation_tables(scalar.bass.m.arch).values()):
        if all(f in fs for f in funcs):
            return scalar.add_instruction(
                mybir.InstLoadActFuncSet(
                    name=scalar.bass.get_next_instruction_name(),
                    act_func_set_id=idx, ins=[], outs=[]))
    raise ValueError("no table set with %s" % funcs)


class Sem:
    def __init__(self, es, nc, name):
        self.h = es.enter_context(nc.semaphore(name))
        self.n = 0

    def inc(self, instr, k=1):
        instr.then_inc(self.h, k)
        self.n += k


def _chunk(c):
    return slice(c * 512, (c + 1) * 512)


def build_program(debug=False, upto='D3'):
    nc = bass.Bass("TRN2", target_bir_lowering=False)

    x_in = nc.dram_tensor("x", [N, E], F32, kind="ExternalInput").ap()
    w_qkv = nc.dram_tensor("w_qkv", [E, 3 * E], F32, kind="ExternalInput").ap()
    w_out = nc.dram_tensor("w_out", [E, E], F32, kind="ExternalInput").ap()
    wv_in = nc.dram_tensor("wv", [E, E], F32, kind="ExternalInput").ap()
    wo_in = nc.dram_tensor("wo", [E, E], F32, kind="ExternalInput").ap()
    wg_in = nc.dram_tensor("w_gate", [E, FF], F32, kind="ExternalInput").ap()
    wu_in = nc.dram_tensor("w_up", [E, FF], F32, kind="ExternalInput").ap()
    wd_in = nc.dram_tensor("w_down", [FF, E], F32, kind="ExternalInput").ap()
    gam_in = nc.dram_tensor("gam", [128, 1], F32, kind="ExternalInput").ap()
    bet_in = nc.dram_tensor("bet", [128, 1], F32, kind="ExternalInput").ap()
    w1_in = nc.dram_tensor("w1t", [128, KE], F32, kind="ExternalInput").ap()
    w2_in = nc.dram_tensor("w2t", [128, KE], F32, kind="ExternalInput").ap()
    bv_in = nc.dram_tensor("bvt", [128, KE], F32, kind="ExternalInput").ap()
    wr_in = nc.dram_tensor("wr_rep", [E, 128], F32, kind="ExternalInput").ap()
    br_in = nc.dram_tensor("br_t", [128, 1], F32, kind="ExternalInput").ap()
    c2_in = nc.dram_tensor("c2", [128, N], F32, kind="ExternalInput").ap()
    s2_in = nc.dram_tensor("s2", [128, N], F32, kind="ExternalInput").ap()
    msk_in = nc.dram_tensor("msk", [128, 4 * 512], BF16, kind="ExternalInput").ap()
    ones_in = nc.dram_tensor("ones128", [128, 128], BF16, kind="ExternalInput").ap()
    eps_in = nc.dram_tensor("epsv", [128, 2], F32, kind="ExternalInput").ap()
    id_in = nc.dram_tensor("ident", [128, 128], BF16, kind="ExternalInput").ap()

    out_t = nc.dram_tensor("out", [N, E], F32, kind="ExternalOutput").ap()

    dk = "ExternalOutput" if debug else "Internal"
    xbf = nc.dram_tensor("xbf", [N, E], BF16, kind="Internal").ap()
    xlo = nc.dram_tensor("xlo", [N, E], BF16, kind="Internal").ap()
    qT = nc.dram_tensor("qT", [H, 128, N], BF16, kind=dk).ap()
    kTs = nc.dram_tensor("kTs", [H, 128, N], BF16, kind=dk).ap()
    vsc = nc.dram_tensor("vsc", [NTI, 128, E], BF16, kind=dk).ap()
    aT = nc.dram_tensor("aT", [H, 128, N], BF16, kind=dk).ap()
    x1sc = nc.dram_tensor("x1sc", [KE, 128, N], BF16, kind=dk).ap()
    msc = nc.dram_tensor("msc", [NC_, 128, 512], BF16, kind=dk).ap()
    vfsc = nc.dram_tensor("vfsc", [KE, 128, N], BF16, kind="Internal").ap()
    hsc = nc.dram_tensor("hsc", [KE, 128, N], BF16, kind=dk).ap()
    h2sc = nc.dram_tensor("h2sc", [KE, 128, N], BF16, kind=dk).ap()
    wvq_bf = nc.dram_tensor("wvq_bf", [E, E], BF16, kind="Internal").ap()
    wout_bf = nc.dram_tensor("wout_bf", [E, E], BF16, kind="Internal").ap()
    wvf_bf = nc.dram_tensor("wvf_bf", [E, E], BF16, kind="Internal").ap()
    wo_bf = nc.dram_tensor("wo_bf", [E, E], BF16, kind="Internal").ap()
    gate_bf = nc.dram_tensor("gate_bf", [NF, E, 128], BF16, kind="Internal").ap()
    up_bf = nc.dram_tensor("up_bf", [NF, E, 128], BF16, kind="Internal").ap()
    down_bf = nc.dram_tensor("down_bf", [KE, FF, 128], BF16, kind="Internal").ap()

    top = contextlib.ExitStack()
    with top:
        ones128 = top.enter_context(nc.sbuf_tensor("ones_sb", [128, 128], BF16))
        ident = top.enter_context(nc.sbuf_tensor("id_sb", [128, 128], BF16))
        gam = top.enter_context(nc.sbuf_tensor("gam_sb", [128, 1], F32))
        bet = top.enter_context(nc.sbuf_tensor("bet_sb", [128, 1], F32))
        w1sb = top.enter_context(nc.sbuf_tensor("w1_sb", [128, KE], F32))
        w2sb = top.enter_context(nc.sbuf_tensor("w2_sb", [128, KE], F32))
        bvsb = top.enter_context(nc.sbuf_tensor("bv_sb", [128, KE], F32))
        brsb = top.enter_context(nc.sbuf_tensor("br_sb", [128, 1], F32))
        epsb = top.enter_context(nc.sbuf_tensor("eps_sb", [128, 2], F32))
        barsb = top.enter_context(nc.sbuf_tensor("bar_sb", [128, 16], F32))
        nc.const_aps.register(F32, LN_EPS, epsb[:, 0:1])
        nc.const_aps.register(F32, RMS_EPS, epsb[:, 1:2])

        def S(name):
            return Sem(top, nc, name)

        s_cst = S("cst"); s_xt = S("xt")
        s_wq = [S("wq0"), S("wq1")]
        s_wcast = S("wcast"); s_ph = S("ph")
        s_a1mm = S("a1mm"); s_a1pf = S("a1pf"); s_a1r = S("a1r")
        s_a1st = S("a1st"); s_a1sf = S("a1sf"); s_a1var = S("a1var")
        s_a1rb = S("a1rb"); s_a1rl = S("a1rl"); s_a1qo = S("a1qo"); s_a1sw = S("a1sw")
        s_xbf = S("xbf")
        s_a1sp = [S("a1sp0"), S("a1sp1")]
        s_wv = S("wv"); s_a2mm = S("a2mm"); s_a2ev = S("a2ev")
        s_a2sp = [S("a2sp0"), S("a2sp1")]
        s_bld = [S("bld0"), S("bld1")]
        s_bmm1 = S("bmm1"); s_bsq1 = S("bsq1")
        s_bsq2 = S("bsq2"); s_bmm2 = S("bmm2"); s_bep = S("bep")
        s_bsp = [S("bsp0"), S("bsp1")]
        s_xlld = [S("xlld0"), S("xlld1")]
        s_xl = S("xl")
        s_xlsp = [S("xlsp0"), S("xlsp1")]
        s_cw = S("cw")
        s_cld = [S("cld0"), S("cld1")]
        s_cx0 = S("cx0")
        s_cmm = S("cmm"); s_cev = S("cev")
        s_csq = S("csq"); s_cst2 = S("cst2"); s_csf = S("csf"); s_cvar = S("cvar")
        s_crb = S("crb"); s_cx1 = S("cx1"); s_cx1sp = S("cx1sp"); s_crt = S("crt")
        s_cms = S("cms")
        s_cmsp = [S("cmsp0"), S("cmsp1")]
        s_d1w = S("d1w")
        s_d1ld = [S("d1ld0"), S("d1ld1")]
        s_d1xq = S("d1xq"); s_d1ss = S("d1ss")
        s_d1rb = S("d1rb"); s_d1h1 = S("d1h1"); s_d1mm = S("d1mm"); s_d1ev = S("d1ev")
        s_d1sp = [S("d1sp0"), S("d1sp1")]
        s_d2w = S("d2w")
        s_d2ld = [S("d2ld0"), S("d2ld1")]
        s_d2mm = S("d2mm"); s_d2h = S("d2h")
        s_d2sq = S("d2sq"); s_d2st = S("d2st"); s_d2rb = S("d2rb")
        s_d2h2 = S("d2h2")
        s_d2sp = [S("d2sp0"), S("d2sp1")]
        s_d2hsp = S("d2hsp")
        s_d3wf = S("d3wf"); s_d3wd = S("d3wd")
        s_d3h2ld = [S("d3h2ld0"), S("d3h2ld1")]
        s_d3xld = S("d3xld")
        s_d3mm = S("d3mm"); s_d3sg = S("d3sg"); s_d3gf = S("d3gf"); s_d3dn = S("d3dn")
        s_d3z = S("d3z"); s_d3zq = S("d3zq"); s_d3st = S("d3st"); s_d3sf = S("d3sf")
        s_d3var = S("d3var"); s_d3rb = S("d3rb"); s_d3x2 = S("d3x2"); s_d3tr = S("d3tr")
        s_d3oev = S("d3oev")
        s_out = [S("outd0"), S("outd1")]

        PHC = [0]
        CSTB = [0]
        _dma_sems_all = [s_cst, s_xt, s_wq[0], s_wq[1], s_wcast,
                         s_a1sp[0], s_a1sp[1], s_wv, s_a2sp[0], s_a2sp[1],
                         s_bld[0], s_bld[1], s_bsp[0], s_bsp[1],
                         s_xlld[0], s_xlld[1], s_xlsp[0], s_xlsp[1],
                         s_cw, s_cld[0], s_cld[1], s_cx0, s_cx1sp, s_cmsp[0], s_cmsp[1],
                         s_d1w, s_d1ld[0], s_d1ld[1], s_d1sp[0], s_d1sp[1],
                         s_d2w, s_d2ld[0], s_d2ld[1], s_d2sp[0], s_d2sp[1], s_d2hsp,
                         s_d3wf, s_d3wd, s_d3h2ld[0], s_d3h2ld[1], s_d3xld,
                         s_out[0], s_out[1]]

        def barrier_block(sync_waits=(), scalar_waits=()):
            tgt = PHC[0] + 5
            with nc.Block() as blk:
                @blk.sync
                def _(e):
                    for (sem, v) in sync_waits:
                        e.wait_ge(sem.h, v)
                    e.nop().then_inc(s_ph.h, 1); s_ph.n += 1
                    e.wait_ge(s_ph.h, tgt)

                @blk.gpsimd
                def _(e):
                    e.nop().then_inc(s_ph.h, 1); s_ph.n += 1
                    e.wait_ge(s_ph.h, tgt)

                @blk.tensor
                def _(e):
                    e.nop().then_inc(s_ph.h, 1); s_ph.n += 1
                    e.wait_ge(s_ph.h, tgt)

                @blk.vector
                def _(e):
                    e.wait_ge(s_cst.h, 144)
                    ii = e.tensor_copy(barsb[:, 0:1], epsb[:, 0:1])
                    ii.then_inc(s_ph.h, 1); s_ph.n += 1
                    e.wait_ge(s_ph.h, tgt)

                @blk.scalar
                def _(e):
                    for (sem, v) in scalar_waits:
                        e.wait_ge(sem.h, v)
                    e.wait_ge(s_cst.h, 144)
                    ii = e.copy(barsb[:, 1:2], epsb[:, 0:1])
                    ii.then_inc(s_ph.h, 1); s_ph.n += 1
                    e.wait_ge(s_ph.h, tgt)
            PHC[0] = s_ph.n
            assert PHC[0] == tgt

        def drain_block():
            with nc.Block() as blk:
                @blk.sync
                def _(sync):
                    for sem in _dma_sems_all:
                        if sem.n > 0:
                            sync.wait_ge(sem.h, sem.n)

        with nc.Block() as blk:
            @blk.sync
            def _(sync):
                for dst, src in [(ones128, ones_in), (ident, id_in),
                                 (gam, gam_in), (bet, bet_in), (w1sb, w1_in),
                                 (w2sb, w2_in), (bvsb, bv_in), (brsb, br_in),
                                 (epsb, eps_in)]:
                    sync.dma_start(dst[:], src).then_inc(s_cst.h, 16)
                    s_cst.n += 16

            @blk.gpsimd
            def _(gpsimd):
                for c in range(NC_):
                    gpsimd.dma_start(xbf[_chunk(c), :], x_in[_chunk(c), :]
                                     ).then_inc(s_xbf.h, 16)
                    s_xbf.n += 16



        CST0 = s_cst.n

        with contextlib.ExitStack() as pa:
            xT = pa.enter_context(nc.sbuf_tensor("xT", [128, KE * N], BF16))

            def xTs(k, c=None):
                if c is None:
                    return xT[:, k * N:(k + 1) * N]
                return xT[:, k * N + c * 512: k * N + (c + 1) * 512]

            with contextlib.ExitStack() as pa1:
                c2sb = pa1.enter_context(nc.sbuf_tensor("c2_sb", [128, N], F32))
                s2sb = pa1.enter_context(nc.sbuf_tensor("s2_sb", [128, N], F32))
                wq = pa1.enter_context(nc.sbuf_tensor("wq_sb", [128, 2 * KE * 128], BF16))
                swp = pa1.enter_context(nc.sbuf_tensor("swp", [128, 2 * 512], F32))
                tc = pa1.enter_context(nc.sbuf_tensor("tc", [128, 512], F32))
                tsb = pa1.enter_context(nc.sbuf_tensor("tsb", [128, 512], F32))
                rbuf = pa1.enter_context(nc.sbuf_tensor("rbuf", [128, 3 * 512], F32))
                rbf = pa1.enter_context(nc.sbuf_tensor("rbf", [128, 2 * 512], BF16))
                rsqb = pa1.enter_context(nc.sbuf_tensor("rsqb", [128, 2 * 512], BF16))
                mbb = pa1.enter_context(nc.sbuf_tensor("mbb", [128, 2 * 512], BF16))
                uqb = pa1.enter_context(nc.sbuf_tensor("uqb", [128, 2 * 512], F32))
                varb = pa1.enter_context(nc.sbuf_tensor("varb", [128, 512], F32))
                rbb = pa1.enter_context(nc.sbuf_tensor("rbb", [128, 2 * 512], F32))
                tb2 = pa1.enter_context(nc.sbuf_tensor("tb2", [128, 512], BF16))
                tb2f = pa1.enter_context(nc.sbuf_tensor("tb2f", [128, 512], F32))
                rlb = pa1.enter_context(nc.sbuf_tensor("rlb", [128, 2 * 512], BF16))
                qob = pa1.enter_context(nc.sbuf_tensor("qob", [128, 2 * 512], BF16))
                p_p = [pa1.enter_context(nc.psum_tensor("pp%d" % i, [128, 512], F32))
                       for i in range(2)]
                p_sa = [pa1.enter_context(nc.psum_tensor("psa%d" % i, [128, 512], F32))
                        for i in range(2)]
                p_sb = [pa1.enter_context(nc.psum_tensor("psb%d" % i, [128, 512], F32))
                        for i in range(2)]

                def b2(buf, u):
                    return buf[:, (u % 2) * 512:(u % 2 + 1) * 512]

                def b3(buf, u):
                    return buf[:, (u % 3) * 512:(u % 3 + 1) * 512]

                with nc.Block() as blk:
                    @blk.sync
                    def _(sync):
                        sync.wait_ge(s_cst.h, CST0)
                        sync.dma_start(c2sb[:], c2_in).then_inc(s_cst.h, 16)
                        sync.dma_start(s2sb[:], s2_in).then_inc(s_cst.h, 16)
                        s_cst.n += 32
                        sync.wait_ge(s_xbf.h, 128)
                        for c in range(NC_):
                            for k in range(KE):
                                sync.dma_start(
                                    xTs(k, c),
                                    xbf[_chunk(c), k * 128:(k + 1) * 128],
                                    transpose=True).then_inc(s_xt.h, 16)
                                s_xt.n += 16
                        for u in range(256):
                            s, c = divmod(u, NC_)
                            t = qT if s < H else kTs
                            sync.wait_ge(s_a1qo.h, u + 1)
                            sync.dma_start(t[s % H][:, _chunk(c)], b2(qob, u)
                                           ).then_inc(s_a1sp[u % 2].h, 16)
                            s_a1sp[u % 2].n += 16

                    @blk.gpsimd
                    def _(gpsimd):
                        casts = []
                        for tgt, srcap in [(wvq_bf, w_qkv[:, 2 * E:3 * E]),
                                           (wout_bf, w_out),
                                           (wvf_bf, wv_in),
                                           (wo_bf, wo_in)]:
                            for i8 in range(8):
                                rs8 = slice(i8 * 256, (i8 + 1) * 256)
                                casts.append((tgt[rs8, :], srcap[rs8, :]))
                        for s in range(2 * H):
                            if s >= 2:
                                gpsimd.wait_ge(s_a1mm.h, 8 * (s - 1))
                            col0 = s * 128 if s < H else E + (s - H) * 128
                            src = w_qkv[:, col0:col0 + 128].rearrange(
                                "(k p) c -> p k c", p=128)
                            dst = wq[:, (s % 2) * KE * 128:(s % 2 + 1) * KE * 128
                                     ].rearrange("p (k c) -> p k c", k=KE)
                            gpsimd.dma_start(dst, src).then_inc(s_wq[s % 2].h, 16)
                            s_wq[s % 2].n += 16
                            gpsimd.dma_start(casts[s][0], casts[s][1]
                                             ).then_inc(s_a1sw.h, 16)
                            s_a1sw.n += 16
                        for f in range(NF):
                            gpsimd.dma_start(gate_bf[f], wg_in[:, f * 128:(f + 1) * 128]
                                             ).then_inc(s_wcast.h, 16)
                            gpsimd.dma_start(up_bf[f], wu_in[:, f * 128:(f + 1) * 128]
                                             ).then_inc(s_wcast.h, 16)
                            s_wcast.n += 32
                        for m in range(KE):
                            gpsimd.dma_start(down_bf[m], wd_in[:, m * 128:(m + 1) * 128]
                                             ).then_inc(s_wcast.h, 16)
                            s_wcast.n += 16

                    @blk.tensor
                    def _(tensor):
                        tensor.wait_ge(s_cst.h, CST0)
                        tensor.wait_ge(s_xt.h, 16 * KE * NC_)

                        def stats(v):
                            tensor.wait_ge(s_a1r.h, v + 1)
                            if v >= 2:
                                tensor.wait_ge(s_a1sf.h, v - 1)
                            tensor.matmul(p_sa[v % 2][:], ones128[:], b2(rbf, v),
                                          start=True, stop=True)
                            ii = tensor.matmul(p_sb[v % 2][:], ones128[:], b2(rsqb, v),
                                               start=True, stop=True)
                            s_a1st.inc(ii)

                        for s in range(2 * H):
                            for c in range(NC_):
                                u = s * NC_ + c
                                tensor.wait_ge(s_wq[s % 2].h, 16 * (s // 2 + 1))
                                if u >= 2:
                                    tensor.wait_ge(s_a1pf.h, u - 1)
                                wqs = wq[:, (s % 2) * KE * 128:(s % 2 + 1) * KE * 128]
                                ii = None
                                for k in range(KE):
                                    ii = tensor.matmul(
                                        p_p[u % 2][:],
                                        wqs[:, k * 128:(k + 1) * 128],
                                        xTs(k, c),
                                        start=(k == 0), stop=(k == KE - 1))
                                s_a1mm.inc(ii)
                                if u >= 1:
                                    stats(u - 1)
                        stats(255)

                    @blk.vector
                    def _(vector):
                        vector.wait_ge(s_cst.h, CST0 + 32)
                        for u in range(258):
                            if u < 256:
                                s, c = divmod(u, NC_)
                                cc = _chunk(c)
                                vector.wait_ge(s_a1mm.h, u + 1)
                                if u >= 2:
                                    vector.wait_ge(s_a1st.h, u - 1)
                                sw = slice((u % 2) * 512, (u % 2 + 1) * 512)
                                vector.tensor_copy(swp[0:64, sw],
                                                   p_p[u % 2][64:128, :])
                                vector.tensor_copy(swp[64:128, sw],
                                                   p_p[u % 2][0:64, :])
                                ii = vector.tensor_mul(tc[:], p_p[u % 2][:],
                                                       c2sb[:, cc])
                                s_a1pf.inc(ii)
                                vector.tensor_mul(tsb[:], swp[:, sw], s2sb[:, cc])
                                vector.tensor_add(b3(rbuf, u), tc[:], tsb[:])
                                vector.tensor_copy(b2(rbf, u), b3(rbuf, u))
                                ii = vector.tensor_mul(b2(rsqb, u), b2(rbf, u),
                                                       b2(rbf, u))
                                s_a1r.inc(ii)
                            if 0 <= u - 1 < 256:
                                v = u - 1
                                vector.wait_ge(s_a1sf.h, v + 1)
                                if v >= 1:
                                    vector.wait_ge(s_a1rb.h, v)
                                vector.tensor_mul(tb2[:], b2(mbb, v), b2(mbb, v))
                                ii = vector.tensor_sub(varb[:], b2(uqb, v), tb2[:])
                                s_a1var.inc(ii)
                            if 0 <= u - 2 < 256:
                                w = u - 2
                                vector.wait_ge(s_a1rb.h, w + 1)
                                if w >= 2:
                                    vector.wait_ge(s_a1qo.h, w - 1)
                                vector.tensor_sub(tb2f[:], b3(rbuf, w), b2(mbb, w))
                                ii = vector.tensor_mul(b2(rlb, w), tb2f[:],
                                                       b2(rbb, w))
                                s_a1rl.inc(ii)

                    @blk.scalar
                    def _(scalar):
                        scalar.wait_ge(s_cst.h, CST0)
                        preload_rsqrt(scalar)
                        for u in range(258):
                            if 0 <= u - 1 < 256:
                                v = u - 1
                                scalar.wait_ge(s_a1st.h, v + 1)
                                if v >= 2:
                                    scalar.wait_ge(s_a1rl.h, v - 1)
                                    scalar.wait_ge(s_a1var.h, v - 1)
                                scalar.activation(b2(mbb, v), p_sa[v % 2][:],
                                                  AF.Identity, scale=1.0 / 128)
                                ii = scalar.activation(b2(uqb, v), p_sb[v % 2][:],
                                                       AF.Identity, scale=1.0 / 128)
                                s_a1sf.inc(ii)
                                scalar.wait_ge(s_a1var.h, v + 1)
                                ii = rsqrt_act(scalar, b2(rbb, v), varb[:], LN_EPS)
                                s_a1rb.inc(ii)
                            if 0 <= u - 2 < 256:
                                w = u - 2
                                scalar.wait_ge(s_a1rl.h, w + 1)
                                if w >= 2:
                                    scalar.wait_ge(s_a1sp[w % 2].h, 16 * (w // 2))
                                ii = scalar.activation(b2(qob, w), b2(rlb, w),
                                                       AF.Identity,
                                                       scale=gam[:, 0:1],
                                                       bias=bet[:, 0:1])
                                s_a1qo.inc(ii)

                barrier_block(sync_waits=[(s_a1sp[0], s_a1sp[0].n), (s_a1sp[1], s_a1sp[1].n)])
                if upto == 'A1':
                    drain_block()
                    return nc

            with contextlib.ExitStack() as pa2:
                wvc = pa2.enter_context(nc.sbuf_tensor("wvc_sb", [128, KE * E], BF16))
                vst = pa2.enter_context(nc.sbuf_tensor("vst", [128, 2 * 512], BF16))
                p_v = [pa2.enter_context(nc.psum_tensor("pv%d" % i, [128, 512], F32))
                       for i in range(8)]

                with nc.Block() as blk:
                    @blk.gpsimd
                    def _(gpsimd):
                        gpsimd.wait_ge(s_ph.h, PHC[0])
                        gpsimd.wait_ge(s_a1sw.h, s_a1sw.n)
                        for k in range(KE):
                            gpsimd.dma_start(
                                wvc[:, k * E:(k + 1) * E],
                                wvq_bf[k * 128:(k + 1) * 128, :]
                            ).then_inc(s_wv.h, 16)
                            s_wv.n += 16

                    @blk.scalar
                    def _(scalar):
                        scalar.wait_ge(s_ph.h, PHC[0])
                        for i in range(NTI):
                            for vc in range(4):
                                u = 4 * i + vc
                                scalar.wait_ge(s_a2mm.h, u + 1)
                                if u >= 2:
                                    scalar.wait_ge(s_a2sp[u % 2].h, 16 * (u // 2))
                                ii = scalar.activation(
                                    vst[:, (u % 2) * 512:(u % 2 + 1) * 512],
                                    p_v[vc + 4 * (i % 2)][:], AF.Identity)
                                s_a2ev.inc(ii)

                    @blk.tensor
                    def _(tensor):
                        tensor.wait_ge(s_ph.h, PHC[0])
                        tensor.wait_ge(s_wv.h, 256)
                        for i in range(NTI):
                            if i >= 2:
                                tensor.wait_ge(s_a2ev.h, 4 * (i - 1))
                            ii = None
                            for k in range(KE):
                                for vc in range(4):
                                    ii = tensor.matmul(
                                        p_v[vc + 4 * (i % 2)][:],
                                        xT[:, k * N + i * 128: k * N + (i + 1) * 128],
                                        wvc[:, k * E + vc * 512: k * E + (vc + 1) * 512],
                                        start=(k == 0), stop=(k == KE - 1))
                                    if k == KE - 1:
                                        s_a2mm.inc(ii)

                    @blk.sync
                    def _(sync):
                        sync.wait_ge(s_ph.h, PHC[0])
                        for i in range(NTI):
                            for vc in range(4):
                                u = 4 * i + vc
                                sync.wait_ge(s_a2ev.h, u + 1)
                                sync.dma_start(
                                    vsc[i][:, vc * 512:(vc + 1) * 512],
                                    vst[:, (u % 2) * 512:(u % 2 + 1) * 512]
                                ).then_inc(s_a2sp[u % 2].h, 16)
                                s_a2sp[u % 2].n += 16

                barrier_block(sync_waits=[(s_a2sp[0], s_a2sp[0].n),
                                          (s_a2sp[1], s_a2sp[1].n)])
                if upto == 'A2':
                    drain_block()
                    return nc

        with contextlib.ExitStack() as pb:
            masks = pb.enter_context(nc.sbuf_tensor("msk_sb", [128, 4 * 512], BF16))
            qsb = pb.enter_context(nc.sbuf_tensor("qsb", [128, 2 * N], BF16))
            ksb = pb.enter_context(nc.sbuf_tensor("ksb", [128, 2 * N], BF16))
            vsb = pb.enter_context(nc.sbuf_tensor("vsb", [128, 2 * N], BF16))
            a2b = pb.enter_context(nc.sbuf_tensor("a2b", [128, 4 * 512], BF16))
            a4b = pb.enter_context(nc.sbuf_tensor("a4b", [128, 4 * 512], BF16))
            rbB = pb.enter_context(nc.sbuf_tensor("rbB", [128, 2 * 512], F32))
            astg = pb.enter_context(nc.sbuf_tensor("astg", [128, 2 * 512], BF16))
            xstg = pb.enter_context(nc.sbuf_tensor("xstg", [128, 2 * 512], F32))
            xbstg = pb.enter_context(nc.sbuf_tensor("xbstg", [128, 2 * 512], BF16))
            xlostg = pb.enter_context(nc.sbuf_tensor("xlostg", [128, 2 * 512], BF16))
            p_s = [pb.enter_context(nc.psum_tensor("ps%d" % i, [128, 512], F32))
                   for i in range(4)]
            p_o = [pb.enter_context(nc.psum_tensor("po%d" % i, [128, 512], F32))
                   for i in range(2)]
            p_d = [pb.enter_context(nc.psum_tensor("pd%d" % i, [128, 512], F32))
                   for i in range(2)]
            LAG = 3

            def xlt(t):
                r, cb = divmod(t, 4)
                return (slice(r * 128, (r + 1) * 128), slice(cb * 512, (cb + 1) * 512))

            units = []
            U = 0
            for h in range(H):
                for c in range(NC_):
                    C = h * NC_ + c
                    for j in range(4 * c + 4):
                        units.append((h, c, j, C, U, j - 4 * c))
                        U += 1
            NU = len(units)
            chunk_last = {}
            for (h, c, j, C, u, p) in units:
                chunk_last[C] = u

            def cs_after(u):
                return (u + 1) - ((u + 1) // 6)

            bep_evac = {}
            bep_astg = {}
            _bep = 0
            for idx, pu in enumerate(units):
                if pu[4] % 6 == 5:
                    _bep += 1
                    bep_evac[pu[4]] = _bep
                if idx >= LAG:
                    pu2 = units[idx - LAG]
                    if pu2[4] == chunk_last[pu2[3]]:
                        _bep += 1
                        bep_astg[pu2[3]] = _bep
            for pu2 in units[-LAG:]:
                if pu2[4] == chunk_last[pu2[3]]:
                    _bep += 1
                    bep_astg[pu2[3]] = _bep

            with nc.Block() as blk:
                @blk.gpsimd
                def _(gpsimd):
                    gpsimd.wait_ge(s_ph.h, PHC[0])

                    def xloads(t):
                        rs, cs2 = xlt(t)
                        o = (t % 2) * 512
                        gpsimd.dma_start(xstg[:, o:o + 512], x_in[rs, cs2]
                                         ).then_inc(s_xlld[t % 2].h, 16)
                        gpsimd.dma_start(xbstg[:, o:o + 512], xbf[rs, cs2]
                                         ).then_inc(s_xlld[t % 2].h, 16)
                        s_xlld[t % 2].n += 32

                    xloads(0)
                    xloads(1)
                    for t in range(128):
                        rs, cs2 = xlt(t)
                        o = (t % 2) * 512
                        gpsimd.wait_ge(s_xlld[t % 2].h, 32 * (t // 2 + 1))
                        if t >= 2:
                            gpsimd.wait_ge(s_xlsp[t % 2].h, 16 * (t // 2))
                        ii = gpsimd.tensor_sub(xlostg[:, o:o + 512],
                                               xstg[:, o:o + 512],
                                               xbstg[:, o:o + 512])
                        s_xl.inc(ii)
                        gpsimd.wait_ge(s_xl.h, s_xl.n)
                        gpsimd.dma_start(xlo[rs, cs2], xlostg[:, o:o + 512]
                                         ).then_inc(s_xlsp[t % 2].h, 16)
                        s_xlsp[t % 2].n += 16
                        if t + 2 < 128:
                            xloads(t + 2)

                @blk.sync
                def _(sync):
                    sync.wait_ge(s_ph.h, PHC[0])
                    sync.dma_start(masks[:], msk_in).then_inc(s_cst.h, 16)
                    s_cst.n += 16
                    CSTB[0] = s_cst.n

                    def loads(h):
                        if h >= 2:
                            sync.wait_ge(s_bsp[0].h, 64 * (h - 1))
                            sync.wait_ge(s_bsp[1].h, 64 * (h - 1))
                        o = (h % 2) * N
                        sync.dma_start(qsb[:, o:o + N], qT[h]).then_inc(s_bld[h % 2].h, 16)
                        s_bld[h % 2].n += 16
                        sync.dma_start(ksb[:, o:o + N], kTs[h]).then_inc(s_bld[h % 2].h, 16)
                        s_bld[h % 2].n += 16
                        src = vsc[:, :, h * 128:(h + 1) * 128].rearrange("j p d -> p j d")
                        dst = vsb[:, o:o + N].rearrange("p (j d) -> p j d", j=NTI)
                        sync.dma_start(dst, src).then_inc(s_bld[h % 2].h, 16)
                        s_bld[h % 2].n += 16

                    loads(0)
                    loads(1)
                    for h in range(H):
                        for c in range(NC_):
                            C = h * NC_ + c
                            sync.wait_ge(s_bep.h, bep_astg[C])
                            sync.dma_start(aT[h][:, _chunk(c)],
                                           astg[:, (C % 2) * 512:(C % 2 + 1) * 512]
                                           ).then_inc(s_bsp[C % 2].h, 16)
                            s_bsp[C % 2].n += 16
                        if h + 2 < H:
                            loads(h + 2)

                @blk.tensor
                def _(tensor):
                    tensor.wait_ge(s_ph.h, PHC[0])

                    def mm2den(pu):
                        (h, c, j, C, u, p) = pu
                        o = (h % 2) * N
                        if j == 0 and C >= 2:
                            tensor.wait_ge(s_bep.h, bep_astg[C - 2])
                        tensor.wait_ge(s_bsq2.h, u + 1)
                        tensor.matmul(p_o[C % 2][:],
                                      vsb[:, o + j * 128: o + (j + 1) * 128],
                                      a4b[:, (u % 4) * 512:(u % 4 + 1) * 512],
                                      start=(j == 0), stop=(j == 4 * c + 3))
                        ii = tensor.matmul(p_d[C % 2][:], ones128[:],
                                           a4b[:, (u % 4) * 512:(u % 4 + 1) * 512],
                                           start=(j == 0), stop=(j == 4 * c + 3))
                        s_bmm2.inc(ii)

                    for idx, pu in enumerate(units):
                        (h, c, j, C, u, p) = pu
                        o = (h % 2) * N
                        if j == 0:
                            tensor.wait_ge(s_bld[h % 2].h, 48 * (h // 2 + 1))
                        if u >= 4:
                            w = u - 4
                            if w % 6 == 5:
                                tensor.wait_ge(s_bep.h, bep_evac[w])
                            else:
                                tensor.wait_ge(s_bsq1.h, cs_after(w))
                        ii = tensor.matmul(p_s[u % 4][:],
                                           ksb[:, o + j * 128: o + (j + 1) * 128],
                                           qsb[:, o + c * 512: o + (c + 1) * 512],
                                           start=True, stop=True)
                        s_bmm1.inc(ii)
                        if idx >= LAG:
                            mm2den(units[idx - LAG])
                    for pu in units[-LAG:]:
                        mm2den(pu)

                @blk.scalar
                def _(scalar):
                    scalar.wait_ge(s_ph.h, PHC[0])
                    scalar.preload_act_table(AF.Square)
                    for pu in units:
                        (h, c, j, C, u, p) = pu
                        if u % 6 == 5:
                            continue
                        scalar.wait_ge(s_bmm1.h, u + 1)
                        if u >= 4:
                            scalar.wait_ge(s_bsq2.h, u - 3)
                        ii = scalar.activation(a2b[:, (u % 4) * 512:(u % 4 + 1) * 512],
                                               p_s[u % 4][:], AF.Square)
                        s_bsq1.inc(ii)

                @blk.vector
                def _(vector):
                    vector.wait_ge(s_ph.h, PHC[0])
                    vector.wait_ge(s_cst.h, CSTB[0])

                    def denom(pu):
                        (h, c, j, C, u, p) = pu
                        if u != chunk_last[C]:
                            return
                        vector.wait_ge(s_bmm2.h, u + 1)
                        if C >= 2:
                            vector.wait_ge(s_bsp[C % 2].h, 16 * (C // 2))
                        cs = slice((C % 2) * 512, (C % 2 + 1) * 512)
                        vector.reciprocal(rbB[:, cs], p_d[C % 2][:])
                        ii = vector.tensor_mul(astg[:, cs], p_o[C % 2][:],
                                               rbB[:, cs])
                        s_bep.inc(ii)

                    for idx, pu in enumerate(units):
                        (h, c, j, C, u, p) = pu
                        sl = slice((u % 4) * 512, (u % 4 + 1) * 512)
                        if u % 6 == 5:
                            vector.wait_ge(s_bmm1.h, u + 1)
                            vector.tensor_copy(a2b[:, sl], p_s[u % 4][:])
                            ii = vector.tensor_mul(a2b[:, sl], a2b[:, sl],
                                                   a2b[:, sl])
                            s_bep.inc(ii)
                        else:
                            vector.wait_ge(s_bsq1.h, cs_after(u))
                        if u >= 4:
                            vector.wait_ge(s_bmm2.h, u - 3)
                        ii = vector.tensor_mul(a4b[:, sl], a2b[:, sl], a2b[:, sl])
                        if p >= 0:
                            ii = vector.tensor_mul(a4b[:, sl], a4b[:, sl],
                                                   masks[:, p * 512:(p + 1) * 512])
                        s_bsq2.inc(ii)
                        if idx >= LAG:
                            denom(units[idx - LAG])
                    for pu in units[-LAG:]:
                        denom(pu)

            barrier_block(sync_waits=[(s_bsp[0], s_bsp[0].n), (s_bsp[1], s_bsp[1].n),
                                      (s_xlsp[0], s_xlsp[0].n),
                                      (s_xlsp[1], s_xlsp[1].n)])
            if upto == 'B':
                drain_block()
                return nc

        with contextlib.ExitStack() as pc:
            wout = pc.enter_context(nc.sbuf_tensor("wout_sb", [128, KE * E], BF16))
            wrsb = pc.enter_context(nc.sbuf_tensor("wr_sb", [128, KE * 128], F32))
            asb = pc.enter_context(nc.sbuf_tensor("asb", [128, KE * 512], BF16))
            x0sb = pc.enter_context(nc.sbuf_tensor("x0sb", [128, KE * 512], BF16))
            x0lo = pc.enter_context(nc.sbuf_tensor("x0lo", [128, KE * 512], BF16))
            ysb = pc.enter_context(nc.sbuf_tensor("ysb", [128, KE * 512], F32))
            ybf = pc.enter_context(nc.sbuf_tensor("ybf", [128, 2 * 512], BF16))
            x1f = pc.enter_context(nc.sbuf_tensor("x1f", [128, KE * 512], F32))
            sqst = pc.enter_context(nc.sbuf_tensor("sqst", [128, 2 * 512], BF16))
            mbc = pc.enter_context(nc.sbuf_tensor("mbc", [128, 2 * 512], BF16))
            uqc = pc.enter_context(nc.sbuf_tensor("uqc", [128, 2 * 512], F32))
            varc = pc.enter_context(nc.sbuf_tensor("varc", [128, 2 * 512], F32))
            rbc = pc.enter_context(nc.sbuf_tensor("rbc", [128, 2 * 512], F32))
            tbc = pc.enter_context(nc.sbuf_tensor("tbc", [128, 512], F32))
            msbst = pc.enter_context(nc.sbuf_tensor("msbst", [128, 2 * 512], BF16))
            p_w = [pc.enter_context(nc.psum_tensor("pw%d" % i, [128, 512], F32))
                   for i in range(2)]
            p_s1 = pc.enter_context(nc.psum_tensor("ps1c", [128, 512], F32))
            p_s2 = pc.enter_context(nc.psum_tensor("ps2c", [128, 512], F32))
            p_r = pc.enter_context(nc.psum_tensor("prc", [128, 512], F32))

            def cbuf(buf, c, m):
                return buf[:, m * 512:(m + 1) * 512]

            with nc.Block() as blk:
                @blk.gpsimd
                def _(gpsimd):
                    gpsimd.wait_ge(s_ph.h, PHC[0])
                    for k in range(KE):
                        gpsimd.dma_start(wout[:, k * E:(k + 1) * E],
                                         wout_bf[k * 128:(k + 1) * 128, :]
                                         ).then_inc(s_cw.h, 16)
                        s_cw.n += 16
                        gpsimd.dma_start(wrsb[:, k * 128:(k + 1) * 128],
                                         wr_in[k * 128:(k + 1) * 128, :]
                                         ).then_inc(s_cw.h, 16)
                        s_cw.n += 16
                    for c in range(NC_):
                        for m in range(KE):
                            cm = c * KE + m
                            gpsimd.wait_ge(s_cx1.h, cm + 1)
                            gpsimd.dma_start(x1sc[m][:, _chunk(c)],
                                             x1f[:, m * 512:(m + 1) * 512]
                                             ).then_inc(s_cx1sp.h, 16)
                            s_cx1sp.n += 16

                @blk.sync
                def _(sync):
                    sync.wait_ge(s_ph.h, PHC[0])

                    def loads(c):
                        if c >= 1:
                            sync.wait_ge(s_cmm.h, 16 * c)
                        for m in range(KE):
                            sync.dma_start(cbuf(asb, c, m), aT[m][:, _chunk(c)]
                                           ).then_inc(s_cld[0].h, 16)
                            s_cld[0].n += 16

                    def x0loads(c):
                        if c >= 1:
                            sync.wait_ge(s_cev.h, 16 * c)
                        for m in range(KE):
                            sync.dma_start(
                                x0sb[:, m * 512:(m + 1) * 512],
                                xbf[_chunk(c), m * 128:(m + 1) * 128],
                                transpose=True).then_inc(s_cx0.h, 16)
                            sync.dma_start(
                                x0lo[:, m * 512:(m + 1) * 512],
                                xlo[_chunk(c), m * 128:(m + 1) * 128],
                                transpose=True).then_inc(s_cx0.h, 16)
                            s_cx0.n += 32

                    loads(0)
                    x0loads(0)
                    for c in range(NC_):
                        if c + 1 < NC_:
                            loads(c + 1)
                            x0loads(c + 1)
                        sync.wait_ge(s_cms.h, c + 1)
                        sync.dma_start(msc[c], msbst[:, (c % 2) * 512:(c % 2 + 1) * 512]
                                       ).then_inc(s_cmsp[c % 2].h, 16)
                        s_cmsp[c % 2].n += 16

                @blk.tensor
                def _(tensor):
                    tensor.wait_ge(s_ph.h, PHC[0])
                    tensor.wait_ge(s_cw.h, 512)

                    def stats(c, m):
                        cm = c * KE + m
                        tensor.wait_ge(s_cev.h, cm + 1)
                        tensor.wait_ge(s_csq.h, cm + 1)
                        if m == 0:
                            tensor.wait_ge(s_csf.h, c)
                        tensor.matmul(p_s1[:], ones128[:],
                                      ybf[:, (cm % 2) * 512:(cm % 2 + 1) * 512],
                                      start=(m == 0), stop=(m == KE - 1))
                        ii = tensor.matmul(p_s2[:], ones128[:],
                                           sqst[:, (cm % 2) * 512:(cm % 2 + 1) * 512],
                                           start=(m == 0), stop=(m == KE - 1))
                        s_cst2.inc(ii)

                    for c in range(NC_):
                        tensor.wait_ge(s_cld[0].h, 256 * (c + 1))
                        for m in range(KE):
                            cm = c * KE + m
                            if cm >= 2:
                                tensor.wait_ge(s_cev.h, cm - 1)
                            ii = None
                            for k in range(KE):
                                ii = tensor.matmul(
                                    p_w[cm % 2][:],
                                    wout[:, k * E + m * 128: k * E + (m + 1) * 128],
                                    cbuf(asb, c, k),
                                    start=(k == 0), stop=(k == KE - 1))
                            s_cmm.inc(ii)
                            if m >= 1:
                                stats(c, m - 1)
                        stats(c, KE - 1)
                        tensor.wait_ge(s_cx1.h, KE * (c + 1))
                        if c >= 1:
                            tensor.wait_ge(s_cms.h, c)
                        ii = None
                        for k in range(KE):
                            ii = tensor.matmul(p_r[:],
                                               wrsb[:, k * 128:(k + 1) * 128],
                                               x1f[:, k * 512:(k + 1) * 512],
                                               start=(k == 0), stop=(k == KE - 1))
                        s_crt.inc(ii)

                @blk.vector
                def _(vector):
                    vector.wait_ge(s_ph.h, PHC[0])
                    for c in range(NC_):
                        vector.wait_ge(s_cx0.h, 512 * (c + 1))
                        for m in range(KE):
                            cm = c * KE + m
                            vector.wait_ge(s_cmm.h, cm + 1)
                            if cm >= 2:
                                vector.wait_ge(s_csq.h, cm - 1)
                                vector.wait_ge(s_cst2.h, cm - 1)
                            vector.tensor_add(ysb[:, m * 512:(m + 1) * 512],
                                              p_w[cm % 2][:],
                                              x0sb[:, m * 512:(m + 1) * 512])
                            vector.tensor_add(ysb[:, m * 512:(m + 1) * 512],
                                              ysb[:, m * 512:(m + 1) * 512],
                                              x0lo[:, m * 512:(m + 1) * 512])
                            ii = vector.tensor_copy(ybf[:, (cm % 2) * 512:(cm % 2 + 1) * 512],
                                                    ysb[:, m * 512:(m + 1) * 512])
                            s_cev.inc(ii)
                        vector.wait_ge(s_csf.h, c + 1)
                        if c >= 2:
                            vector.wait_ge(s_crb.h, c - 1)
                        cs = slice((c % 2) * 512, (c % 2 + 1) * 512)
                        vector.tensor_mul(tbc[:], mbc[:, cs], mbc[:, cs])
                        ii = vector.tensor_sub(varc[:, cs], uqc[:, cs], tbc[:])
                        s_cvar.inc(ii)
                        vector.wait_ge(s_crb.h, c + 1)
                        if c >= 1:
                            vector.wait_ge(s_crt.h, c)
                            vector.wait_ge(s_cx1sp.h, 256 * c)
                        for m in range(KE):
                            cm = c * KE + m
                            vector.tensor_sub(tbc[:], ysb[:, m * 512:(m + 1) * 512],
                                              mbc[:, cs])
                            ii = vector.tensor_mul(x1f[:, m * 512:(m + 1) * 512],
                                                   tbc[:], rbc[:, cs])
                            s_cx1.inc(ii)
                        vector.wait_ge(s_crt.h, c + 1)
                        if c >= 2:
                            vector.wait_ge(s_cmsp[c % 2].h, 16 * (c // 2))
                        ii = vector.tensor_scalar(msbst[:, cs], p_r[:],
                                                  brsb[:, 0:1], 0.0,
                                                  ALU.add, ALU.is_gt)
                        s_cms.inc(ii)

                @blk.scalar
                def _(scalar):
                    scalar.wait_ge(s_ph.h, PHC[0])
                    preload_rsqrt(scalar)
                    for c in range(NC_):
                        cs = slice((c % 2) * 512, (c % 2 + 1) * 512)
                        for m in range(KE):
                            cm = c * KE + m
                            scalar.wait_ge(s_cev.h, cm + 1)
                            if cm >= 2:
                                scalar.wait_ge(s_cst2.h, cm - 1)
                            ii = scalar.activation(
                                sqst[:, (cm % 2) * 512:(cm % 2 + 1) * 512],
                                ybf[:, (cm % 2) * 512:(cm % 2 + 1) * 512], AF.Square)
                            s_csq.inc(ii)
                        scalar.wait_ge(s_cst2.h, 16 * (c + 1))
                        if c >= 2:
                            scalar.wait_ge(s_cvar.h, c - 1)
                            scalar.wait_ge(s_cx1.h, 16 * (c - 1))
                        scalar.activation(mbc[:, cs], p_s1[:], AF.Identity,
                                          scale=1.0 / E)
                        ii = scalar.activation(uqc[:, cs], p_s2[:], AF.Identity,
                                               scale=1.0 / E)
                        s_csf.inc(ii)
                        scalar.wait_ge(s_cvar.h, c + 1)
                        ii = rsqrt_act(scalar, rbc[:, cs], varc[:, cs], LN_EPS)
                        s_crb.inc(ii)

            barrier_block(sync_waits=[(s_cx1sp, s_cx1sp.n), (s_cmsp[0], s_cmsp[0].n), (s_cmsp[1], s_cmsp[1].n)])
            if upto == 'C':
                drain_block()
                return nc

        with contextlib.ExitStack() as pd1:
            wvsb = pd1.enter_context(nc.sbuf_tensor("wv_sb", [128, KE * E], BF16))
            x1d = pd1.enter_context(nc.sbuf_tensor("x1d", [128, 2 * KE * 512], BF16))
            xqb = pd1.enter_context(nc.sbuf_tensor("xqb", [128, 2 * 512], BF16))
            h1sb = pd1.enter_context(nc.sbuf_tensor("h1sb", [128, 2 * KE * 512], BF16))
            ust = pd1.enter_context(nc.sbuf_tensor("ustd1", [128, 512], F32))
            rb1 = pd1.enter_context(nc.sbuf_tensor("rb1", [128, 2 * 512], BF16))
            vstd = pd1.enter_context(nc.sbuf_tensor("vstd", [128, 2 * 512], BF16))
            td1 = pd1.enter_context(nc.sbuf_tensor("td1", [128, 512], BF16))
            p_ss = pd1.enter_context(nc.psum_tensor("pssd1", [128, 512], F32))
            p_v1 = [pd1.enter_context(nc.psum_tensor("pv1%d" % i, [128, 512], F32))
                    for i in range(2)]

            def dbuf(buf, c, m):
                o = ((c % 2) * KE + m) * 512
                return buf[:, o:o + 512]

            with nc.Block() as blk:
                @blk.gpsimd
                def _(gpsimd):
                    gpsimd.wait_ge(s_ph.h, PHC[0])
                    for k in range(KE):
                        gpsimd.dma_start(wvsb[:, k * E:(k + 1) * E],
                                         wvf_bf[k * 128:(k + 1) * 128, :]
                                         ).then_inc(s_d1w.h, 16)
                        s_d1w.n += 16

                @blk.sync
                def _(sync):
                    sync.wait_ge(s_ph.h, PHC[0])
                    for c in range(NC_):
                        if c >= 2:
                            sync.wait_ge(s_d1h1.h, 16 * (c - 1))
                        for m in range(KE):
                            sync.dma_start(dbuf(x1d, c, m), x1sc[m][:, _chunk(c)]
                                           ).then_inc(s_d1ld[c % 2].h, 16)
                            s_d1ld[c % 2].n += 16

                @blk.vector
                def _(vector):
                    vector.wait_ge(s_ph.h, PHC[0])
                    for c in range(NC_):
                        vector.wait_ge(s_d1ld[c % 2].h, 256 * (c // 2 + 1))
                        for k in range(KE):
                            ck = c * KE + k
                            if ck >= 2:
                                vector.wait_ge(s_d1ss.h, ck - 1)
                            ii = vector.tensor_mul(
                                xqb[:, (ck % 2) * 512:(ck % 2 + 1) * 512],
                                dbuf(x1d, c, k), dbuf(x1d, c, k))
                            s_d1xq.inc(ii)
                        vector.wait_ge(s_d1rb.h, c + 1)
                        if c >= 2:
                            vector.wait_ge(s_d1mm.h, 16 * (c - 1))
                        for k in range(KE):
                            vector.tensor_mul(td1[:], dbuf(x1d, c, k),
                                              rb1[:, (c % 2) * 512:(c % 2 + 1) * 512])
                            ii = vector.tensor_scalar(dbuf(h1sb, c, k), td1[:],
                                                      w1sb[:, k:k + 1], None, ALU.mult)
                            s_d1h1.inc(ii)

                @blk.tensor
                def _(tensor):
                    tensor.wait_ge(s_ph.h, PHC[0])
                    tensor.wait_ge(s_d1w.h, 256)
                    for c in range(NC_):
                        for k in range(KE):
                            ck = c * KE + k
                            tensor.wait_ge(s_d1xq.h, ck + 1)
                            if c >= 1 and k == 0:
                                tensor.wait_ge(s_d1rb.h, c)
                            ii = tensor.matmul(
                                p_ss[:], ones128[:],
                                xqb[:, (ck % 2) * 512:(ck % 2 + 1) * 512],
                                start=(k == 0), stop=(k == KE - 1))
                            s_d1ss.inc(ii)
                        tensor.wait_ge(s_d1h1.h, 16 * (c + 1))
                        for m in range(KE):
                            cm = c * KE + m
                            if cm >= 2:
                                tensor.wait_ge(s_d1ev.h, cm - 1)
                            ii = None
                            for k in range(KE):
                                ii = tensor.matmul(
                                    p_v1[cm % 2][:],
                                    wvsb[:, k * E + m * 128: k * E + (m + 1) * 128],
                                    dbuf(h1sb, c, k),
                                    start=(k == 0), stop=(k == KE - 1))
                            s_d1mm.inc(ii)

                @blk.scalar
                def _(scalar):
                    scalar.wait_ge(s_ph.h, PHC[0])
                    preload_rsqrt(scalar)
                    for c in range(NC_):
                        scalar.wait_ge(s_d1ss.h, 16 * (c + 1))
                        scalar.activation(ust[:], p_ss[:], AF.Identity, scale=1.0 / E)
                        ii = rsqrt_act(scalar,
                                       rb1[:, (c % 2) * 512:(c % 2 + 1) * 512],
                                       ust[:], RMS_EPS)
                        s_d1rb.inc(ii)
                        for m in range(KE):
                            cm = c * KE + m
                            scalar.wait_ge(s_d1mm.h, cm + 1)
                            if cm >= 2:
                                scalar.wait_ge(s_d1sp[cm % 2].h, 16 * (cm // 2))
                            ii = scalar.activation(
                                vstd[:, (cm % 2) * 512:(cm % 2 + 1) * 512],
                                p_v1[cm % 2][:], AF.Identity, bias=bvsb[:, m:m + 1])
                            s_d1ev.inc(ii)
                            scalar.dma_start(vfsc[m][:, _chunk(c)],
                                             vstd[:, (cm % 2) * 512:(cm % 2 + 1) * 512]
                                             ).then_inc(s_d1sp[cm % 2].h, 16)
                            s_d1sp[cm % 2].n += 16

            barrier_block(scalar_waits=[(s_d1sp[0], s_d1sp[0].n), (s_d1sp[1], s_d1sp[1].n)])
            if upto == 'D1':
                drain_block()
                return nc

        with contextlib.ExitStack() as pd2:
            wosb = pd2.enter_context(nc.sbuf_tensor("wo_sb", [128, KE * E], BF16))
            x1d2 = pd2.enter_context(nc.sbuf_tensor("x1d2", [128, 2 * KE * 512], BF16))
            vd = pd2.enter_context(nc.sbuf_tensor("vd", [128, 2 * KE * 512], BF16))
            hsb = pd2.enter_context(nc.sbuf_tensor("hsb2", [128, KE * 512], BF16))
            hsqst = pd2.enter_context(nc.sbuf_tensor("hsqst", [128, 2 * 512], BF16))
            ust2 = pd2.enter_context(nc.sbuf_tensor("ustd2", [128, 512], F32))
            rb2 = pd2.enter_context(nc.sbuf_tensor("rb2", [128, 2 * 512], BF16))
            h2st = pd2.enter_context(nc.sbuf_tensor("h2st", [128, 2 * 512], BF16))
            td2 = pd2.enter_context(nc.sbuf_tensor("td2", [128, 512], BF16))
            p_w2 = [pd2.enter_context(nc.psum_tensor("pw2%d" % i, [128, 512], F32))
                    for i in range(2)]
            p_ss2 = pd2.enter_context(nc.psum_tensor("pss2", [128, 512], F32))

            def dbuf2(buf, c, m):
                o = ((c % 2) * KE + m) * 512
                return buf[:, o:o + 512]

            with nc.Block() as blk:
                @blk.gpsimd
                def _(gpsimd):
                    gpsimd.wait_ge(s_ph.h, PHC[0])
                    for k in range(KE):
                        gpsimd.dma_start(wosb[:, k * E:(k + 1) * E],
                                         wo_bf[k * 128:(k + 1) * 128, :]
                                         ).then_inc(s_d2w.h, 16)
                        s_d2w.n += 16

                @blk.sync
                def _(sync):
                    sync.wait_ge(s_ph.h, PHC[0])

                    def loads(c):
                        if c >= 2:
                            sync.wait_ge(s_d2mm.h, 16 * (c - 1))
                            sync.wait_ge(s_d2h.h, 16 * (c - 1))
                        for m in range(KE):
                            sync.dma_start(dbuf2(x1d2, c, m), x1sc[m][:, _chunk(c)]
                                           ).then_inc(s_d2ld[c % 2].h, 16)
                            s_d2ld[c % 2].n += 16
                            sync.dma_start(dbuf2(vd, c, m), vfsc[m][:, _chunk(c)]
                                           ).then_inc(s_d2ld[c % 2].h, 16)
                            s_d2ld[c % 2].n += 16

                    loads(0)
                    loads(1)
                    for c in range(NC_):
                        for m in range(KE):
                            cm = c * KE + m
                            sync.wait_ge(s_d2h.h, cm + 1)
                            sync.dma_start(hsc[m][:, _chunk(c)],
                                           hsb[:, m * 512:(m + 1) * 512]
                                           ).then_inc(s_d2hsp.h, 16)
                            s_d2hsp.n += 16
                        if c + 2 < NC_:
                            loads(c + 2)

                @blk.tensor
                def _(tensor):
                    tensor.wait_ge(s_ph.h, PHC[0])
                    tensor.wait_ge(s_d2w.h, 256)

                    def stats(c, m):
                        cm = c * KE + m
                        tensor.wait_ge(s_d2sq.h, cm + 1)
                        if m == 0:
                            tensor.wait_ge(s_d2rb.h, c)
                        ii = tensor.matmul(p_ss2[:], ones128[:],
                                           hsqst[:, (cm % 2) * 512:(cm % 2 + 1) * 512],
                                           start=(m == 0), stop=(m == KE - 1))
                        s_d2st.inc(ii)

                    for c in range(NC_):
                        tensor.wait_ge(s_d2ld[c % 2].h, 512 * (c // 2 + 1))
                        for m in range(KE):
                            cm = c * KE + m
                            if cm >= 2:
                                tensor.wait_ge(s_d2h.h, cm - 1)
                            ii = None
                            for k in range(KE):
                                ii = tensor.matmul(
                                    p_w2[cm % 2][:],
                                    wosb[:, k * E + m * 128: k * E + (m + 1) * 128],
                                    dbuf2(vd, c, k),
                                    start=(k == 0), stop=(k == KE - 1))
                            s_d2mm.inc(ii)
                            if m >= 1:
                                stats(c, m - 1)
                        stats(c, KE - 1)

                @blk.vector
                def _(vector):
                    vector.wait_ge(s_ph.h, PHC[0])
                    for c in range(NC_):
                        for m in range(KE):
                            cm = c * KE + m
                            vector.wait_ge(s_d2mm.h, cm + 1)
                            if m == 0:
                                vector.wait_ge(s_d2sq.h, 16 * c)
                                vector.wait_ge(s_d2hsp.h, 256 * c)
                            ii = vector.tensor_add(hsb[:, m * 512:(m + 1) * 512],
                                                   p_w2[cm % 2][:], dbuf2(x1d2, c, m))
                            s_d2h.inc(ii)
                        vector.wait_ge(s_d2rb.h, c + 1)
                        for m in range(KE):
                            cm = c * KE + m
                            if cm >= 2:
                                vector.wait_ge(s_d2sp[cm % 2].h, 16 * (cm // 2))
                            vector.tensor_mul(td2[:], hsb[:, m * 512:(m + 1) * 512],
                                              rb2[:, (c % 2) * 512:(c % 2 + 1) * 512])
                            ii = vector.tensor_scalar(
                                h2st[:, (cm % 2) * 512:(cm % 2 + 1) * 512],
                                td2[:], w2sb[:, m:m + 1], None, ALU.mult)
                            s_d2h2.inc(ii)

                @blk.scalar
                def _(scalar):
                    scalar.wait_ge(s_ph.h, PHC[0])
                    preload_rsqrt(scalar)
                    for c in range(NC_):
                        for m in range(KE):
                            cm = c * KE + m
                            scalar.wait_ge(s_d2h.h, cm + 1)
                            if cm >= 2:
                                scalar.wait_ge(s_d2st.h, cm - 1)
                            ii = scalar.activation(
                                hsqst[:, (cm % 2) * 512:(cm % 2 + 1) * 512],
                                hsb[:, m * 512:(m + 1) * 512], AF.Square)
                            s_d2sq.inc(ii)
                        scalar.wait_ge(s_d2st.h, 16 * (c + 1))
                        if c >= 2:
                            scalar.wait_ge(s_d2h2.h, 16 * (c - 1))
                        scalar.activation(ust2[:], p_ss2[:], AF.Identity, scale=1.0 / E)
                        ii = rsqrt_act(scalar,
                                       rb2[:, (c % 2) * 512:(c % 2 + 1) * 512],
                                       ust2[:], RMS_EPS)
                        s_d2rb.inc(ii)
                        for m in range(KE):
                            cm = c * KE + m
                            scalar.wait_ge(s_d2h2.h, cm + 1)
                            scalar.dma_start(h2sc[m][:, _chunk(c)],
                                             h2st[:, (cm % 2) * 512:(cm % 2 + 1) * 512]
                                             ).then_inc(s_d2sp[cm % 2].h, 16)
                            s_d2sp[cm % 2].n += 16

            barrier_block(sync_waits=[(s_d2hsp, s_d2hsp.n)],
                          scalar_waits=[(s_d2sp[0], s_d2sp[0].n),
                                        (s_d2sp[1], s_d2sp[1].n)])
            if upto == 'D2':
                drain_block()
                return nc

        with contextlib.ExitStack() as pd3:
            x1d3 = pd3.enter_context(nc.sbuf_tensor("x1d3", [128, KE * 512], BF16))
            hd3 = pd3.enter_context(nc.sbuf_tensor("hd3", [128, KE * 512], BF16))
            h2d = pd3.enter_context(nc.sbuf_tensor("h2d", [128, 2 * KE * 512], BF16))
            msbuf = pd3.enter_context(nc.sbuf_tensor("msbuf", [128, 512], BF16))
            wgf = pd3.enter_context(nc.sbuf_tensor("wgf", [128, 2 * KE * 128], BF16))
            wuf = pd3.enter_context(nc.sbuf_tensor("wuf", [128, 2 * KE * 128], BF16))
            wdm = pd3.enter_context(nc.sbuf_tensor("wdm", [128, 2 * NF * 128], BF16))
            gbuf = pd3.enter_context(nc.sbuf_tensor("gbuf", [128, NF * 512], BF16))
            sgst = pd3.enter_context(nc.sbuf_tensor("sgst", [128, 2 * 512], BF16))
            s1st = pd3.enter_context(nc.sbuf_tensor("s1st", [128, 512], BF16))
            s2st = pd3.enter_context(nc.sbuf_tensor("s2st", [128, 512], F32))
            s3st = pd3.enter_context(nc.sbuf_tensor("s3st", [128, 512], BF16))
            zsb = pd3.enter_context(nc.sbuf_tensor("zsb", [128, KE * 512], BF16))
            zqst = pd3.enter_context(nc.sbuf_tensor("zqst", [128, 2 * 512], BF16))
            mbz = pd3.enter_context(nc.sbuf_tensor("mbz", [128, 512], BF16))
            uqz = pd3.enter_context(nc.sbuf_tensor("uqz", [128, 512], BF16))
            varz = pd3.enter_context(nc.sbuf_tensor("varz", [128, 512], BF16))
            rbz = pd3.enter_context(nc.sbuf_tensor("rbz", [128, 512], BF16))
            tz = pd3.enter_context(nc.sbuf_tensor("tz", [128, 512], BF16))
            x2st = pd3.enter_context(nc.sbuf_tensor("x2st", [128, 2 * 512], BF16))
            ost = pd3.enter_context(nc.sbuf_tensor("ost", [128, 2 * 512], F32))
            p_g = pd3.enter_context(nc.psum_tensor("pg", [128, 512], F32))
            p_u = pd3.enter_context(nc.psum_tensor("pu", [128, 512], F32))
            p_mlp = [pd3.enter_context(nc.psum_tensor("pmlp%d" % i, [128, 512], F32))
                     for i in range(2)]
            p_z = pd3.enter_context(nc.psum_tensor("pz", [128, 512], F32))
            p_z2 = pd3.enter_context(nc.psum_tensor("pz2", [128, 512], F32))
            p_tr = [pd3.enter_context(nc.psum_tensor("ptr%d" % i, [128, 512], BF16))
                    for i in range(2)]

            with nc.Block() as blk:
                @blk.sync
                def _(sync):
                    sync.wait_ge(s_ph.h, PHC[0])
                    sync.wait_ge(s_wcast.h, s_wcast.n)

                    def wloads(c):
                        if c >= 2:
                            sync.wait_ge(s_d3mm.h, NF * (c - 1))
                        for m in range(KE):
                            sync.dma_start(
                                h2d[:, ((c % 2) * KE + m) * 512:((c % 2) * KE + m + 1) * 512],
                                h2sc[m][:, _chunk(c)]).then_inc(s_d3h2ld[c % 2].h, 16)
                            s_d3h2ld[c % 2].n += 16

                    def xloads(c):
                        if c >= 1:
                            sync.wait_ge(s_d3z.h, KE * c)
                        for m in range(KE):
                            sync.dma_start(x1d3[:, m * 512:(m + 1) * 512],
                                           x1sc[m][:, _chunk(c)]).then_inc(s_d3xld.h, 16)
                            s_d3xld.n += 16
                            sync.dma_start(hd3[:, m * 512:(m + 1) * 512],
                                           hsc[m][:, _chunk(c)]).then_inc(s_d3xld.h, 16)
                            s_d3xld.n += 16
                        sync.dma_start(msbuf[:], msc[c]).then_inc(s_d3xld.h, 16)
                        s_d3xld.n += 16

                    wloads(0)
                    xloads(0)
                    wloads(1)
                    for c in range(NC_):
                        for f in range(NF):
                            F_ = c * NF + f
                            if F_ >= 2:
                                sync.wait_ge(s_d3mm.h, F_ - 1)
                            o = (F_ % 2) * KE * 128
                            sync.dma_start(
                                wgf[:, o:o + KE * 128].rearrange("p (k c) -> p k c", k=KE),
                                gate_bf[f].rearrange("(k p) c -> p k c", p=128)
                            ).then_inc(s_d3wf.h, 16)
                            sync.dma_start(
                                wuf[:, o:o + KE * 128].rearrange("p (k c) -> p k c", k=KE),
                                up_bf[f].rearrange("(k p) c -> p k c", p=128)
                            ).then_inc(s_d3wf.h, 16)
                            s_d3wf.n += 32
                        for m in range(KE):
                            cm = c * KE + m
                            if cm >= 2:
                                sync.wait_ge(s_d3dn.h, cm - 1)
                            o = (cm % 2) * NF * 128
                            sync.dma_start(
                                wdm[:, o:o + NF * 128].rearrange("p (f c) -> p f c", f=NF),
                                down_bf[m].rearrange("(f p) c -> p f c", p=128)
                            ).then_inc(s_d3wd.h, 16)
                            s_d3wd.n += 16
                        if c + 1 < NC_:
                            xloads(c + 1)
                        if c + 2 < NC_:
                            wloads(c + 2)

                @blk.tensor
                def _(tensor):
                    tensor.wait_ge(s_ph.h, PHC[0])
                    for c in range(NC_):
                        tensor.wait_ge(s_d3h2ld[c % 2].h, 256 * (c // 2 + 1))
                        for f in range(NF):
                            F_ = c * NF + f
                            tensor.wait_ge(s_d3wf.h, 32 * (F_ + 1))
                            if F_ >= 1:
                                tensor.wait_ge(s_d3sg.h, F_)
                                tensor.wait_ge(s_d3gf.h, F_)
                            for k in range(KE):
                                tensor.matmul(
                                    p_g[:],
                                    wgf[:, (F_ % 2) * KE * 128 + k * 128:
                                        (F_ % 2) * KE * 128 + (k + 1) * 128],
                                    h2d[:, ((c % 2) * KE + k) * 512:
                                        ((c % 2) * KE + k + 1) * 512],
                                    start=(k == 0), stop=(k == KE - 1))
                            ii = None
                            for k in range(KE):
                                ii = tensor.matmul(
                                    p_u[:],
                                    wuf[:, (F_ % 2) * KE * 128 + k * 128:
                                        (F_ % 2) * KE * 128 + (k + 1) * 128],
                                    h2d[:, ((c % 2) * KE + k) * 512:
                                        ((c % 2) * KE + k + 1) * 512],
                                    start=(k == 0), stop=(k == KE - 1))
                            s_d3mm.inc(ii)

                        def stats(c, m):
                            cm = c * KE + m
                            tensor.wait_ge(s_d3z.h, cm + 1)
                            tensor.wait_ge(s_d3zq.h, cm + 1)
                            if m == 0:
                                tensor.wait_ge(s_d3sf.h, c)
                            tensor.matmul(p_z[:], ones128[:],
                                          zsb[:, m * 512:(m + 1) * 512],
                                          start=(m == 0), stop=(m == KE - 1))
                            ii = tensor.matmul(p_z2[:], ones128[:],
                                               zqst[:, (cm % 2) * 512:(cm % 2 + 1) * 512],
                                               start=(m == 0), stop=(m == KE - 1))
                            s_d3st.inc(ii)

                        tensor.wait_ge(s_d3gf.h, NF * (c + 1))
                        for m in range(KE):
                            cm = c * KE + m
                            tensor.wait_ge(s_d3wd.h, 16 * (cm + 1))
                            if cm >= 2:
                                tensor.wait_ge(s_d3z.h, cm - 1)
                            ii = None
                            for f in range(NF):
                                ii = tensor.matmul(
                                    p_mlp[cm % 2][:],
                                    wdm[:, (cm % 2) * NF * 128 + f * 128:
                                        (cm % 2) * NF * 128 + (f + 1) * 128],
                                    gbuf[:, f * 512:(f + 1) * 512],
                                    start=(f == 0), stop=(f == NF - 1))
                            s_d3dn.inc(ii)
                            if m >= 1:
                                stats(c, m - 1)
                        stats(c, KE - 1)
                        for m in range(KE):
                            cm = c * KE + m
                            tensor.wait_ge(s_d3x2.h, cm + 1)
                            if cm >= 2:
                                tensor.wait_ge(s_d3oev.h, cm - 1)
                            ii = None
                            for t in range(4):
                                ii = tensor.transpose(
                                    p_tr[cm % 2][:, t * 128:(t + 1) * 128],
                                    x2st[:, (cm % 2) * 512 + t * 128:
                                         (cm % 2) * 512 + (t + 1) * 128],
                                    ident[:])
                            s_d3tr.inc(ii)

                @blk.scalar
                def _(scalar):
                    scalar.wait_ge(s_ph.h, PHC[0])
                    for c in range(NC_):
                        scalar.preload_act_table(AF.Silu)
                        for f in range(NF):
                            F_ = c * NF + f
                            scalar.wait_ge(s_d3mm.h, F_ + 1)
                            if F_ >= 2:
                                scalar.wait_ge(s_d3gf.h, F_ - 1)
                            ii = scalar.activation(
                                sgst[:, (F_ % 2) * 512:(F_ % 2 + 1) * 512],
                                p_g[:], AF.Silu)
                            s_d3sg.inc(ii)
                        for m in range(KE):
                            cm = c * KE + m
                            scalar.wait_ge(s_d3z.h, cm + 1)
                            if cm >= 2:
                                scalar.wait_ge(s_d3st.h, cm - 1)
                            ii = scalar.activation(
                                zqst[:, (cm % 2) * 512:(cm % 2 + 1) * 512],
                                zsb[:, m * 512:(m + 1) * 512], AF.Square)
                            s_d3zq.inc(ii)
                        scalar.wait_ge(s_d3st.h, 16 * (c + 1))
                        scalar.activation(mbz[:], p_z[:], AF.Identity, scale=1.0 / E)
                        ii = scalar.activation(uqz[:], p_z2[:], AF.Identity,
                                               scale=1.0 / E)
                        s_d3sf.inc(ii)
                        preload_rsqrt(scalar)
                        scalar.wait_ge(s_d3var.h, c + 1)
                        ii = rsqrt_act(scalar, rbz[:], varz[:], LN_EPS)
                        s_d3rb.inc(ii)
                        for m in range(KE):
                            cm = c * KE + m
                            scalar.wait_ge(s_d3tr.h, cm + 1)
                            if cm >= 2:
                                scalar.wait_ge(s_out[cm % 2].h, 16 * (cm // 2))
                            ii = scalar.activation(
                                ost[:, (cm % 2) * 512:(cm % 2 + 1) * 512],
                                p_tr[cm % 2][:], AF.Identity)
                            s_d3oev.inc(ii)
                            dst = out_t[_chunk(c), m * 128:(m + 1) * 128].rearrange(
                                "(t p) e -> p t e", p=128)
                            src = ost[:, (cm % 2) * 512:(cm % 2 + 1) * 512].rearrange(
                                "p (t e) -> p t e", t=4)
                            scalar.dma_start(dst, src).then_inc(s_out[cm % 2].h, 16)
                            s_out[cm % 2].n += 16

                @blk.vector
                def _(vector):
                    vector.wait_ge(s_ph.h, PHC[0])
                    for c in range(NC_):
                        for f in range(NF):
                            F_ = c * NF + f
                            vector.wait_ge(s_d3sg.h, F_ + 1)
                            if f == 0 and c >= 1:
                                vector.wait_ge(s_d3dn.h, KE * c)
                            ii = vector.tensor_mul(gbuf[:, f * 512:(f + 1) * 512],
                                                   sgst[:, (F_ % 2) * 512:(F_ % 2 + 1) * 512],
                                                   p_u[:])
                            s_d3gf.inc(ii)
                        vector.wait_ge(s_d3xld.h, (2 * KE + 1) * 16 * (c + 1))
                        for m in range(KE):
                            cm = c * KE + m
                            vector.wait_ge(s_d3dn.h, cm + 1)
                            vector.tensor_sub(s1st[:], hd3[:, m * 512:(m + 1) * 512],
                                              x1d3[:, m * 512:(m + 1) * 512])
                            vector.tensor_add(s2st[:], p_mlp[cm % 2][:], s1st[:])
                            vector.tensor_mul(s3st[:], msbuf[:], s2st[:])
                            ii = vector.scalar_tensor_tensor(
                                zsb[:, m * 512:(m + 1) * 512],
                                x1d3[:, m * 512:(m + 1) * 512], 2.0, s3st[:],
                                ALU.mult, ALU.add)
                            s_d3z.inc(ii)
                        vector.wait_ge(s_d3sf.h, c + 1)
                        vector.tensor_mul(tz[:], mbz[:], mbz[:])
                        ii = vector.tensor_sub(varz[:], uqz[:], tz[:])
                        s_d3var.inc(ii)
                        vector.wait_ge(s_d3rb.h, c + 1)
                        for m in range(KE):
                            cm = c * KE + m
                            if cm >= 2:
                                vector.wait_ge(s_d3tr.h, cm - 1)
                            vector.tensor_sub(tz[:], zsb[:, m * 512:(m + 1) * 512],
                                              mbz[:])
                            ii = vector.tensor_mul(
                                x2st[:, (cm % 2) * 512:(cm % 2 + 1) * 512],
                                tz[:], rbz[:])
                            s_d3x2.inc(ii)

            drain_block()

    return nc


def _host_consts():
    pos = np.arange(N, dtype=np.float64)
    inv = 1.0 / (10000.0 ** (np.arange(0, 128, 2, dtype=np.float64) / 128.0))
    ang = pos[:, None] * inv[None, :]
    c = np.cos(ang).astype(np.float32)
    s = np.sin(ang).astype(np.float32)
    c2 = np.ascontiguousarray(np.concatenate([c, c], axis=1).T)
    s2 = np.ascontiguousarray(np.concatenate([-s, s], axis=1).T)
    kk = np.arange(128)[:, None]
    qq = np.arange(512)[None, :]
    msk = np.concatenate([(qq >= kk + 128 * p) for p in range(4)],
                         axis=1).astype(ml_dtypes.bfloat16)
    ones = np.ones((128, 128), dtype=ml_dtypes.bfloat16)
    ident = np.eye(128, dtype=ml_dtypes.bfloat16)
    return c2, s2, np.ascontiguousarray(msk), ones, ident


def kernel(L0_attn_x, L0_attn_w_qkv, L0_attn_w_out, L0_attn_ln_scale, L0_attn_ln_bias,
           L0_ffn_w_router, L0_ffn_b_router, L0_ffn_norm1_w, L0_ffn_wv, L0_ffn_bv,
           L0_ffn_wo, L0_ffn_norm2_w, L0_ffn_w_gate, L0_ffn_w_up, L0_ffn_w_down):
    c2, s2, msk, ones, ident = _host_consts()
    A = np.ascontiguousarray
    shared = {
        "w_qkv": A(L0_attn_w_qkv, dtype=np.float32),
        "w_out": A(L0_attn_w_out, dtype=np.float32),
        "wv": A(L0_ffn_wv, dtype=np.float32),
        "wo": A(L0_ffn_wo, dtype=np.float32),
        "w_gate": A(L0_ffn_w_gate, dtype=np.float32),
        "w_up": A(L0_ffn_w_up, dtype=np.float32),
        "w_down": A(L0_ffn_w_down, dtype=np.float32),
        "gam": A(np.asarray(L0_attn_ln_scale, dtype=np.float32).reshape(128, 1)),
        "bet": A(np.asarray(L0_attn_ln_bias, dtype=np.float32).reshape(128, 1)),
        "w1t": A(np.asarray(L0_ffn_norm1_w, dtype=np.float32).reshape(KE, 128).T),
        "w2t": A(np.asarray(L0_ffn_norm2_w, dtype=np.float32).reshape(KE, 128).T),
        "bvt": A(np.asarray(L0_ffn_bv, dtype=np.float32).reshape(KE, 128).T),
        "wr_rep": A(np.broadcast_to(
            np.asarray(L0_ffn_w_router, dtype=np.float32)[:, None], (E, 128))),
        "br_t": A(np.broadcast_to(
            np.asarray(L0_ffn_b_router, dtype=np.float32), (128, 1))),
        "c2": c2, "s2": s2, "msk": msk, "ones128": ones, "ident": ident,
        "epsv": A(np.broadcast_to(np.array([LN_EPS, RMS_EPS], dtype=np.float32),
                                  (128, 2))),
    }
    xs = np.asarray(L0_attn_x, dtype=np.float32)
    nc = build_program(debug=False)
    in_maps = []
    for i in range(8):
        m = dict(shared)
        m["x"] = A(xs[i])
        in_maps.append(m)
    res = run_bass_kernel_spmd(nc, in_maps, core_ids=list(range(8)))
    return np.stack([r["out"] for r in res.results], axis=0)
